# version 05 (final)
# baseline (speedup 1.0000x reference)
import math
import contextlib
import numpy as np
import ml_dtypes
import concourse.bass as bass
import concourse.mybir as mybir
from concourse.bass_utils import run_bass_kernel_spmd

F32 = mybir.dt.float32
BF16 = mybir.dt.bfloat16
AF = mybir.ActivationFunctionType
ALU = mybir.AluOpType
AX = mybir.AxisListType
BF = ml_dtypes.bfloat16

NCORES = 8
SEQ = 8192
D = 2048
NT = 1024
NCH = D // 128
RS = 1.0 / math.sqrt(128.0)

BASE_LR = 0.01
BASE_LR_INV = BASE_LR + math.log(-math.expm1(-BASE_LR))
NS_COEFFS = [(4.0848, -6.8946, 2.927), (3.9505, -6.3029, 2.6377),
             (3.7418, -5.5913, 2.3037), (2.8769, -3.1427, 1.2046),
             (2.8366, -3.0525, 1.2012)]

ENG_TE, ENG_SC, ENG_VE, ENG_GP, ENG_SYQ, ENG_GPQ = range(6)
_DMA_ENGS = (ENG_SYQ, ENG_GPQ)


class _Op:
    __slots__ = ("eng", "fn", "after", "idx", "inc", "seq", "planned")

    def __init__(self, eng, fn, after, idx):
        self.eng = eng
        self.fn = fn
        self.after = after
        self.idx = idx
        self.inc = 0
        self.planned = 0


class Prog:
    def __init__(self, nc):
        self.nc = nc
        self.ops = []

    def _add(self, eng, fn, after):
        deps = []
        for a in after:
            if a is None:
                continue
            if isinstance(a, (list, tuple)):
                deps.extend(x for x in a if x is not None)
            else:
                deps.append(a)
        op = _Op(eng, fn, deps, len(self.ops))
        self.ops.append(op)
        return op.idx

    def te(self, fn, after=()):
        return self._add(ENG_TE, fn, after)

    def sc(self, fn, after=()):
        return self._add(ENG_SC, fn, after)

    def ve(self, fn, after=()):
        return self._add(ENG_VE, fn, after)

    def gp(self, fn, after=()):
        return self._add(ENG_GP, fn, after)

    def sy_dma(self, out, in_, after=()):
        return self._add(ENG_SYQ, lambda e: e.dma_start(out=out, in_=in_), after)

    def gp_dma(self, out, in_, after=()):
        return self._add(ENG_GPQ, lambda e: e.dma_start(out=out, in_=in_), after)



class Prog2(Prog):

    DMA_POOL = 12
    STRICT = True

    def finalize(self, final_engine=None):
        nc = self.nc
        ops = self.ops
        K = self.DMA_POOL
        dma_no = {}
        qcount = {ENG_SYQ: 0, ENG_GPQ: 0}
        for op in ops:
            if op.eng in _DMA_ENGS:
                dma_no[op.idx] = qcount[op.eng]
                qcount[op.eng] += 1
                op.inc = 16
        if self.STRICT:
            for op in ops:
                if op.eng not in _DMA_ENGS:
                    op.inc = 1
        else:
            for op in ops:
                for d in op.after:
                    p = ops[d]
                    if p.eng not in _DMA_ENGS and p.eng != op.eng and p.inc == 0:
                        p.inc = 1
        counts = [0] * 6
        for op in ops:
            if op.inc and op.eng not in _DMA_ENGS:
                counts[op.eng] += op.inc
                op.planned = counts[op.eng]
            elif op.eng in _DMA_ENGS:
                op.planned = 16 * (dma_no[op.idx] // K + 1)
        streams = [[] for _ in range(6)]
        for op in ops:
            streams[op.eng].append(op)
        gp_merged = sorted(streams[ENG_GP] + streams[ENG_GPQ], key=lambda o: o.idx)
        extra = {}
        for q in _DMA_ENGS:
            qs = streams[q]
            for i, op in enumerate(qs):
                if i >= K:
                    extra.setdefault(op.idx, []).append(qs[i - K].idx)
        exec_streams = [streams[ENG_TE], streams[ENG_SC], streams[ENG_VE],
                        gp_merged, streams[ENG_SYQ]]
        pos = [0] * len(exec_streams)
        done = [False] * len(ops)
        ndone = 0
        while ndone < len(ops):
            progress = False
            for e in range(len(exec_streams)):
                while pos[e] < len(exec_streams[e]):
                    op = exec_streams[e][pos[e]]
                    deps = op.after + extra.get(op.idx, [])
                    if all(done[d] for d in deps):
                        done[op.idx] = True
                        ndone += 1
                        pos[e] += 1
                        progress = True
                    else:
                        break
            if not progress:
                stuck = []
                for e in range(len(exec_streams)):
                    if pos[e] < len(exec_streams[e]):
                        o = exec_streams[e][pos[e]]
                        stuck.append((o.idx, o.eng, o.after))
                raise RuntimeError(f"Prog deadlock; stuck (idx,eng,after): {stuck[:6]}")

        with contextlib.ExitStack() as ctx:
            csem = {e: ctx.enter_context(nc.semaphore(f"c{e}"))
                    for e in (ENG_TE, ENG_SC, ENG_VE, ENG_GP)}
            dsem = {q: [ctx.enter_context(nc.semaphore(f"d{q}_{j}")) for j in range(K)]
                    for q in _DMA_ENGS}
            blk = ctx.enter_context(nc.Block())

            def sem_of(p):
                if p.eng in _DMA_ENGS:
                    return dsem[p.eng][dma_no[p.idx] % K]
                return csem[p.eng]

            def emit_stream(eng_obj, stream, own_engs):
                import os
                dbg = os.environ.get("PROG2_TRACE")
                waited = {}
                for op in stream:
                    if dbg:
                        import sys as _sys
                        print(f"[{op.eng}] op{op.idx} inc={op.inc} planned={op.planned} after={op.after}", file=_sys.stderr)
                    if op.eng in _DMA_ENGS:
                        n = dma_no[op.idx]
                        if n >= K:
                            sname = (op.eng, n % K)
                            tgt = 16 * (n // K)
                            if waited.get(sname, 0) < tgt:
                                eng_obj.wait_ge(dsem[op.eng][n % K], tgt)
                                waited[sname] = tgt
                    for d in op.after:
                        p = ops[d]
                        if (not self.STRICT) and p.eng in own_engs and p.eng not in _DMA_ENGS:
                            continue
                        key = (p.eng, dma_no[p.idx] % K) if p.eng in _DMA_ENGS else (p.eng,)
                        tgt = p.planned
                        if waited.get(key, 0) < tgt:
                            if dbg:
                                import sys as _sys
                                print(f"    wait {key} >= {tgt}", file=_sys.stderr)
                            eng_obj.wait_ge(sem_of(p), tgt)
                            waited[key] = tgt
                    inst = op.fn(eng_obj)
                    if op.inc:
                        assert inst is not None, f"op {op.idx} returned None"
                        inst.then_inc(sem_of(op), op.inc)

            @blk.tensor
            def _(e):
                emit_stream(e, streams[ENG_TE], (ENG_TE,))

            @blk.scalar
            def _(e):
                emit_stream(e, streams[ENG_SC], (ENG_SC,))

            @blk.vector
            def _(e):
                emit_stream(e, streams[ENG_VE], (ENG_VE,))

            @blk.gpsimd
            def _(e):
                emit_stream(e, gp_merged, (ENG_GP,))

            @blk.sync
            def _(e):
                emit_stream(e, streams[ENG_SYQ], ())


def MM(P, out, lhsT, rhs, start, stop, after):
    return P.te(lambda e, o=out, l=lhsT, r=rhs, s=start, t=stop:
                e.matmul(o, l, r, start=s, stop=t), after)


def TP(P, out, in_, ident, after):
    return P.te(lambda e, o=out, i=in_, d=ident: e.transpose(o, i, d), after)



def build_ffn0pre(with_attnsum, debug=False):
    nc = bass.Bass("TRN2", target_bir_lowering=False, debug=False)
    x_d = nc.dram_tensor("x", [D, NT], F32, kind="ExternalInput").ap()
    sim_d = nc.dram_tensor("simw", [D, 16], F32, kind="ExternalInput").ap()
    gates_d = nc.dram_tensor("gates", [1, 16], F32, kind="ExternalInput").ap()
    wq_d = nc.dram_tensor("wq", [D, D], F32, kind="ExternalInput").ap()
    wk_d = nc.dram_tensor("wk", [D, D], F32, kind="ExternalInput").ap()
    wv_d = nc.dram_tensor("wv", [D, D], F32, kind="ExternalInput").ap()
    eyeb_d = nc.dram_tensor("eyeb_d", [128, 128], BF16, kind="ExternalInput").ap()
    eyef_d = nc.dram_tensor("eyef_d", [128, 128], F32, kind="ExternalInput").ap()
    if with_attnsum:
        parts_d = nc.dram_tensor("parts", [NCORES * D, NT], BF16, kind="ExternalInput").ap()
        xout_d = nc.dram_tensor("xout", [D, NT], F32, kind="ExternalOutput").ap()
    qf_d = nc.dram_tensor("qf", [128, NT], BF16, kind="ExternalOutput").ap()
    kf_d = nc.dram_tensor("kf", [128, NT], BF16, kind="ExternalOutput").ap()
    vt_d = nc.dram_tensor("vt", [NT, 128], BF16, kind="ExternalOutput").ap()
    wtm_d = nc.dram_tensor("wtm", [NT, 16], F32, kind="ExternalOutput").ap()
    wT_d = nc.dram_tensor("wT", [16, NT], F32, kind="ExternalOutput").ap()
    if debug:
        ltdbg_d = nc.dram_tensor("ltdbg", [16, NT], F32, kind="ExternalOutput").ap()
        lgdbg_d = nc.dram_tensor("lgdbg", [128, 8 * 16], F32, kind="ExternalOutput").ap()
        gscdbg_d = nc.dram_tensor("gscdbg", [128, 8 * 16], F32, kind="ExternalOutput").ap()
        invcdbg_d = nc.dram_tensor("invcdbg", [128, 8], F32, kind="ExternalOutput").ap()
        xnbdbg_d = nc.dram_tensor("xnbdbg", [128, NCH * NT], BF16, kind="ExternalOutput").ap()
        coldbg_d = nc.dram_tensor("coldbg", [128, 32], F32, kind="ExternalOutput").ap()
        gtdbg_d = nc.dram_tensor("gtdbg", [1, 32], F32, kind="ExternalOutput").ap()
        simbdbg_d = nc.dram_tensor("simbdbg", [128, NCH * 16], BF16, kind="ExternalOutput").ap()

    with contextlib.ExitStack() as ctx:
        E = ctx.enter_context
        xs_guard = nc.sbuf_tensor("s_xs", [128, NCH * NT], F32, side="right")
        xs = xs_guard.__enter__()
        xnb = E(nc.sbuf_tensor("s_xnb", [128, NCH * NT], BF16))
        xstat = E(nc.sbuf_tensor("s_xstat", [128, 2 * NT], BF16))
        sqs = E(nc.sbuf_tensor("s_sqs", [128, 2 * NT], BF16))
        simb = E(nc.sbuf_tensor("s_simb", [128, NCH * 16], BF16))
        simsq = E(nc.sbuf_tensor("s_simsq", [128, NCH * 16], F32))
        row = E(nc.sbuf_tensor("s_row", [1, 6 * NT], F32))
        gt = E(nc.sbuf_tensor("s_gt", [1, 32], F32))
        col = E(nc.sbuf_tensor("s_col", [128, 32], F32))
        invc = E(nc.sbuf_tensor("s_invc", [128, 8], F32))
        eyeb = E(nc.sbuf_tensor("s_eyeb", [128, 128], BF16))
        eyef = E(nc.sbuf_tensor("s_eyef", [128, 128], F32))
        onesb = E(nc.sbuf_tensor("s_onesb", [128, 1], BF16))
        onesf = E(nc.sbuf_tensor("s_onesf", [128, 1], F32))
        onesr = E(nc.sbuf_tensor("s_onesr", [1, 128], F32))
        mrep = E(nc.sbuf_tensor("s_mrep", [128, NT], F32))
        rrep = E(nc.sbuf_tensor("s_rrep", [128, NT], F32))
        ltsb = E(nc.sbuf_tensor("s_ltsb", [16, NT], F32))
        lg = E(nc.sbuf_tensor("s_lg", [128, 8 * 16], F32))
        gsc = E(nc.sbuf_tensor("s_gsc", [128, 8 * 16], F32))
        wtm = E(nc.sbuf_tensor("s_wtm", [128, 8 * 16], F32))
        wTt = E(nc.sbuf_tensor("s_wTt", [16, NT], F32))
        sgr = E(nc.sbuf_tensor("s_sgr", [128, 16], F32))
        qa = E(nc.sbuf_tensor("s_qa", [128, 8 * 128], F32))
        ka = E(nc.sbuf_tensor("s_ka", [128, 8 * 128], F32))
        va = E(nc.sbuf_tensor("s_va", [128, 8 * 128], F32))
        qab = E(nc.sbuf_tensor("s_qab", [128, 8 * 128], BF16))
        kab = E(nc.sbuf_tensor("s_kab", [128, 8 * 128], BF16))
        vtb = E(nc.sbuf_tensor("s_vtb", [128, 8 * 128], BF16))
        qfb = E(nc.sbuf_tensor("s_qfb", [128, NT], BF16))
        kfb = E(nc.sbuf_tensor("s_kfb", [128, NT], BF16))
        if with_attnsum:
            pts = E(nc.sbuf_tensor("s_pts", [128, 4 * NT], BF16))

        PA = E(nc.psum_tensor("s_PA", [128, 1024], F32))
        PB = E(nc.psum_tensor("s_PB", [128, 1024], F32))
        PQ = E(nc.psum_tensor("s_PQ", [128, 512], F32))
        PK = E(nc.psum_tensor("s_PK", [128, 512], F32))
        PV = E(nc.psum_tensor("s_PV", [128, 512], F32))
        PTB = E(nc.psum_tensor("s_PTB", [128, 128], BF16))

        P = Prog2(nc)
        ld_x = P.sy_dma(xs[:], x_d.rearrange("(g p) n -> p g n", g=NCH))
        ld_sim = P.gp_dma(simb[:].rearrange("p (g e) -> p g e", g=NCH),
                          sim_d.rearrange("(g p) e -> p g e", g=NCH))
        ld_eb = P.sy_dma(eyeb[:], eyeb_d)
        ld_ef = P.sy_dma(eyef[:], eyef_d)
        ld_gt = P.sy_dma(gt[0:1, 0:16], gates_d)
        ms_ones = P.ve(lambda e: e.memset(onesb[:], 1.0))
        ms_onesf = P.ve(lambda e: e.memset(onesf[:], 1.0))
        ms_onesr = P.ve(lambda e: e.memset(onesr[:], 1.0))

        x_ready = ld_x
        if with_attnsum:
            ldp, mmc = {}, {}
            sum_evs = []
            for g in range(NCH):
                PSX = (PA, PB)[g % 2]
                mml = {}
                for r in range(NCORES):
                    n = g * NCORES + r
                    slot = n % 4
                    q = P.sy_dma if n % 2 == 0 else P.gp_dma
                    ldp[(g, r)] = q(pts[:, slot * NT:(slot + 1) * NT],
                                    parts_d[r * D + g * 128:r * D + (g + 1) * 128, :],
                                    after=[mmc.get(n - 4)])
                    for half in range(2):
                        war = ([sum_evs[-4], sum_evs[-3]]
                               if len(sum_evs) >= 4 else []) if r == 0 else []
                        mml[half] = MM(P, PSX[:, half * 512:(half + 1) * 512],
                                       eyeb[:],
                                       pts[:, slot * NT + half * 512:
                                            slot * NT + (half + 1) * 512],
                                       r == 0, r == NCORES - 1,
                                       [ldp[(g, r)], ld_eb] + war)
                    mmc[g * NCORES + r] = mml[1]
                for half in range(2):
                    ev = P.ve(lambda e, g=g, half=half, PSX=PSX: e.scalar_tensor_tensor(
                        xs[:, g * NT + half * 512:g * NT + (half + 1) * 512],
                        PSX[:, half * 512:(half + 1) * 512], 1.0,
                        xs[:, g * NT + half * 512:g * NT + (half + 1) * 512],
                        ALU.mult, ALU.add), after=[mml[half], ld_x])
                    sum_evs.append(ev)
            x_ready = sum_evs[-1]
            st_x = P.sy_dma(xout_d.rearrange("(g p) n -> p g n", g=NCH),
                            xs[:].rearrange("p (g n) -> p g n", g=NCH),
                            after=[x_ready] + sum_evs)
        else:
            st_x = None

        s1a = s1b = s2a = s2b = None
        casts, sqss, s1ops, s2ops = [], [], [], []
        for g in range(NCH):
            slot = g % 2
            war_cast = [s1ops[2 * (g - 2) + 1], sqss[g - 2]] if g >= 2 else []
            c = P.ve(lambda e, g=g, s=slot: e.tensor_copy(
                xstat[:, s * NT:(s + 1) * NT], xs[:, g * NT:(g + 1) * NT]),
                after=[x_ready] + war_cast + ([casts[g - 2]] if g >= 2 else []) +
                      ([st_x] if (with_attnsum and g == 0) else []))
            casts.append(c)
            war_sq = [s2ops[2 * (g - 2) + 1]] if g >= 2 else []
            q = P.sc(lambda e, s=slot: e.activation(
                sqs[:, s * NT:(s + 1) * NT], xstat[:, s * NT:(s + 1) * NT], AF.Square),
                after=[c] + war_sq + ([sqss[g - 2]] if g >= 2 else []))
            sqss.append(q)
            for h in range(2):
                s1ops.append(MM(P, PA[0:1, h * 512:(h + 1) * 512], onesb[:],
                                xstat[:, slot * NT + h * 512:slot * NT + (h + 1) * 512],
                                g == 0, g == NCH - 1, [c, ms_ones]))
                s2ops.append(MM(P, PB[0:1, h * 512:(h + 1) * 512], onesb[:],
                                sqs[:, slot * NT + h * 512:slot * NT + (h + 1) * 512],
                                g == 0, g == NCH - 1, [q, ms_ones]))
        mean = row[0:1, 0 * NT:1 * NT]
        var = row[0:1, 1 * NT:2 * NT]
        rstd = row[0:1, 2 * NT:3 * NT]
        invrn = row[0:1, 3 * NT:4 * NT]
        t4 = row[0:1, 4 * NT:5 * NT]
        t5 = row[0:1, 5 * NT:6 * NT]
        v1 = P.ve(lambda e: e.tensor_scalar_mul(mean, PA[0:1, 0:NT], 1.0 / D),
                  after=[s1ops[-1], s1ops[-2]])
        v2 = P.ve(lambda e: e.tensor_scalar_mul(var, PB[0:1, 0:NT], 1.0 / D),
                  after=[s2ops[-1], s2ops[-2]])
        v3 = P.ve(lambda e: e.tensor_mul(t4, mean, mean), after=[v1])
        v4 = P.ve(lambda e: e.tensor_sub(var, var, t4), after=[v2, v3])
        v5 = P.ve(lambda e: e.tensor_scalar_add(t4, var, 1e-5), after=[v4])
        s_sq = P.sc(lambda e: e.activation(t5, t4, AF.Sqrt), after=[v5])
        v6 = P.ve(lambda e: e.reciprocal(rstd, t5), after=[s_sq])
        v7 = P.ve(lambda e: e.tensor_scalar_mul(t4, var, float(D)), after=[v4, s_sq])
        v8 = P.ve(lambda e: e.tensor_mul(t4, t4, rstd), after=[v6, v7])
        v9 = P.ve(lambda e: e.tensor_mul(t4, t4, rstd), after=[v8])
        s_rn = P.sc(lambda e: e.activation(t5, t4, AF.Sqrt), after=[v9])
        v10 = P.ve(lambda e: e.tensor_scalar_max(t4, t5, 1e-12), after=[s_rn])
        v11 = P.ve(lambda e: e.reciprocal(invrn, t4), after=[v10])

        reps = []
        for h in range(2):
            reps.append(MM(P, PA[:, h * 512:(h + 1) * 512], onesr[0:1, :],
                           mean[0:1, h * 512:(h + 1) * 512], True, True,
                           [v1, v3, ms_onesr]))
        ev_m = P.ve(lambda e: e.tensor_copy(mrep[:], PA[:, 0:NT]), after=reps)
        reps2 = []
        for h in range(2):
            reps2.append(MM(P, PA[:, h * 512:(h + 1) * 512], onesr[0:1, :],
                            rstd[0:1, h * 512:(h + 1) * 512], True, True,
                            [v6, ev_m]))
        ev_r = P.ve(lambda e: e.tensor_copy(rrep[:], PA[:, 0:NT]), after=reps2)

        xn_ops = []
        for g in range(NCH):
            a = P.ve(lambda e, g=g: e.tensor_sub(
                xs[:, g * NT:(g + 1) * NT], xs[:, g * NT:(g + 1) * NT], mrep[:]),
                after=[ev_m, casts[g]] + ([sqss[g]] if True else []))
            b = P.ve(lambda e, g=g: e.tensor_mul(
                xnb[:, g * NT:(g + 1) * NT], xs[:, g * NT:(g + 1) * NT], rrep[:]),
                after=[a, ev_r])
            xn_ops.append(b)

        sim_sqs = []
        for g in range(NCH):
            sim_sqs.append(P.sc(lambda e, g=g: e.activation(
                simsq[:, g * 16:(g + 1) * 16], simb[:, g * 16:(g + 1) * 16], AF.Square),
                after=[ld_sim]))
        cn_ops = []
        for g in range(NCH):
            cn_ops.append(MM(P, PB[0:1, 0:16], onesf[:], simsq[:, g * 16:(g + 1) * 16],
                             g == 0, g == NCH - 1,
                             [sim_sqs[g], ms_onesf, v2]))
        cnrow = gt[0:1, 16:32]
        ev_cn = P.ve(lambda e: e.tensor_copy(cnrow, PB[0:1, 0:16]), after=[cn_ops[-1]])
        s_cns = P.sc(lambda e: e.activation(cnrow, cnrow, AF.Sqrt), after=[ev_cn])
        v_cnm = P.ve(lambda e: e.tensor_scalar_max(cnrow, cnrow, 1e-12), after=[s_cns])
        s_sg = P.sc(lambda e: e.activation(gt[0:1, 0:16], gt[0:1, 0:16], AF.Sigmoid),
                    after=[ld_gt])
        rep_sg = MM(P, PA[:, 0:16], onesr[0:1, :], gt[0:1, 0:16], True, True,
                    [s_sg, ev_r])
        ev_sg = P.ve(lambda e: e.tensor_copy(sgr[:], PA[:, 0:16]), after=[rep_sg])
        tp_cn = TP(P, PA[0:16, 16:17], cnrow, eyef[0:1, 0:1], [v_cnm, ev_sg, ld_ef])
        ev_cnc = P.ve(lambda e: e.tensor_copy(col[0:16, 0:1], PA[0:16, 16:17]),
                      after=[tp_cn])
        v_cnr = P.ve(lambda e: e.reciprocal(col[0:16, 1:2], col[0:16, 0:1]),
                     after=[ev_cnc])
        tp_inv, ev_inv = [], []
        for j in range(8):
            prev_rd = ev_inv[j - 1] if j else ev_cnc
            t = TP(P, PA[:, 17 + j:18 + j], invrn[0:1, j * 128:(j + 1) * 128],
                   eyef[0:1, 0:1], [v11, ev_sg, prev_rd])
            tp_inv.append(t)
            ev_inv.append(P.ve(lambda e, j=j: e.tensor_copy(
                invc[:, j:j + 1], PA[:, 17 + j:18 + j]), after=[t]))

        lt_ops = []
        for g in range(NCH):
            for h in range(2):
                lt_ops.append(MM(P, PB[0:16, h * 512:(h + 1) * 512],
                                 simb[:, g * 16:(g + 1) * 16],
                                 xnb[:, g * NT + h * 512:g * NT + (h + 1) * 512],
                                 g == 0, g == NCH - 1,
                                 [ld_sim, xn_ops[g], ev_cn]))
        ev_lt = P.ve(lambda e: e.tensor_single_scalar(
            ltsb[:], PB[0:16, :], col[0:16, 1:2], ALU.mult),
            after=[lt_ops[-1], lt_ops[-2], v_cnr])

        w_ops = []
        prev_ev = None
        for j in range(8):
            tp = TP(P, PB[0:128, 0:16], ltsb[0:16, j * 128:(j + 1) * 128],
                    eyef[0:16, 0:16], [ev_lt, ld_ef, prev_ev])
            L = lg[:, j * 16:(j + 1) * 16]
            G = gsc[:, j * 16:(j + 1) * 16]
            W = wtm[:, j * 16:(j + 1) * 16]
            ev = P.ve(lambda e, L=L, j=j: e.scalar_tensor_tensor(
                L, PB[0:128, 0:16], invc[:, j:j + 1], sgr[:], ALU.mult, ALU.subtract),
                after=[tp, ev_inv[j], ev_sg])
            prev_ev = ev
            g1 = P.ve(lambda e, G=G, L=L: e.tensor_scalar_max(G, L, 0.0), after=[ev])
            g2 = P.ve(lambda e, W=W, L=L: e.tensor_single_scalar(W, L, 0.0, ALU.is_gt),
                      after=[ev])
            g3 = P.ve(lambda e, W=W: e.reduce_sum(col[:, 0:1], W, axis=AX.X), after=[g2])
            g4 = P.ve(lambda e, L=L: e.reduce_max(col[:, 1:2], L, axis=AX.X), after=[ev])
            g5 = P.ve(lambda e, L=L: e.tensor_single_scalar(
                col[:, 8:24], L, col[:, 1:2], ALU.is_ge), after=[g4])
            g6 = P.ve(lambda e, L=L: e.scalar_tensor_tensor(
                col[:, 8:24], col[:, 8:24], -1e9, L, ALU.mult, ALU.add), after=[g5])
            g7 = P.ve(lambda e: e.reduce_max(col[:, 2:3], col[:, 8:24], axis=AX.X),
                      after=[g6])
            g8 = P.ve(lambda e, L=L: e.tensor_single_scalar(
                col[:, 8:24], L, col[:, 2:3], ALU.is_ge), after=[g7])
            g9 = P.ve(lambda e, W=W: e.tensor_sub(col[:, 8:24], col[:, 8:24], W),
                      after=[g8])
            g10 = P.ve(lambda e: e.tensor_single_scalar(
                col[:, 3:4], col[:, 0:1], 0.5, ALU.is_lt), after=[g3])
            g11 = P.ve(lambda e, W=W: e.scalar_tensor_tensor(
                W, col[:, 8:24], col[:, 3:4], W, ALU.mult, ALU.add), after=[g9, g10])
            s_e = P.sc(lambda e, L=L, G=G: e.activation(L, G, AF.Exp), after=[g1, g8])
            g12 = P.ve(lambda e, L=L, W=W: e.tensor_mul(L, L, W), after=[s_e, g11])
            g13 = P.ve(lambda e, L=L: e.reduce_sum(col[:, 4:5], L, axis=AX.X),
                       after=[g12])
            g14 = P.ve(lambda e: e.reciprocal(col[:, 5:6], col[:, 4:5]), after=[g13])
            g15 = P.ve(lambda e, W=W, L=L: e.tensor_single_scalar(
                W, L, col[:, 5:6], ALU.mult), after=[g14])
            w_ops.append(g15)

        if debug:
            dbg_lt = P.sy_dma(ltdbg_d, ltsb[:], after=[ev_lt])
            dbg_lg = P.sy_dma(lgdbg_d, lg[:], after=w_ops)
            dbg_gsc = P.sy_dma(gscdbg_d, gsc[:], after=w_ops)
            dbg_invc = P.sy_dma(invcdbg_d, invc[:], after=ev_inv)
            dbg_xnb = P.sy_dma(xnbdbg_d, xnb[:], after=[xn_ops[-1]] + xn_ops)
            dbg_simb = P.sy_dma(simbdbg_d, simb[:], after=[ld_sim])
            dbg_col = P.sy_dma(coldbg_d, col[:], after=[v_cnr])
            dbg_gt = P.sy_dma(gtdbg_d, gt[:], after=[v_cnm, s_sg])
        wT_ops = []
        for j in range(8):
            tp = TP(P, PB[0:16, 16:144], wtm[:, j * 16:(j + 1) * 16], eyef[:],
                    [w_ops[-1]] + ([wT_ops[-1]] if wT_ops else []))
            wT_ops.append(P.ve(lambda e, j=j: e.tensor_copy(
                wTt[0:16, j * 128:(j + 1) * 128], PB[0:16, 16:144]), after=[tp]))

        xs_guard.__exit__(None, None, None)
        wsp = E(nc.sbuf_tensor("s_wsp", [128, 3 * NCH * 512], BF16, side="right"))
        acc_done = []
        last_evac = {}
        last_ps_read = {}
        for span in range(4):
            wls = []
            for wi, wd in enumerate((wq_d, wk_d, wv_d)):
                prev = acc_done[-1] if acc_done else [xn_ops[-1], st_x]
                wls.append(P.gp_dma(
                    wsp[:, wi * NCH * 512:(wi + 1) * NCH * 512].rearrange(
                        "p (g n) -> p g n", g=NCH),
                    wd[:, span * 512:(span + 1) * 512].rearrange(
                        "(g p) n -> p g n", g=NCH),
                    after=[prev]))
            span_evacs = []
            for t in range(8):
                mms = {0: [], 1: [], 2: []}
                for g in range(NCH):
                    lhs = xnb[:, g * NT + t * 128:g * NT + (t + 1) * 128]
                    for wi, PS in enumerate((PQ, PK, PV)):
                        war = last_ps_read.get(wi, None)
                        mms[wi].append(MM(
                            P, PS[:], lhs,
                            wsp[:, wi * NCH * 512 + g * 512:wi * NCH * 512 + (g + 1) * 512],
                            g == 0, g == NCH - 1,
                            [xn_ops[g], wls[wi]] + ([war] if g == 0 else [])))
                for wi, acc in enumerate((qa, ka, va)):
                    PS = (PQ, PK, PV)[wi]
                    evs = []
                    for eb in range(4):
                        eidx = span * 4 + eb
                        wcol = wtm[:, t * 16 + eidx:t * 16 + eidx + 1]
                        src = PS[:, eb * 128:(eb + 1) * 128]
                        dst = acc[:, t * 128:(t + 1) * 128]
                        if span == 0 and eb == 0:
                            evs.append(P.ve(lambda e, d=dst, s=src, w=wcol:
                                            e.tensor_single_scalar(d, s, w, ALU.mult),
                                            after=[mms[wi][-1], w_ops[t]]))
                        else:
                            chain = evs[-1] if evs else last_evac[(wi, t)]
                            evs.append(P.ve(lambda e, d=dst, s=src, w=wcol:
                                            e.scalar_tensor_tensor(d, s, w, d, ALU.mult, ALU.add),
                                            after=[mms[wi][-1], w_ops[t], chain]))
                    last_evac[(wi, t)] = evs[-1]
                    last_ps_read[wi] = evs[-1]
                    span_evacs.extend(evs)
            acc_done.append(span_evacs[-1])
        all_evacs = [last_evac[(wi, t)] for wi in range(3) for t in range(8)]

        c_q = P.ve(lambda e: e.tensor_copy(qab[:], qa[:]), after=all_evacs)
        c_k = P.ve(lambda e: e.tensor_copy(kab[:], ka[:]), after=all_evacs)
        c_v = P.ve(lambda e: e.tensor_copy(vtb[:], va[:]), after=all_evacs)
        tp_ops = []
        prev_cp = None
        for t in range(16):
            src = qab if t < 8 else kab
            tt = t % 8
            tp = TP(P, PTB[:], src[:, tt * 128:(tt + 1) * 128], eyeb[:],
                    [c_q if t < 8 else c_k, ld_eb] + ([prev_cp] if prev_cp else []))
            dstt = (t < 8)
            cp = P.ve(lambda e, t=t, tt=tt, d=dstt: e.tensor_copy(
                (qfb if d else kfb)[:, tt * 128:(tt + 1) * 128], PTB[:]), after=[tp])
            prev_cp = cp
            tp_ops.append(cp)

        stores = []
        stores.append(P.sy_dma(qf_d, qfb[:], after=[tp_ops[7]] + tp_ops[:8]))
        stores.append(P.sy_dma(kf_d, kfb[:], after=tp_ops[8:]))
        stores.append(P.sy_dma(vt_d.rearrange("(t p) d -> p t d", p=128),
                               vtb[:].rearrange("p (t d) -> p t d", t=8), after=[c_v]))
        stores.append(P.sy_dma(wtm_d.rearrange("(t p) e -> p t e", p=128),
                               wtm[:].rearrange("p (t e) -> p t e", t=8), after=w_ops))
        stores.append(P.sy_dma(wT_d, wTt[:], after=wT_ops))
        if with_attnsum:
            stores.append(st_x)
        if debug:
            stores += [dbg_lt, dbg_lg, dbg_gsc, dbg_invc, dbg_xnb, dbg_simb, dbg_col, dbg_gt]
        P.gp(lambda e: e.nop(), after=stores)
        P.finalize()

    return nc



def emit_ln(P, nc, E, tag, xs, xnb, PA, PB, x_ready, extra_first=(), xs_readers=()):
    xstat = E(nc.sbuf_tensor(f"s_xst{tag}", [128, 2 * NT], BF16))
    sqs = E(nc.sbuf_tensor(f"s_sq{tag}", [128, 2 * NT], BF16))
    row = E(nc.sbuf_tensor(f"s_row{tag}", [1, 6 * NT], F32))
    onesb = E(nc.sbuf_tensor(f"s_ob{tag}", [128, 1], BF16))
    onesr = E(nc.sbuf_tensor(f"s_or{tag}", [1, 128], F32))
    mrep = E(nc.sbuf_tensor(f"s_mr{tag}", [128, NT], F32))
    rrep = E(nc.sbuf_tensor(f"s_rr{tag}", [128, NT], F32))
    m1 = P.ve(lambda e: e.memset(onesb[:], 1.0))
    m2 = P.ve(lambda e: e.memset(onesr[:], 1.0))
    casts, sqss, s1ops, s2ops = [], [], [], []
    for g in range(NCH):
        slot = g % 2
        war_cast = [s1ops[2 * (g - 2) + 1], sqss[g - 2], casts[g - 2]] if g >= 2 else []
        c = P.ve(lambda e, g=g, s=slot: e.tensor_copy(
            xstat[:, s * NT:(s + 1) * NT], xs[:, g * NT:(g + 1) * NT]),
            after=[x_ready] + war_cast + (list(extra_first) if g == 0 else []))
        casts.append(c)
        war_sq = [s2ops[2 * (g - 2) + 1], sqss[g - 2]] if g >= 2 else []
        q = P.sc(lambda e, s=slot: e.activation(
            sqs[:, s * NT:(s + 1) * NT], xstat[:, s * NT:(s + 1) * NT], AF.Square),
            after=[c] + war_sq)
        sqss.append(q)
        for h in range(2):
            s1ops.append(MM(P, PA[0:1, h * 512:(h + 1) * 512], onesb[:],
                            xstat[:, slot * NT + h * 512:slot * NT + (h + 1) * 512],
                            g == 0, g == NCH - 1, [c, m1] + (list(extra_first) if g == 0 else [])))
            s2ops.append(MM(P, PB[0:1, h * 512:(h + 1) * 512], onesb[:],
                            sqs[:, slot * NT + h * 512:slot * NT + (h + 1) * 512],
                            g == 0, g == NCH - 1, [q, m1] + (list(extra_first) if g == 0 else [])))
    mean = row[0:1, 0 * NT:1 * NT]
    var = row[0:1, 1 * NT:2 * NT]
    rstd = row[0:1, 2 * NT:3 * NT]
    t4 = row[0:1, 4 * NT:5 * NT]
    t5 = row[0:1, 5 * NT:6 * NT]
    v1 = P.ve(lambda e: e.tensor_scalar_mul(mean, PA[0:1, 0:NT], 1.0 / D),
              after=[s1ops[-1], s1ops[-2], m2])
    v2 = P.ve(lambda e: e.tensor_scalar_mul(var, PB[0:1, 0:NT], 1.0 / D),
              after=[s2ops[-1], s2ops[-2]])
    v3 = P.ve(lambda e: e.tensor_mul(t4, mean, mean), after=[v1])
    v4 = P.ve(lambda e: e.tensor_sub(var, var, t4), after=[v2, v3])
    v5 = P.ve(lambda e: e.tensor_scalar_add(t4, var, 1e-5), after=[v4])
    s_sq = P.sc(lambda e: e.activation(t5, t4, AF.Sqrt), after=[v5, sqss[-1]])
    v6 = P.ve(lambda e: e.reciprocal(rstd, t5), after=[s_sq])
    reps = []
    for h in range(2):
        reps.append(MM(P, PA[:, h * 512:(h + 1) * 512], onesr[0:1, :],
                       mean[0:1, h * 512:(h + 1) * 512], True, True, [v1, v3, m2]))
    ev_m = P.ve(lambda e: e.tensor_copy(mrep[:], PA[:, 0:NT]), after=reps + [v6])
    reps2 = []
    for h in range(2):
        reps2.append(MM(P, PA[:, h * 512:(h + 1) * 512], onesr[0:1, :],
                        rstd[0:1, h * 512:(h + 1) * 512], True, True, [v6, ev_m]))
    ev_r = P.ve(lambda e: e.tensor_copy(rrep[:], PA[:, 0:NT]), after=reps2)
    xn_ops = []
    prev_b = None
    for g in range(NCH):
        a = P.ve(lambda e, g=g: e.tensor_sub(
            xs[:, g * NT:(g + 1) * NT], xs[:, g * NT:(g + 1) * NT], mrep[:]),
            after=[ev_m, casts[g], sqss[g] if g >= NCH - 2 else None,
                   s1ops[2 * g + 1], s2ops[2 * g + 1]] + list(xs_readers))
        b = P.ve(lambda e, g=g: e.tensor_mul(
            xnb[:, g * NT:(g + 1) * NT], xs[:, g * NT:(g + 1) * NT], rrep[:]),
            after=[a, ev_r])
        xn_ops.append(b)
    stats = dict(mean=mean, var=var, rstd=rstd, row=row, v4=v4, v6=v6,
                 ev_m=ev_m, ev_r=ev_r, reps2=reps2, onesb=onesb, onesr=onesr,
                 m1=m1, m2=m2)
    return xn_ops, stats



def build_ffn0post():
    nc = bass.Bass("TRN2", target_bir_lowering=False, debug=False)
    x_d = nc.dram_tensor("x", [D, NT], F32, kind="ExternalInput").ap()
    qf_d = nc.dram_tensor("qf", [128, NT], BF16, kind="ExternalInput").ap()
    kfull_d = nc.dram_tensor("kfull", [128, SEQ], BF16, kind="ExternalInput").ap()
    vfull_d = nc.dram_tensor("vfull", [SEQ, 128], BF16, kind="ExternalInput").ap()
    wT_d = nc.dram_tensor("wTi", [16, NT], F32, kind="ExternalInput").ap()
    oc_d = nc.dram_tensor("oc", [NCH, NCH, 128, 128], F32, kind="ExternalInput").ap()
    sels_d = nc.dram_tensor("sels", [16, D], F32, kind="ExternalInput").ap()
    cmask_d = nc.dram_tensor("cmask", [SEQ, NT], BF16, kind="ExternalInput").ap()
    xo_d = nc.dram_tensor("xo", [D, NT], F32, kind="ExternalOutput").ap()
    xn_d = nc.dram_tensor("xn", [D, NT], BF16, kind="ExternalOutput").ap()
    import os as _os
    _dbg2 = bool(_os.environ.get("FFN0POST_DEBUG"))
    if _dbg2:
        wadbg_d = nc.dram_tensor("wadbg", [128, NCH * NT], BF16, kind="ExternalOutput").ap()
        zdbg_d = nc.dram_tensor("zdbg", [1, NT], F32, kind="ExternalOutput").ap()
        rcdbg_d = nc.dram_tensor("rcdbg", [16, NT], F32, kind="ExternalOutput").ap()

    with contextlib.ExitStack() as ctx:
        E = ctx.enter_context
        xs = E(nc.sbuf_tensor("s_xs", [128, NCH * NT], F32))
        xnb = E(nc.sbuf_tensor("s_xnb", [128, NCH * NT], BF16))
        wTs = E(nc.sbuf_tensor("s_wTs", [16, NT], F32))
        selsb = E(nc.sbuf_tensor("s_sels", [16, D], F32))
        rowa = E(nc.sbuf_tensor("s_rowa", [1, 3 * NT], F32))
        bcol = E(nc.sbuf_tensor("s_bcol", [128, 1], F32))
        onesb2 = E(nc.sbuf_tensor("s_onesb2", [128, 1], BF16))
        onesr2 = E(nc.sbuf_tensor("s_onesr2", [1, 128], F32))
        rcombo = E(nc.sbuf_tensor("s_rcombo", [16, NT], F32))
        repsb = E(nc.sbuf_tensor("s_repsb", [128, NT], F32))
        g_q = nc.sbuf_tensor("s_qfb", [128, NT], BF16, side="right"); qfb = g_q.__enter__()
        g_k = nc.sbuf_tensor("s_kfb", [128, SEQ], BF16, side="right"); kfb = g_k.__enter__()
        g_v = nc.sbuf_tensor("s_vfb", [128, 64 * 128], BF16, side="right"); vfb = g_v.__enter__()
        g_sq = nc.sbuf_tensor("s_sq", [128, NT], BF16, side="right"); sq = g_sq.__enter__()
        g_E = nc.sbuf_tensor("s_Eslot", [128, 2 * NT], BF16, side="right"); Eslot = g_E.__enter__()
        g_m = nc.sbuf_tensor("s_mslot", [128, 2 * NT], BF16, side="right"); mslot = g_m.__enter__()
        PS0 = E(nc.psum_tensor("s_PS0", [128, 1024], F32))
        PS1 = E(nc.psum_tensor("s_PS1", [128, 1024], F32))
        PAO = E(nc.psum_tensor("s_PAO", [128, 1024], F32))
        PZ = E(nc.psum_tensor("s_PZ", [1, 1024], F32))

        P = Prog2(nc)
        ld_x = P.sy_dma(xs[:], x_d.rearrange("(g p) n -> p g n", g=NCH))
        ld_q = P.sy_dma(qfb[:], qf_d)
        ld_k = P.sy_dma(kfb[:], kfull_d)
        ld_v = P.sy_dma(vfb[:].rearrange("p (t d) -> p t d", t=64),
                        vfull_d.rearrange("(t p) d -> p t d", p=128))
        ld_w = P.sy_dma(wTs[:], wT_d)
        ld_sel = P.sy_dma(selsb[:], sels_d)
        m1 = P.ve(lambda e: e.memset(onesb2[:], 1.0))
        m2 = P.ve(lambda e: e.memset(onesr2[:], 1.0))

        sq_q = P.sc(lambda e: e.activation(sq[:], qfb[:], AF.Square), after=[ld_q])
        mmq = [MM(P, PZ[0:1, h * 512:(h + 1) * 512], onesb2[:],
                  sq[:, h * 512:(h + 1) * 512], True, True, [sq_q, m1])
               for h in range(2)]
        ev_q2 = P.ve(lambda e: e.tensor_copy(rowa[0:1, 0:NT], PZ[0:1, :]), after=mmq)
        vq = P.ve(lambda e: e.reduce_max(rowa[0:1, 2 * NT:2 * NT + 1],
                                         rowa[0:1, 0:NT], axis=AX.X), after=[ev_q2])
        k2ops = []
        prev_sqk = sq_q
        prev_ev = ev_q2
        for c in range(8):
            sk = P.sc(lambda e, c=c: e.activation(
                sq[:], kfb[:, c * NT:(c + 1) * NT], AF.Square),
                after=[ld_k, prev_ev, prev_sqk])
            mk = [MM(P, PZ[0:1, h * 512:(h + 1) * 512], onesb2[:],
                     sq[:, h * 512:(h + 1) * 512], True, True, [sk, prev_ev])
                  for h in range(2)]
            ev = P.ve(lambda e: e.tensor_copy(rowa[0:1, NT:2 * NT], PZ[0:1, :]), after=mk)
            ev2 = P.ve(lambda e, c=c: e.reduce_max(
                rowa[0:1, 2 * NT + 8 + c:2 * NT + 9 + c], rowa[0:1, NT:2 * NT],
                axis=AX.X), after=[ev])
            prev_sqk = sk
            prev_ev = ev2
            k2ops.append(ev2)
        vk = P.ve(lambda e: e.reduce_max(rowa[0:1, 2 * NT + 1:2 * NT + 2],
                                         rowa[0:1, 2 * NT + 8:2 * NT + 16], axis=AX.X), after=k2ops)
        vm = P.ve(lambda e: e.tensor_mul(rowa[0:1, 2 * NT:2 * NT + 1],
                                         rowa[0:1, 2 * NT:2 * NT + 1],
                                         rowa[0:1, 2 * NT + 1:2 * NT + 2]), after=[vq, vk])
        s_m = P.sc(lambda e: e.activation(rowa[0:1, 2 * NT:2 * NT + 1],
                                          rowa[0:1, 2 * NT:2 * NT + 1], AF.Sqrt),
                   after=[vm, prev_sqk])
        v_b = P.ve(lambda e: e.tensor_scalar_mul(rowa[0:1, 2 * NT:2 * NT + 1],
                                                 rowa[0:1, 2 * NT:2 * NT + 1],
                                                 -1.001 * RS), after=[s_m])
        rep_b = MM(P, PS0[:, 0:1], onesr2[0:1, :], rowa[0:1, 2 * NT:2 * NT + 1],
                   True, True, [v_b, m2])
        ev_b = P.ve(lambda e: e.tensor_copy(bcol[:], PS0[:, 0:1]), after=[rep_b])

        PS = (PS0, PS1)
        mload, sTop, expop, mulop, avop, zop = {}, {}, {}, {}, {}, {}
        for kt in range(64):
            sl = kt % 2
            mload[kt] = P.sy_dma(
                mslot[:, sl * NT:(sl + 1) * NT],
                cmask_d[kt * 128:(kt + 1) * 128, :],
                after=[mulop.get(kt - 2)])
            for h in range(2):
                sTop[(kt, h)] = MM(
                    P, PS[sl][:, h * 512:(h + 1) * 512],
                    kfb[:, kt * 128:(kt + 1) * 128],
                    qfb[:, h * 512:(h + 1) * 512],
                    True, True,
                    [ld_k, ld_q, ev_b if kt < 2 else None,
                     expop.get((kt - 2, h)), zop.get((kt - 2, 1))])
                expop[(kt, h)] = P.sc(lambda e, sl=sl, h=h: e.activation(
                    Eslot[:, sl * NT + h * 512:sl * NT + (h + 1) * 512],
                    PS[sl][:, h * 512:(h + 1) * 512],
                    AF.Exp, scale=RS, bias=bcol[:]),
                    after=[sTop[(kt, h)], ev_b, avop.get((kt - 2, h)),
                           zop.get((kt - 2, h)), s_m])
            mulop[kt] = P.ve(lambda e, sl=sl: e.tensor_mul(
                Eslot[:, sl * NT:(sl + 1) * NT],
                Eslot[:, sl * NT:(sl + 1) * NT],
                mslot[:, sl * NT:(sl + 1) * NT]),
                after=[expop[(kt, 0)], expop[(kt, 1)], mload[kt], ev_b])
            for h in range(2):
                avop[(kt, h)] = MM(
                    P, PAO[:, h * 512:(h + 1) * 512],
                    vfb[:, kt * 128:(kt + 1) * 128],
                    Eslot[:, sl * NT + h * 512:sl * NT + (h + 1) * 512],
                    kt == 0, kt == 63,
                    [ld_v, mulop[kt]])
                zop[(kt, h)] = MM(
                    P, PZ[0:1, h * 512:(h + 1) * 512], onesb2[:],
                    Eslot[:, sl * NT + h * 512:sl * NT + (h + 1) * 512],
                    kt == 0, kt == 63,
                    [mulop[kt], ev_q2 if kt == 0 else None,
                     prev_ev if kt == 0 else None])
        ph1_done = [avop[(63, 0)], avop[(63, 1)], zop[(63, 0)], zop[(63, 1)],
                    mulop[63], expop[(63, 0)], expop[(63, 1)],
                    sTop[(63, 0)], sTop[(63, 1)], mload[63]]
        for g in (g_m, g_E, g_sq, g_v, g_k, g_q):
            g.__exit__(None, None, None)
        g_wa = nc.sbuf_tensor("s_wa", [128, NCH * NT], BF16, side="right"); wa = g_wa.__enter__()
        g_oc = nc.sbuf_tensor("s_ocs", [128, 2 * NCH * 128], BF16, side="right"); ocs = g_oc.__enter__()
        ev_z = P.ve(lambda e: e.tensor_copy(rowa[0:1, 0:NT], PZ[0:1, :]),
                    after=[zop[(63, 0)], zop[(63, 1)], vk])
        v_rz = P.ve(lambda e: e.reciprocal(rowa[0:1, NT:2 * NT], rowa[0:1, 0:NT]),
                    after=[ev_z])
        rep_rz = [MM(P, PS0[0:16, h * 512:(h + 1) * 512], onesr2[0:1, 0:16],
                     rowa[0:1, NT + h * 512:NT + (h + 1) * 512], True, True,
                     [v_rz, expop[(63, 0)], expop[(63, 1)], m2])
                  for h in range(2)]
        ev_rc = P.ve(lambda e: e.tensor_mul(rcombo[:], wTs[:], PS0[0:16, :]),
                     after=rep_rz + [ld_w])
        wa_ops = []
        prev_cp = ev_rc
        for e_i in range(16):
            rep = [MM(P, PS1[:, h * 512:(h + 1) * 512],
                      selsb[:, e_i * 128:(e_i + 1) * 128],
                      rcombo[0:16, h * 512:(h + 1) * 512], True, True,
                      [ev_rc, ld_sel, prev_cp])
                   for h in range(2)]
            cp = P.sc(lambda e: e.copy(repsb[:], PS1[:, :]), after=rep)
            w_ = P.ve(lambda e, e_i=e_i: e.tensor_mul(
                wa[:, e_i * NT:(e_i + 1) * NT], PAO[:, :], repsb[:]),
                after=[cp] + ph1_done)
            prev_cp = w_
            wa_ops.append(w_)

        if _dbg2:
            d1 = P.sy_dma(wadbg_d, wa[:], after=wa_ops)
            d2 = P.sy_dma(zdbg_d, rowa[0:1, 0:NT], after=[ev_z, v_rz])
            d3 = P.sy_dma(rcdbg_d, rcombo[:], after=[ev_rc])
        oc_loads = {}
        mmo = {}
        evo = []
        for co in range(NCH):
            sl = co % 2
            oc_loads[co] = P.gp_dma(
                ocs[:, sl * NCH * 128:(sl + 1) * NCH * 128].rearrange(
                    "p (g c) -> p g c", g=NCH),
                oc_d[:, co, :, :].rearrange("g p c -> p g c"),
                after=[mmo.get((co - 2, NCH - 1, 1))] + (ph1_done if co < 2 else []))
            for g in range(NCH):
                for h in range(2):
                    mmo[(co, g, h)] = MM(
                        P, PAO[:, h * 512:(h + 1) * 512],
                        ocs[:, sl * NCH * 128 + g * 128:sl * NCH * 128 + (g + 1) * 128],
                        wa[:, g * NT + h * 512:g * NT + (h + 1) * 512],
                        g == 0, g == NCH - 1,
                        [oc_loads[co], wa_ops[g],
                         wa_ops[-1] if (co == 0 and g == 0) else None,
                         evo[-1] if (g == 0 and evo) else None])
            ev = P.ve(lambda e, co=co: e.scalar_tensor_tensor(
                xs[:, co * NT:(co + 1) * NT], PAO[:, :], 0.5,
                xs[:, co * NT:(co + 1) * NT], ALU.mult, ALU.add),
                after=[mmo[(co, NCH - 1, 0)], mmo[(co, NCH - 1, 1)], ld_x])
            evo.append(ev)

        st1 = P.sy_dma(xo_d.rearrange("(g p) n -> p g n", g=NCH),
                       xs[:].rearrange("p (g n) -> p g n", g=NCH),
                       after=evo)
        g_oc.__exit__(None, None, None)
        g_wa.__exit__(None, None, None)
        xn_ops, _stats = emit_ln(P, nc, E, "L", xs, xnb, PS0, PS1,
                                 evo[-1], extra_first=[ev_rc, wa_ops[-1]] + evo,
                                 xs_readers=[st1])
        st2 = P.sy_dma(xn_d.rearrange("(g p) n -> p g n", g=NCH),
                       xnb[:].rearrange("p (g n) -> p g n", g=NCH),
                       after=xn_ops)
        P.gp(lambda e: e.nop(), after=[st1, st2] + ([d1, d2, d3] if _dbg2 else []))
        P.finalize()
    return nc



def build_attn0():
    nc = bass.Bass("TRN2", target_bir_lowering=False, debug=False)
    xnf_d = nc.dram_tensor("xnf", [D, SEQ], BF16, kind="ExternalInput").ap()
    wq6_d = nc.dram_tensor("wq6", [D, 768], F32, kind="ExternalInput").ap()
    wo2_d = nc.dram_tensor("wo2", [256, D], F32, kind="ExternalInput").ap()
    tri_d = nc.dram_tensor("tri", [128, 4 * 512], BF16, kind="ExternalInput").ap()
    part_d = nc.dram_tensor("part", [D, SEQ], BF16, kind="ExternalOutput").ap()
    _H = 5632
    qdram = dict(
        w1t=nc.dram_tensor("w1t", [D, _H], F32, kind="ExternalInput").ap(),
        w2t=nc.dram_tensor("w2t", [D, _H], F32, kind="ExternalInput").ap(),
        w3t=nc.dram_tensor("w3t", [_H, D], F32, kind="ExternalInput").ap(),
        sl1=nc.dram_tensor("sl1", [D, _H // 8], F32, kind="ExternalInput").ap(),
        sl2=nc.dram_tensor("sl2", [D, _H // 8], F32, kind="ExternalInput").ap(),
        sl3=nc.dram_tensor("sl3", [_H, D // 8], F32, kind="ExternalInput").ap(),
        q1o=nc.dram_tensor("q1o", [D, _H // 8], BF16, kind="ExternalOutput").ap(),
        q2o=nc.dram_tensor("q2o", [D, _H // 8], BF16, kind="ExternalOutput").ap(),
        q3o=nc.dram_tensor("q3o", [_H, D // 8], BF16, kind="ExternalOutput").ap(),
    )

    NSP = SEQ // 512
    xnf_v = xnf_d.rearrange("(g p) n -> p g n", g=NCH)
    with contextlib.ExitStack() as ctx:
        E = ctx.enter_context
        qk = [E(nc.sbuf_tensor(f"s_qk{j}", [128, 2 * SEQ], BF16)) for j in range(2)]
        vsb = [E(nc.sbuf_tensor(f"s_v{j}", [128, 64 * 128], BF16)) for j in range(2)]
        aosb = [E(nc.sbuf_tensor(f"s_ao{j}", [128, SEQ], BF16)) for j in range(2)]
        wq6b = E(nc.sbuf_tensor("s_wq6", [128, NCH * 768], BF16))
        wo2b = E(nc.sbuf_tensor("s_wo2", [128, 2 * D], BF16))
        trib = E(nc.sbuf_tensor("s_tri", [128, 4 * 512], BF16))
        xsl = E(nc.sbuf_tensor("s_xsl", [128, 4 * 512], BF16))
        rowa = E(nc.sbuf_tensor("s_rowa", [1, NT + 64], F32))
        bcol = [E(nc.sbuf_tensor(f"s_bc{j}", [128, 1], F32)) for j in range(2)]
        onesb = E(nc.sbuf_tensor("s_onesb", [128, 1], BF16))
        onesr = E(nc.sbuf_tensor("s_onesr", [1, 128], F32))
        rzrep = E(nc.sbuf_tensor("s_rzrep", [128, 512], F32))
        Esl = E(nc.sbuf_tensor("s_Esl", [128, 6 * 512], BF16))
        sq = Esl[:, 0:1024]
        ppsb = E(nc.sbuf_tensor("s_ppsb", [128, 2 * 512], BF16))
        PQ0 = E(nc.psum_tensor("s_PQ0", [128, 512], F32))
        PK0 = E(nc.psum_tensor("s_PK0", [128, 512], F32))
        PQ1 = E(nc.psum_tensor("s_PQ1", [128, 512], F32))
        PK1 = E(nc.psum_tensor("s_PK1", [128, 512], F32))
        PV0 = E(nc.psum_tensor("s_PV0", [128, 512], F32))
        PV1 = E(nc.psum_tensor("s_PV1", [128, 512], F32))
        PZ = E(nc.psum_tensor("s_PZ", [1, 512], F32))
        PZ2 = E(nc.psum_tensor("s_PZ2", [1, 512], F32))
        PRP = PV0

        P = Prog2(nc)
        ld_w = P.gp_dma(wq6b[:].rearrange("p (g r) -> p g r", g=NCH),
                        wq6_d.rearrange("(g p) r -> p g r", g=NCH))
        ld_o = P.gp_dma(wo2b[:].rearrange("p (j c) -> p j c", j=2),
                        wo2_d.rearrange("(j p) c -> p j c", j=2))
        ld_t = P.sy_dma(trib[:], tri_d)
        quant_sts = emit_quant(P, nc, E, qdram)
        m1 = P.ve(lambda e: e.memset(onesb[:], 1.0))
        m2 = P.ve(lambda e: e.memset(onesr[:], 1.0))

        PQK = (PQ0, PK0, PQ1, PK1)
        ev_all = []
        mm_last = {}
        ld_hist = []
        ev_hist = []
        ev_by_ps = {}
        for sp in range(NSP):
            mms = {r: [] for r in range(6)}
            ldc2 = {}
            for c in range(NCH):
                cp = c // 2
                if c % 2 == 0:
                    sl2 = cp % 2
                    war_mm = [mm_last.get((sp, c - 3)), mm_last.get((sp, c - 4))] \
                        if c >= 4 else ([mm_last.get((sp - 1, NCH - 1)),
                                         mm_last.get((sp - 1, NCH - 2))] if sp else [])
                    q = P.sy_dma if cp % 2 == 0 else P.gp_dma
                    ld2 = q(xsl[:, sl2 * 1024:(sl2 + 1) * 1024].rearrange(
                                "p (c n) -> p c n", c=2),
                            xnf_v[:, c:c + 2, sp * 512:(sp + 1) * 512],
                            after=war_mm)
                    ldc2[cp] = ld2
                ldc = ldc2[cp]
                ld_hist.append(ldc)
                tile = xsl[:, (cp % 2) * 1024 + (c % 2) * 512:
                           (cp % 2) * 1024 + (c % 2) * 512 + 512]
                for r, coloff in enumerate((0, 128, 384, 512)):
                    war_ps = [ev_by_ps.get(r)] if c == 0 else []
                    mms[r].append(MM(
                        P, PQK[r][:],
                        wq6b[:, c * 768 + coloff:c * 768 + coloff + 128],
                        tile, c == 0, c == NCH - 1,
                        [ldc, ld_w] + war_ps))
                for j in range(2):
                    for sub in range(4):
                        war_ps = [ev_by_ps.get(4 + j)] if c == 0 else []
                        mms[4 + j].append(MM(
                            P, (PV0, PV1)[j][:, sub * 128:(sub + 1) * 128],
                            tile[:, sub * 128:(sub + 1) * 128],
                            wq6b[:, c * 768 + 256 + j * 384:c * 768 + 384 + j * 384],
                            c == 0 and sub == 0, c == NCH - 1 and sub == 3,
                            [ldc, ld_w] + war_ps))
                mm_last[(sp, c)] = mms[5][-1]
            evs = []
            for j in range(2):
                e1 = P.sc(lambda e, j=j, sp=sp: e.activation(
                    qk[j][:, sp * 512:(sp + 1) * 512], (PQ0, PQ1)[j][:], AF.Copy),
                    after=[mms[2 * j][-1]])
                ev_by_ps[2 * j] = e1
                e2 = P.sc(lambda e, j=j, sp=sp: e.activation(
                    qk[j][:, SEQ + sp * 512:SEQ + (sp + 1) * 512], (PK0, PK1)[j][:], AF.Copy),
                    after=[mms[2 * j + 1][-1]])
                ev_by_ps[2 * j + 1] = e2
                e3 = P.ve(lambda e, j=j, sp=sp: e.tensor_copy(
                    vsb[j][:, sp * 512:(sp + 1) * 512], (PV0, PV1)[j][:]),
                    after=[mms[4 + j][-1]])
                ev_by_ps[4 + j] = e3
                evs += [e1, e2, e3]
            ev_hist.append(evs[-1])
            ev_all.extend(evs)

        bnd = []
        for j in range(2):
            deps_prev = [bnd[-1]] if bnd else []
            sq_ops = []
            mx_ops = []
            prev = [m1, m2] + ev_all + deps_prev
            for half in range(2):
                for c in range(8):
                    sqo = P.sc(lambda e, j=j, half=half, c=c: e.activation(
                        sq, qk[j][:, half * SEQ + c * NT:half * SEQ + (c + 1) * NT],
                        AF.Square), after=prev + ([sq_ops[-1]] if sq_ops else []) +
                        ([mx_ops[-1]] if mx_ops else []))
                    evh = []
                    for h in range(2):
                        mmh = MM(P, PZ[0:1, :], onesb[:],
                                 sq[:, h * 512:(h + 1) * 512], True, True,
                                 [sqo] + ([mx_ops[-1]] if mx_ops else []) + evh)
                        evh.append(P.ve(lambda e, h=h: e.tensor_copy(
                            rowa[0:1, h * 512:(h + 1) * 512], PZ[0:1, :]), after=[mmh]))
                    mx = P.ve(lambda e, j=j, half=half, c=c: e.reduce_max(
                        rowa[0:1, NT + half * 16 + c:NT + half * 16 + c + 1],
                        rowa[0:1, 0:1024], axis=AX.X), after=evh)
                    sq_ops.append(sqo)
                    mx_ops.append(mx)
            mq = P.ve(lambda e, j=j: e.reduce_max(rowa[0:1, NT + 40:NT + 41],
                                                  rowa[0:1, NT:NT + 8], axis=AX.X),
                      after=mx_ops)
            mk_ = P.ve(lambda e, j=j: e.reduce_max(rowa[0:1, NT + 41:NT + 42],
                                                   rowa[0:1, NT + 16:NT + 24], axis=AX.X),
                       after=mx_ops)
            vm = P.ve(lambda e: e.tensor_mul(rowa[0:1, NT + 40:NT + 41],
                                             rowa[0:1, NT + 40:NT + 41],
                                             rowa[0:1, NT + 41:NT + 42]), after=[mq, mk_])
            sm = P.sc(lambda e: e.activation(rowa[0:1, NT + 40:NT + 41],
                                             rowa[0:1, NT + 40:NT + 41], AF.Sqrt),
                      after=[vm] + sq_ops)
            vb = P.ve(lambda e: e.tensor_scalar_mul(rowa[0:1, NT + 40:NT + 41],
                                                    rowa[0:1, NT + 40:NT + 41],
                                                    -1.001 * RS), after=[sm])
            rp = MM(P, PRP[:, 0:1], onesr[0:1, :], rowa[0:1, NT + 40:NT + 41],
                    True, True, [vb, m2])
            eb = P.ve(lambda e, j=j: e.tensor_copy(bcol[j][:], PRP[:, 0:1]), after=[rp])
            bnd.append(eb)

        PS = (PQ0, PK0, PQ1, PK1)
        PRPh = (PRP, PV1)
        PZh = (PZ, PZ2)
        ao_done = []
        prev_erp = None
        prev_eao = None
        expop = [{} for _ in range(2)]
        avop = [{} for _ in range(2)]
        zops = [{} for _ in range(2)]
        ev_qc = [[], []]
        steps = []
        for qc in range(NSP):
            steps += [(qc, kt) for kt in range(4 * qc + 4)]
            steps.append(("z", qc))
        sched = []
        for idx in range(len(steps)):
            sched.append((0, steps[idx]))
            sched.append((1, steps[idx]))
        pend = []

        def flush_one():
            jh2, qc2, kt2, dm = pend.pop(0)
            nkt2 = 4 * qc2 + 4
            sle2 = 3 * jh2 + kt2 % 3
            avop[jh2][(qc2, kt2)] = MM(
                P, PRPh[jh2][:], vsb[jh2][:, kt2 * 128:(kt2 + 1) * 128],
                Esl[:, sle2 * 512:(sle2 + 1) * 512],
                kt2 == 0, kt2 == nkt2 - 1,
                [dm, (bnd[1] if qc2 == 0 else ev_qc[jh2][-1]) if kt2 == 0 else None])
            zops[jh2][(qc2, kt2)] = MM(
                P, PZh[jh2][0:1, :], onesb[:],
                Esl[:, sle2 * 512:(sle2 + 1) * 512],
                kt2 == 0, kt2 == nkt2 - 1,
                [dm, (bnd[1] if qc2 == 0 else ev_qc[jh2][-1]) if kt2 == 0 else None])

        for jh, step in sched:
            if step[0] == "z":
                while any(p[0] == jh for p in pend):
                    flush_one()
                qc = step[1]
                nkt = 4 * qc + 4
                ez = P.ve(lambda e, jh=jh: e.tensor_copy(
                    rowa[0:1, 0:512], PZh[jh][0:1, :]),
                    after=[zops[jh][(qc, nkt - 1)], prev_eao])
                rz = P.ve(lambda e: e.reciprocal(rowa[0:1, 512:1024], rowa[0:1, 0:512]),
                          after=[ez])
                rpz = MM(P, PS[2 * jh][:], onesr[0:1, :], rowa[0:1, 512:1024],
                         True, True, [rz, expop[jh][(qc, nkt - 1)]])
                erp = P.sc(lambda e, jh=jh: e.copy(rzrep[:], PS[2 * jh][:]),
                           after=[rpz, prev_eao])
                prev_erp = erp
                eao = P.ve(lambda e, jh=jh, qc=qc: e.tensor_mul(
                    aosb[jh][:, qc * 512:(qc + 1) * 512], PRPh[jh][:], rzrep[:]),
                    after=[erp, avop[jh][(qc, nkt - 1)]])
                prev_eao = eao
                ev_qc[jh].append(eao)
                if qc == NSP - 1:
                    ao_done.append(eao)
                continue
            qc, kt = step
            nkt = 4 * qc + 4
            sl = 2 * jh + kt % 2
            sle = 3 * jh + kt % 3
            qrhs = qk[jh][:, qc * 512:(qc + 1) * 512]
            klhs = qk[jh][:, SEQ + kt * 128:SEQ + kt * 128 + 128]
            deps = [bnd[jh], expop[jh].get((qc, kt - 2))]
            if kt < 2:
                deps += [ev_qc[jh][-1] if ev_qc[jh] else None]
                if qc == 0:
                    deps += ev_all[-6:] + [bnd[1]]
                else:
                    deps += [expop[jh].get((qc - 1, 4 * qc - 2 + (kt % 2)))]
                    if kt == 0:
                        deps += [prev_erp]
            sT = MM(P, PS[sl][:], klhs, qrhs, True, True, deps)
            ex = P.sc(lambda e, sle=sle, sl=sl, b=bcol[jh]: e.activation(
                Esl[:, sle * 512:(sle + 1) * 512], PS[sl][:], AF.Exp,
                scale=RS, bias=b[:]),
                after=[sT, bnd[1], avop[jh].get((qc, kt - 3)),
                       zops[jh].get((qc, kt - 3))])
            expop[jh][(qc, kt)] = ex
            m_i = kt - 4 * qc
            dep_mask = ex
            if m_i >= 0:
                dep_mask = P.ve(lambda e, sle=sle, m_i=m_i: e.tensor_mul(
                    Esl[:, sle * 512:(sle + 1) * 512],
                    Esl[:, sle * 512:(sle + 1) * 512],
                    trib[:, m_i * 512:(m_i + 1) * 512]),
                    after=[ex, ld_t])
            pend.append((jh, qc, kt, dep_mask))
            while len(pend) > 4:
                flush_one()

        sts = []
        mmo = {}
        prev_ev_bank = {}
        prev_ev = None
        for co in range(NCH):
            for sp in range(NSP):
                sl = (co * NSP + sp) % 2
                bank = (PV0, PV1)[sl]
                for j in range(2):
                    mmo[(co, sp, j)] = MM(
                        P, bank[:],
                        wo2b[:, j * D + co * 128:j * D + (co + 1) * 128],
                        aosb[j][:, sp * 512:(sp + 1) * 512],
                        j == 0, j == 1,
                        [ld_o, ao_done[j],
                         prev_ev_bank.get(sl) if j == 0 else None,
                         ao_done[1] if (co == 0 and sp < 2) else None,
                         prev_eao if (co == 0 and sp < 2) else None])
                ev = P.ve(lambda e, sl=sl, bank=bank: e.tensor_copy(
                    ppsb[:, sl * 512:(sl + 1) * 512], bank[:]),
                    after=[mmo[(co, sp, 1)], prev_ev,
                           sts[-2] if len(sts) >= 2 else None])
                st = P.sy_dma(part_d[co * 128:(co + 1) * 128, sp * 512:(sp + 1) * 512],
                              ppsb[:, sl * 512:(sl + 1) * 512], after=[ev])
                prev_ev_bank[sl] = ev
                prev_ev = ev
                sts.append(st)
        P.gp(lambda e: e.nop(), after=sts + quant_sts)
        P.finalize()
    return nc



def emit_quant(P, nc, E, dram):
    H = 5632
    SL12 = H // NCORES
    SL3 = D // NCORES
    w1_d, w2_d, w3_d = dram["w1t"], dram["w2t"], dram["w3t"]
    sl1_d, sl2_d, sl3_d = dram["sl1"], dram["sl2"], dram["sl3"]
    q1_d, q2_d, q3_d = dram["q1o"], dram["q2o"], dram["q3o"]
    scr_d = nc.dram_tensor("q_scr", [1, 128], F32, kind="Internal").ap()
    scr2_d = nc.dram_tensor("q_scr2", [1, 8], F32, kind="Internal").ap()

    blk = E(nc.sbuf_tensor("s_qblk", [128, 2 * 2048], F32))
    qa = E(nc.sbuf_tensor("s_qqa", [128, 704], F32))
    qb = E(nc.sbuf_tensor("s_qqb", [128, 704], F32))
    qo = E(nc.sbuf_tensor("s_qqo", [128, 704], BF16))
    acc = E(nc.sbuf_tensor("s_qacc", [128, 152], F32))
    thr = E(nc.sbuf_tensor("s_qthr", [128, 8], F32))
    g1 = E(nc.sbuf_tensor("s_qg1", [1, 144], F32))

    specs = [("w1", w1_d, D, H, [(0, 2048), (2048, 4096), (4096, 5632)], SL12, sl1_d, q1_d, 0),
             ("w2", w2_d, D, H, [(0, 2048), (2048, 4096), (4096, 5632)], SL12, sl2_d, q2_d, 1),
             ("w3", w3_d, H, D, [(0, 2048)], SL3, sl3_d, q3_d, 2)]
    aoff = [0, 50, 100]
    sts = []
    prev_tail = None
    for mi, (nm, wd, R, C, SPL, SL, sld, qd, tcol) in enumerate(specs):
        nch = R // 128
        wv = wd.rearrange("(g p) c -> p g c", g=nch)
        abs_ops = []
        nblk = 0
        for g in range(nch):
            for (c0, c1) in SPL:
                CS = c1 - c0
                sl = nblk % 2
                war = [abs_ops[nblk - 2]] if nblk >= 2 else ([prev_tail] if prev_tail else [])
                ld = P.gp_dma(blk[:, sl * 2048:sl * 2048 + CS],
                              wv[:, g, c0:c1], after=war)
                ab = P.sc(lambda e, sl=sl, CS=CS, col=aoff[mi] + nblk: e.activation(
                    blk[:, sl * 2048:sl * 2048 + CS], blk[:, sl * 2048:sl * 2048 + CS],
                    AF.Abs, accum_out=acc[:, col:col + 1]),
                    after=[ld] + ([abs_ops[-1]] if abs_ops else []))
                abs_ops.append(ab)
                nblk += 1
        red = P.ve(lambda e, mi=mi, nblk=nblk: e.reduce_sum(
            acc[:, 148:149], acc[:, aoff[mi]:aoff[mi] + nblk],
            axis=AX.X), after=abs_ops + ([prev_tail] if prev_tail else []))
        d1 = P.gp_dma(scr_d.rearrange("o p -> p o"), acc[:, 148:149], after=[red])
        d2 = P.gp_dma(g1[0:1, 16:144], scr_d, after=[d1])
        vtot = P.ve(lambda e, mi=mi: e.reduce_sum(
            g1[0:1, mi * 2:mi * 2 + 1], g1[0:1, 16:144], axis=AX.X), after=[d2, red])
        gi = g1[0:1, mi * 2:mi * 2 + 1]
        e2 = P.ve(lambda e, gi=gi, R=R, C=C: e.tensor_scalar(
            gi, gi, 0.5 / (R * C), 0.5e-8, ALU.mult, ALU.add), after=[vtot])
        e3 = P.ve(lambda e, gi=gi, mi=mi: e.tensor_scalar_mul(
            g1[0:1, mi * 2 + 1:mi * 2 + 2], gi, -1.0), after=[e2])
        d3 = P.gp_dma(scr2_d[0:1, mi * 2:mi * 2 + 2], g1[0:1, mi * 2:mi * 2 + 2],
                      after=[e3])
        d4 = P.gp_dma(thr[:, 2 * tcol:2 * tcol + 1],
                      scr2_d[0:1, mi * 2:mi * 2 + 1].broadcast_to([128, 1]),
                      after=[d3])
        d5 = P.gp_dma(thr[:, 2 * tcol + 1:2 * tcol + 2],
                      scr2_d[0:1, mi * 2 + 1:mi * 2 + 2].broadcast_to([128, 1]),
                      after=[d3])
        slv = sld.rearrange("(g p) c -> p g c", g=nch)
        qv = qd.rearrange("(g p) c -> p g c", g=nch)
        q_sts = []
        prev_q = None
        a2s = []
        for g in range(nch):
            sl = g % 2
            ldq = P.gp_dma(blk[:, sl * 2048:sl * 2048 + SL], slv[:, g, :],
                           after=[abs_ops[-1]] + ([a2s[g - 2]] if g >= 2 else []))
            a1 = P.sc(lambda e, sl=sl, SL=SL, c=tcol: e.activation(
                qa[:, 0:SL], blk[:, sl * 2048:sl * 2048 + SL], AF.Sign,
                bias=thr[:, 2 * c + 1:2 * c + 2]),
                after=[ldq, d4, d5] + ([prev_q] if prev_q else []))
            a2 = P.sc(lambda e, sl=sl, SL=SL, c=tcol: e.activation(
                qb[:, 0:SL], blk[:, sl * 2048:sl * 2048 + SL], AF.Sign,
                bias=thr[:, 2 * c:2 * c + 1]),
                after=[ldq, d4, d5, a1])
            a2s.append(a2)
            vv = P.ve(lambda e, SL=SL: e.tensor_add(qo[:, 0:SL], qa[:, 0:SL], qb[:, 0:SL]),
                      after=[a1, a2] + ([q_sts[-1]] if q_sts else []))
            stq = P.gp_dma(qv[:, g, :], qo[:, 0:SL], after=[vv])
            prev_q = vv
            q_sts.append(stq)
        sts.extend(q_sts)
        prev_tail = q_sts[-1]
    return sts



def build_l1ffn():
    nc = bass.Bass("TRN2", target_bir_lowering=False, debug=False)
    H = 5632
    NH = H // 128
    x_d = nc.dram_tensor("x", [D, NT], F32, kind="ExternalInput").ap()
    xn_d = nc.dram_tensor("xni", [D, NT], BF16, kind="ExternalInput").ap()
    q1_d = nc.dram_tensor("q1t", [NH, NCH, 128, 128], BF16, kind="ExternalInput").ap()
    q2_d = nc.dram_tensor("q2t", [NH, NCH, 128, 128], BF16, kind="ExternalInput").ap()
    q3_d = nc.dram_tensor("q3t", [NH, NCH, 128, 128], BF16, kind="ExternalInput").ap()
    xo_d = nc.dram_tensor("xo", [D, NT], F32, kind="ExternalOutput").ap()
    xn2_d = nc.dram_tensor("xn", [D, NT], BF16, kind="ExternalOutput").ap()

    with contextlib.ExitStack() as ctx:
        E = ctx.enter_context
        xs = E(nc.sbuf_tensor("s_xs", [128, NCH * NT], F32))
        xnb = E(nc.sbuf_tensor("s_xnb", [128, NCH * NT], BF16))
        w12 = E(nc.sbuf_tensor("s_w12", [128, 4 * NCH * 128], BF16))
        w3s = E(nc.sbuf_tensor("s_w3s", [128, 2 * NH * 128], BF16))
        sgs = E(nc.sbuf_tensor("s_sgs", [128, 512], F32))
        g_h = nc.sbuf_tensor("s_hsb", [128, NH * 512], BF16, side="right")
        hsb = g_h.__enter__()
        PX1 = E(nc.psum_tensor("s_PX1", [128, 512], F32))
        PX2 = E(nc.psum_tensor("s_PX2", [128, 512], F32))
        PO = E(nc.psum_tensor("s_PO", [128, 512], F32))
        PA = E(nc.psum_tensor("s_PA", [128, 1024], F32))
        PB = E(nc.psum_tensor("s_PB", [128, 1024], F32))

        P = Prog2(nc)
        ld_x = P.sy_dma(xs[:], x_d.rearrange("(g p) n -> p g n", g=NCH))
        ld_xn = P.sy_dma(xnb[:], xn_d.rearrange("(g p) n -> p g n", g=NCH))

        resid = []
        h_last = {}
        w12_lds, w3_lds = {}, {}
        mm_tail = {}
        sg_ops = {}
        for half in range(2):
            for mt in range(NH):
                sl = mt % 2
                war = [sg_ops.get((half, mt - 2)), h_last.get((half, mt - 2))] \
                    if mt >= 2 else [sg_ops.get((half - 1, NH - 2)),
                                     h_last.get((half - 1, NH - 2)),
                                     sg_ops.get((half - 1, NH - 1)),
                                     h_last.get((half - 1, NH - 1))]
                l1 = P.gp_dma(w12[:, sl * 2 * NCH * 128:sl * 2 * NCH * 128 + NCH * 128]
                              .rearrange("p (g m) -> p g m", g=NCH),
                              q1_d[mt].rearrange("g p m -> p g m"), after=war)
                l2 = P.gp_dma(w12[:, sl * 2 * NCH * 128 + NCH * 128:(sl + 1) * 2 * NCH * 128]
                              .rearrange("p (g m) -> p g m", g=NCH),
                              q2_d[mt].rearrange("g p m -> p g m"), after=war)
                w12_lds[(half, mt)] = (l1, l2)
                mm1 = mm2 = None
                for c in range(NCH):
                    base = sl * 2 * NCH * 128
                    rhs = xnb[:, c * NT + half * 512:c * NT + (half + 1) * 512]
                    mm1 = MM(P, PX1[:], w12[:, base + c * 128:base + (c + 1) * 128],
                             rhs, c == 0, c == NCH - 1,
                             [l1, ld_xn] + ([sg_ops.get((half, mt - 1)),
                                             sg_ops.get((half - 1, NH - 1))] if c == 0 else []))
                    mm2 = MM(P, PX2[:], w12[:, base + NCH * 128 + c * 128:base + NCH * 128 + (c + 1) * 128],
                             rhs, c == 0, c == NCH - 1,
                             [l2, ld_xn] + ([h_last.get((half, mt - 1)),
                                             h_last.get((half - 1, NH - 1))] if c == 0 else []))
                mm_tail[(half, mt)] = mm2
                sg = P.sc(lambda e: e.activation(sgs[:], PX1[:], AF.Sigmoid, scale=0.5),
                          after=[mm1])
                sg_ops[(half, mt)] = sg
                hh = P.ve(lambda e, mt=mt: e.scalar_tensor_tensor(
                    hsb[:, mt * 512:(mt + 1) * 512], PX2[:], 0.5, sgs[:],
                    ALU.mult, ALU.mult), after=[sg, mm2])
                h_last[(half, mt)] = hh
            for co in range(NCH):
                sl = co % 2
                war = [mm_tail.get((half, co - 2, "o"))] if co >= 2 else \
                    ([mm_tail.get((half - 1, NCH - 2, "o")),
                      mm_tail.get((half - 1, NCH - 1, "o"))] if half else [])
                l3 = P.gp_dma(w3s[:, sl * NH * 128:(sl + 1) * NH * 128]
                              .rearrange("p (g m) -> p g m", g=NH),
                              q3_d[:, co, :, :].rearrange("g p m -> p g m"),
                              after=war)
                w3_lds[(half, co)] = l3
                mmo = None
                for hc in range(NH):
                    mmo = MM(P, PO[:],
                             w3s[:, sl * NH * 128 + hc * 128:sl * NH * 128 + (hc + 1) * 128],
                             hsb[:, hc * 512:(hc + 1) * 512],
                             hc == 0, hc == NH - 1,
                             [l3, h_last[(half, hc)]] +
                             ([resid[-1]] if (co or half) and hc == 0 else []))
                mm_tail[(half, co, "o")] = mmo
                rv = P.ve(lambda e, co=co, half=half: e.scalar_tensor_tensor(
                    xs[:, co * NT + half * 512:co * NT + (half + 1) * 512],
                    PO[:], 0.25,
                    xs[:, co * NT + half * 512:co * NT + (half + 1) * 512],
                    ALU.mult, ALU.add), after=[mmo, ld_x])
                resid.append(rv)

        st1 = P.sy_dma(xo_d.rearrange("(g p) n -> p g n", g=NCH),
                       xs[:].rearrange("p (g n) -> p g n", g=NCH),
                       after=resid)
        g_h.__exit__(None, None, None)
        xn_ops, _ = emit_ln(P, nc, E, "L", xs, xnb, PA, PB,
                            resid[-1], extra_first=resid + [st1],
                            xs_readers=[st1])
        st2 = P.sy_dma(xn2_d.rearrange("(g p) n -> p g n", g=NCH),
                       xnb[:].rearrange("p (g n) -> p g n", g=NCH),
                       after=xn_ops)
        P.gp(lambda e: e.nop(), after=[st1, st2])
        P.finalize()
    return nc



def build_l1attn_pre():
    nc = bass.Bass("TRN2", target_bir_lowering=False, debug=False)
    xn_d = nc.dram_tensor("xni", [D, NT], BF16, kind="ExternalInput").ap()
    wqkvT_d = nc.dram_tensor("wqkvT", [D, 3 * D], F32, kind="ExternalInput").ap()
    wlrT_d = nc.dram_tensor("wlrT", [D, 48], F32, kind="ExternalInput").ap()
    w0T_d = nc.dram_tensor("w0T", [D, 128], F32, kind="ExternalInput").ap()
    w2T_d = nc.dram_tensor("w2T", [D, 128], F32, kind="ExternalInput").ap()
    w1_d = nc.dram_tensor("w1f", [D, 128], F32, kind="ExternalInput").ap()
    eyeb_d = nc.dram_tensor("eyeb_d", [128, 128], BF16, kind="ExternalInput").ap()
    dwp_d = nc.dram_tensor("dwp", [3 * D, 128], F32, kind="ExternalOutput").ap()
    qn_d = nc.dram_tensor("qn", [D, NT], BF16, kind="ExternalOutput").ap()

    with contextlib.ExitStack() as ctx:
        E = ctx.enter_context
        xnb = E(nc.sbuf_tensor("s_xnb", [128, NCH * NT], BF16))
        ktm = E(nc.sbuf_tensor("s_ktm", [128, 8 * D], BF16))
        vtm = E(nc.sbuf_tensor("s_vtm", [128, 8 * D], BF16))
        wsp = E(nc.sbuf_tensor("s_wsp", [128, 2 * NCH * 512], BF16))
        wlrb = E(nc.sbuf_tensor("s_wlrb", [128, NCH * 48], BF16))
        qstrip = E(nc.sbuf_tensor("s_qstrip", [128, 4 * NT], BF16))
        qtmp = E(nc.sbuf_tensor("s_qtmp", [128, 2 * 512], BF16))
        sqscr = E(nc.sbuf_tensor("s_sqscr", [128, 128], F32))
        ncol = E(nc.sbuf_tensor("s_ncol", [128, 16], F32))
        lrtm = E(nc.sbuf_tensor("s_lrtm", [128, 8 * 48], F32))
        lsc = E(nc.sbuf_tensor("s_lsc", [128, 8 * 48], F32))
        lscb = E(nc.sbuf_tensor("s_lscb", [128, 8 * 48], F32))
        w0b = E(nc.sbuf_tensor("s_w0b", [128, 16 * 128], BF16))
        w2b = E(nc.sbuf_tensor("s_w2b", [128, 16 * 128], BF16))
        w1b = E(nc.sbuf_tensor("s_w1b", [128, 16 * 128], BF16))
        kfmh = E(nc.sbuf_tensor("s_kfmh", [128, 2 * NT], BF16))
        vfmh = E(nc.sbuf_tensor("s_vfmh", [128, 2 * NT], BF16))
        gbaf = E(nc.sbuf_tensor("s_gbaf", [128, NT], F32))
        sgf = E(nc.sbuf_tensor("s_sgf", [128, NT], F32))
        hbmf = E(nc.sbuf_tensor("s_hbmf", [128, NT], F32))
        dhf = E(nc.sbuf_tensor("s_dhf", [128, NT], F32))
        slgf = E(nc.sbuf_tensor("s_slgf", [128, NT], F32))
        hid16 = E(nc.sbuf_tensor("s_hid16", [128, NT], BF16))
        dhm16 = E(nc.sbuf_tensor("s_dhm16", [128, NT], BF16))
        dgb16 = E(nc.sbuf_tensor("s_dgb16", [128, NT], BF16))
        hlr = E(nc.sbuf_tensor("s_hlr", [128, NT], BF16))
        klr0 = E(nc.sbuf_tensor("s_klr0", [128, NT], BF16))
        klr2 = E(nc.sbuf_tensor("s_klr2", [128, NT], BF16))
        dwsml = E(nc.sbuf_tensor("s_dwsml", [128, 128], F32))
        eyeb = E(nc.sbuf_tensor("s_eyeb", [128, 128], BF16))
        PG = E(nc.psum_tensor("s_PG", [128, 1024], F32))
        PH = E(nc.psum_tensor("s_PH", [128, 1024], F32))
        PD = E(nc.psum_tensor("s_PD", [128, 1024], F32))
        PT2 = E(nc.psum_tensor("s_PT2", [128, 128], BF16))
        PW = E(nc.psum_tensor("s_PW", [128, 128], F32))

        P = Prog2(nc)
        ld_xn = P.sy_dma(xnb[:], xn_d.rearrange("(g p) n -> p g n", g=NCH))
        ld_ey = P.sy_dma(eyeb[:], eyeb_d)
        ld_w0 = P.gp_dma(w0b[:].rearrange("p (g m) -> p g m", g=NCH),
                         w0T_d.rearrange("(g p) m -> p g m", g=NCH))
        ld_w2 = P.gp_dma(w2b[:].rearrange("p (g m) -> p g m", g=NCH),
                         w2T_d.rearrange("(g p) m -> p g m", g=NCH))
        ld_w1 = P.gp_dma(w1b[:].rearrange("p (g m) -> p g m", g=NCH),
                         w1_d.rearrange("(g p) m -> p g m", g=NCH))
        ld_lr = P.gp_dma(wlrb[:].rearrange("p (g m) -> p g m", g=NCH),
                         wlrT_d.rearrange("(g p) m -> p g m", g=NCH))

        lr_v2 = []
        v0_prev = {}
        for nt in range(8):
            pb = PH[:, (nt % 2) * 512:(nt % 2) * 512 + 48]
            mmr = None
            for c in range(NCH):
                mmr = MM(P, pb,
                         xnb[:, c * NT + nt * 128:c * NT + (nt + 1) * 128],
                         wlrb[:, c * 48:(c + 1) * 48],
                         c == 0, c == NCH - 1,
                         [ld_xn, ld_lr] +
                         ([v0_prev.get(nt % 2)] if c == 0 else []))
            raw = lsc[:, nt * 48:(nt + 1) * 48]
            bb = lscb[:, nt * 48:(nt + 1) * 48]
            v0 = P.ve(lambda e, raw=raw, pb=pb: e.tensor_scalar(
                raw, pb, 1.0, BASE_LR_INV, ALU.mult, ALU.add), after=[mmr])
            v0_prev[nt % 2] = v0
            a1 = P.sc(lambda e, raw=raw, bb=bb: e.activation(bb, raw, AF.Abs),
                      after=[v0])
            a2 = P.sc(lambda e, bb=bb: e.activation(bb, bb, AF.Exp, scale=-1.0),
                      after=[a1])
            a3 = P.sc(lambda e, bb=bb: e.activation(bb, bb, AF.Ln, bias=1.0),
                      after=[a2])
            dst = lrtm[:, nt * 48:(nt + 1) * 48]
            v1 = P.ve(lambda e, d=dst, raw=raw: e.tensor_scalar_max(d, raw, 0.0),
                      after=[v0, a1])
            v2 = P.ve(lambda e, d=dst, bb=bb: e.tensor_add(d, d, bb),
                      after=[v1, a3])
            lr_v2.append(v2)

        state = dict(tp=None, stq=None, sq=None, gmm={}, pd={}, ncol={},
                     qtmp={}, slot={})
        ktail, vtail = {}, {}
        st_qs = []

        def emit_span(sp, gi):
            j = sp // 4
            hbase = (sp % 4) * 4
            sl = gi % 2
            wld = P.gp_dma(
                wsp[:, sl * 8192:(sl + 1) * 8192].rearrange(
                    "p (g m) -> p g m", g=NCH),
                wqkvT_d[:, sp * 512:(sp + 1) * 512].rearrange(
                    "(g p) m -> p g m", g=NCH),
                after=[state["slot"].get(gi - 2)])
            for nt in range(8):
                gx = (gi * 8 + nt) % 2
                pb = PD[:, gx * 512:gx * 512 + 512]
                mmq = None
                for c in range(NCH):
                    mmq = MM(P, pb,
                             xnb[:, c * NT + nt * 128:c * NT + (nt + 1) * 128],
                             wsp[:, sl * 8192 + c * 512:sl * 8192 + (c + 1) * 512],
                             c == 0, c == NCH - 1,
                             [ld_xn, wld] +
                             ([state["pd"].get(gx)] if c == 0 else []))
                state["slot"][gi] = mmq
                if j == 0:
                    dst = qtmp[:, (nt % 2) * 512:(nt % 2) * 512 + 512]
                    war = state["qtmp"].get(nt % 2, [])
                elif j == 1:
                    dst = ktm[:, nt * D + hbase * 128:nt * D + hbase * 128 + 512]
                    war = []
                else:
                    dst = vtm[:, nt * D + hbase * 128:nt * D + hbase * 128 + 512]
                    war = []
                evq = P.sc(lambda e, d=dst, pb=pb: e.activation(d, pb, AF.Silu),
                           after=[mmq] + war)
                state["pd"][gx] = evq
                if j == 2:
                    for hs in range(4):
                        vtail[(hbase + hs, nt)] = evq
                    continue
                nb = (nt % 2) * 8
                sqs = []
                for hs in range(4):
                    tile = dst[:, hs * 128:(hs + 1) * 128]
                    sqo = P.sc(lambda e, t=tile, c=nb + hs: e.activation(
                        sqscr[:], t, AF.Square, accum_out=ncol[:, c:c + 1]),
                        after=[evq, state["sq"], state["ncol"].get(nt % 2)])
                    state["sq"] = sqo
                    sqs.append(sqo)
                sro = P.sc(lambda e, nb=nb: e.activation(
                    ncol[:, nb + 4:nb + 8], ncol[:, nb:nb + 4], AF.Sqrt),
                    after=sqs + state["ncol"].get(nt % 2, []))
                va = P.ve(lambda e, nb=nb: e.tensor_scalar_add(
                    ncol[:, nb + 4:nb + 8], ncol[:, nb + 4:nb + 8], 1e-5),
                    after=[sro])
                vb = P.ve(lambda e, nb=nb: e.reciprocal(
                    ncol[:, nb + 4:nb + 8], ncol[:, nb + 4:nb + 8]), after=[va])
                vcs = []
                for hs in range(4):
                    tile = dst[:, hs * 128:(hs + 1) * 128]
                    vc = P.ve(lambda e, t=tile, c=nb + 4 + hs: e.tensor_single_scalar(
                        t, t, ncol[:, c:c + 1], ALU.mult), after=[vb])
                    vcs.append(vc)
                    if j == 1:
                        ktail[(hbase + hs, nt)] = vc
                    else:
                        tp = TP(P, PT2[:], tile, eyeb[:],
                                [vc, ld_ey, state["tp"]])
                        cp = P.ve(lambda e, hs=hs, nt=nt: e.tensor_copy(
                            qstrip[:, hs * NT + nt * 128:hs * NT + (nt + 1) * 128],
                            PT2[:]), after=[tp, state["stq"]])
                        state["tp"] = cp
                        vcs[-1] = cp
                state["ncol"][nt % 2] = vcs
                if j == 0:
                    state["qtmp"][nt % 2] = vcs
            if j == 0:
                stq = P.sy_dma(
                    qn_d[hbase * 128:(hbase + 4) * 128, :].rearrange(
                        "(h p) n -> p h n", h=4),
                    qstrip[:].rearrange("p (h n) -> p h n", h=4),
                    after=[state["tp"]])
                state["stq"] = stq
                st_qs.append(stq)

        for gi, sp in enumerate((4, 5, 6, 7, 8, 9, 10, 11)):
            emit_span(sp, gi)

        pw = dict(e_d=None, e_g=None, e_h=None, e_s=None, e_slg=None,
                  v_hid=None, v_dhm=None, v_dg=None, vb2=None, vb4=None,
                  vdgb=None, dw0=None, dw1=None, dw2=None, scl=None,
                  dwev=None, stw=None)
        dw_sts = []
        for h in range(16):
            par = h % 2
            kbuf = kfmh[:, par * NT:(par + 1) * NT]
            vbuf = vfmh[:, par * NT:(par + 1) * NT]
            cpk, cpv = [], []
            for nt in range(8):
                tpk = TP(P, PT2[:], ktm[:, nt * D + h * 128:nt * D + (h + 1) * 128],
                         eyeb[:], [ktail[(h, nt)], ld_ey, state["tp"]])
                ck = P.sc(lambda e, kbuf=kbuf, nt=nt: e.copy(
                    kbuf[:, nt * 128:(nt + 1) * 128], PT2[:]), after=[tpk])
                state["tp"] = ck
                cpk.append(ck)
                tpv = TP(P, PT2[:], vtm[:, nt * D + h * 128:nt * D + (h + 1) * 128],
                         eyeb[:], [vtail[(h, nt)], state["tp"]])
                cv = P.ve(lambda e, vbuf=vbuf, nt=nt: e.tensor_copy(
                    vbuf[:, nt * 128:(nt + 1) * 128], PT2[:]), after=[tpv])
                state["tp"] = cv
                cpv.append(cv)
            mm_g = []
            for nt in range(8):
                mm_g.append(MM(P, PG[:, nt * 128:(nt + 1) * 128],
                               kbuf[:, nt * 128:(nt + 1) * 128],
                               w0b[:, h * 128:(h + 1) * 128],
                               nt % 4 == 0, nt % 4 == 3,
                               [cpk[nt], ld_w0] +
                               ([pw["e_d"]] if nt in (0, 4) else [])))
            e_g = P.sc(lambda e: e.copy(gbaf[:], PG[:]),
                       after=[mm_g[3], mm_g[7], pw["vb2"], pw["e_slg"]])
            mm_h = []
            for nt in range(8):
                mm_h.append(MM(P, PH[:, nt * 128:(nt + 1) * 128],
                               kbuf[:, nt * 128:(nt + 1) * 128],
                               w2b[:, h * 128:(h + 1) * 128],
                               nt % 4 == 0, nt % 4 == 3,
                               [cpk[nt], ld_w2] +
                               (([pw["e_h"]] if h else
                                 [v0_prev[0], v0_prev[1]]) if nt in (0, 4) else [])))
            e_h = P.sc(lambda e: e.copy(hbmf[:], PH[:]),
                       after=[mm_h[3], mm_h[7], pw["v_dg"], pw["v_hid"]])
            mm_d = []
            for nt in range(8):
                mm_d.append(MM(P, PG[:, nt * 128:(nt + 1) * 128],
                               vbuf[:, nt * 128:(nt + 1) * 128],
                               w1b[:, h * 128:(h + 1) * 128],
                               nt % 4 == 0, nt % 4 == 3,
                               [cpv[nt], ld_w1] +
                               ([e_g] if nt in (0, 4) else [])))
            e_d = P.sc(lambda e: e.copy(dhf[:], PG[:]),
                       after=[mm_d[3], mm_d[7], pw["vdgb"]])
            e_s = P.sc(lambda e: e.activation(sgf[:], gbaf[:], AF.Sigmoid),
                       after=[e_g, pw["vb4"]])
            e_slg = P.sc(lambda e: e.activation(slgf[:], gbaf[:], AF.Silu),
                         after=[e_g, pw["vdgb"]])
            v_hid = P.ve(lambda e: e.tensor_mul(hid16[:], slgf[:], hbmf[:]),
                         after=[e_slg, e_h, pw["scl"]])
            v_dhm = P.ve(lambda e: e.tensor_mul(dhm16[:], dhf[:], slgf[:]),
                         after=[e_slg, e_d, pw["dw2"]])
            v_dg = P.ve(lambda e: e.tensor_mul(dhf[:], dhf[:], hbmf[:]),
                        after=[v_dhm, e_d, e_h])
            v_b1 = P.sc(lambda e: e.activation(slgf[:], sgf[:], AF.Copy,
                                               scale=-1.0, bias=1.0),
                        after=[e_s, v_hid, v_dhm])
            v_b2 = P.ve(lambda e: e.tensor_mul(slgf[:], gbaf[:], slgf[:]),
                        after=[v_b1])
            v_b3 = P.sc(lambda e: e.activation(slgf[:], slgf[:], AF.Copy,
                                               bias=1.0), after=[v_b2])
            v_b4 = P.ve(lambda e: e.tensor_mul(slgf[:], slgf[:], sgf[:]),
                        after=[v_b3])
            v_dgb = P.ve(lambda e: e.tensor_mul(dgb16[:], dhf[:], slgf[:]),
                         after=[v_b4, v_dg, pw["dw0"]])
            scs = []
            for nt in range(8):
                l0 = lrtm[:, nt * 48 + h * 3 + 0:nt * 48 + h * 3 + 1]
                l1c = lrtm[:, nt * 48 + h * 3 + 1:nt * 48 + h * 3 + 2]
                l2c = lrtm[:, nt * 48 + h * 3 + 2:nt * 48 + h * 3 + 3]
                sc0 = P.ve(lambda e, nt=nt, h=h, l=l0: e.tensor_single_scalar(
                    klr0[:, nt * 128:(nt + 1) * 128],
                    ktm[:, nt * D + h * 128:nt * D + (h + 1) * 128],
                    l, ALU.mult),
                    after=[ktail[(h, nt)], lr_v2[nt], pw["dw0"]])
                sc1 = P.ve(lambda e, nt=nt, l=l1c: e.tensor_single_scalar(
                    hlr[:, nt * 128:(nt + 1) * 128],
                    hid16[:, nt * 128:(nt + 1) * 128],
                    l, ALU.mult), after=[v_hid, lr_v2[nt], pw["dw1"]])
                sc2 = P.ve(lambda e, nt=nt, h=h, l=l2c: e.tensor_single_scalar(
                    klr2[:, nt * 128:(nt + 1) * 128],
                    ktm[:, nt * D + h * 128:nt * D + (h + 1) * 128],
                    l, ALU.mult),
                    after=[ktail[(h, nt)], lr_v2[nt], pw["dw2"]])
                scs.append((sc0, sc1, sc2))
            dw_specs = (
                (dgb16, klr0, v_dgb, 0),
                (None, hlr, None, 1),
                (dhm16, klr2, v_dhm, 2))
            for lh, rh, dep, j2 in dw_specs:
                mmw = None
                for nt in range(8):
                    lhs = (vtm[:, nt * D + h * 128:nt * D + (h + 1) * 128]
                           if j2 == 1 else lh[:, nt * 128:(nt + 1) * 128])
                    dps = [scs[nt][j2]]
                    if j2 == 1:
                        dps.append(vtail[(h, nt)])
                    else:
                        dps.append(dep)
                    mmw = MM(P, PW[:], lhs, rh[:, nt * 128:(nt + 1) * 128],
                             nt == 0, nt == 7,
                             dps + ([pw["dwev"]] if nt == 0 else []))
                evw = P.ve(lambda e: e.tensor_copy(dwsml[:], PW[:]),
                           after=[mmw, pw["stw"]])
                stw = P.sy_dma(dwp_d[(j2 * 16 + h) * 128:(j2 * 16 + h + 1) * 128, :],
                               dwsml[:], after=[evw])
                dw_sts.append(stw)
                pw["dwev"] = evw
                pw["stw"] = stw
                pw["dw%d" % j2] = mmw
            pw.update(e_d=e_d, e_g=e_g, e_h=e_h, e_s=e_s, e_slg=e_slg,
                      v_hid=v_hid, v_dhm=v_dhm, v_dg=v_dg, vb2=v_b2,
                      vb4=v_b4, vdgb=v_dgb, scl=scs[-1][1])
            if h in (1, 3, 5, 7):
                qi = (h - 1) // 2
                emit_span(qi, 8 + qi)
        P.gp(lambda e: e.nop(), after=dw_sts + st_qs)
        P.finalize()
    return nc



def build_l1attn_post():
    nc = bass.Bass("TRN2", target_bir_lowering=False, debug=False)
    x_d = nc.dram_tensor("x", [D, NT], F32, kind="ExternalInput").ap()
    qn_d = nc.dram_tensor("qni", [D, NT], BF16, kind="ExternalInput").ap()
    dwall_d = nc.dram_tensor("dwall", [NCORES * 3 * D, 128], F32, kind="ExternalInput").ap()
    w0n_d = nc.dram_tensor("w0n", [D, 128], F32, kind="ExternalInput").ap()
    w1n_d = nc.dram_tensor("w1n", [D, 128], F32, kind="ExternalInput").ap()
    w2n_d = nc.dram_tensor("w2n", [D, 128], F32, kind="ExternalInput").ap()
    onw_d = nc.dram_tensor("onw", [128, 1], F32, kind="ExternalInput").ap()
    woT_d = nc.dram_tensor("woT", [NCH, NCH, 128, 128], F32, kind="ExternalInput").ap()
    eyeb_d = nc.dram_tensor("eyeb_d", [128, 128], BF16, kind="ExternalInput").ap()
    eyef_d = nc.dram_tensor("eyef_d", [128, 128], F32, kind="ExternalInput").ap()
    xo_d = nc.dram_tensor("xo", [D, NT], F32, kind="ExternalOutput").ap()
    xn_d = nc.dram_tensor("xn", [D, NT], BF16, kind="ExternalOutput").ap()

    dwv = dwall_d.rearrange("(r j p) m -> p r j m", r=NCORES, j=48)
    wn_ds = (w0n_d, w1n_d, w2n_d)
    with contextlib.ExitStack() as ctx:
        E = ctx.enter_context
        xs = E(nc.sbuf_tensor("s_xs", [128, NCH * NT], F32))
        ocat = E(nc.sbuf_tensor("s_ocat", [128, 16 * NT], BF16))
        wfT = [E(nc.sbuf_tensor(f"s_wfT{i}", [128, 16 * 128], BF16)) for i in range(3)]
        eyeb = E(nc.sbuf_tensor("s_eyeb", [128, 128], BF16))
        eyef = E(nc.sbuf_tensor("s_eyef", [128, 128], F32))
        onesf = E(nc.sbuf_tensor("s_onesf", [128, 1], F32))
        onesr = E(nc.sbuf_tensor("s_onesr", [1, 128], F32))
        onwc = E(nc.sbuf_tensor("s_onw", [128, 1], F32))
        xb16 = E(nc.sbuf_tensor("s_xb16", [128, 6 * 128], BF16))
        tb16 = E(nc.sbuf_tensor("s_tb16", [128, 6 * 128], BF16))
        ab16 = E(nc.sbuf_tensor("s_ab16", [128, 6 * 128], BF16))
        bm16 = E(nc.sbuf_tensor("s_bm16", [128, 6 * 128], BF16))
        colA = E(nc.sbuf_tensor("s_colA", [128, 48], F32))
        g1A = E(nc.sbuf_tensor("s_g1A", [1, 24], F32))

        P = Prog2(nc)
        ld_x = P.sy_dma(xs[:], x_d.rearrange("(g p) n -> p g n", g=NCH))
        ld_ey = P.sy_dma(eyeb[:], eyeb_d)
        ld_ef = P.sy_dma(eyef[:], eyef_d)
        ld_on = P.sy_dma(onwc[:], onw_d)
        m1 = P.ve(lambda e: e.memset(onesf[:], 1.0))
        m2 = P.ve(lambda e: e.memset(onesr[:], 1.0))

        gA = nc.sbuf_tensor("s_phA", [128, 40 * 128], F32, side="right")
        phA = gA.__enter__()
        OF_DW = 0
        OF_WO = 6 * 128
        OF_SL = 12 * 128
        OF_SC = 28 * 128
        OF_W2 = 34 * 128
        gPW1 = nc.psum_tensor("s_PW1", [128, 128], F32)
        PW1 = gPW1.__enter__()
        gPT2 = nc.psum_tensor("s_PT2", [128, 128], BF16)
        PT2 = gPT2.__enter__()
        gP6 = [nc.psum_tensor(f"s_P6_{k}", [128, 128], F32) for k in range(6)]
        P6 = [g.__enter__() for g in gP6]
        prev_tp = None
        evd_prev = None
        slot_mms = {}
        ldws = {}
        final_ops = []
        for b in range(8):
            mats = list(range(b * 6, b * 6 + 6))
            last_x, last_t, last_p = {}, {}, {}
            for j2 in mats:
                k = j2 % 6
                sl = j2 % 2
                dst = phA[:, OF_SL + sl * 8 * 128:OF_SL + (sl + 1) * 8 * 128]
                qf = P.sy_dma if sl == 0 else P.gp_dma
                ldj = qf(dst.rearrange("p (r m) -> p r m", r=8),
                         dwv[:, :, j2, :], after=[slot_mms.get(j2 - 2)])
                ldws[j2] = P.sy_dma(
                    phA[:, OF_WO + k * 128:OF_WO + (k + 1) * 128],
                    wn_ds[j2 // 16][(j2 % 16) * 128:(j2 % 16 + 1) * 128, :],
                    after=[final_ops[j2 - 6][0]] if j2 >= 6 else [])
                mms = None
                for r in range(NCORES):
                    mms = MM(P, PW1[:], eyef[:],
                             phA[:, OF_SL + sl * 8 * 128 + r * 128:
                                 OF_SL + sl * 8 * 128 + (r + 1) * 128],
                             r == 0, r == NCORES - 1,
                             [ldj, ld_ef, evd_prev] if r == 0 else [ldj])
                slot_mms[j2] = mms
                old = [final_ops[j2 - 6][0]] if j2 >= 6 else []
                oldt = [final_ops[j2 - 6][1]] if j2 >= 6 else []
                evd = P.ve(lambda e, k=k: e.tensor_copy(
                    phA[:, OF_DW + k * 128:OF_DW + (k + 1) * 128], PW1[:]),
                    after=[mms] + old)
                evd_prev = evd
                cst = P.ve(lambda e, k=k: e.tensor_copy(
                    xb16[:, k * 128:(k + 1) * 128],
                    phA[:, OF_DW + k * 128:OF_DW + (k + 1) * 128]),
                    after=[evd] + oldt)
                sq1 = P.sc(lambda e, k=k: e.activation(
                    phA[:, OF_SC + k * 128:OF_SC + (k + 1) * 128],
                    xb16[:, k * 128:(k + 1) * 128], AF.Square,
                    accum_out=colA[:, k * 8:k * 8 + 1]),
                    after=[cst] + old)
                nmm = MM(P, P6[k][0:1, 0:1], onesf[:], colA[:, k * 8:k * 8 + 1],
                         True, True, [sq1, m1] + old)
                e1 = P.ve(lambda e, k=k: e.tensor_copy(
                    g1A[0:1, k * 4:k * 4 + 1], P6[k][0:1, 0:1]), after=[nmm])
                s1_ = P.sc(lambda e, k=k: e.activation(
                    g1A[0:1, k * 4 + 1:k * 4 + 2], g1A[0:1, k * 4:k * 4 + 1],
                    AF.Sqrt), after=[e1])
                e2 = P.ve(lambda e, k=k: e.tensor_scalar_add(
                    g1A[0:1, k * 4 + 1:k * 4 + 2], g1A[0:1, k * 4 + 1:k * 4 + 2],
                    1e-7), after=[s1_])
                e3 = P.ve(lambda e, k=k: e.reciprocal(
                    g1A[0:1, k * 4 + 2:k * 4 + 3], g1A[0:1, k * 4 + 1:k * 4 + 2]),
                    after=[e2])
                rp = MM(P, P6[k][:, 1:2], onesr[0:1, :],
                        g1A[0:1, k * 4 + 2:k * 4 + 3], True, True, [e3, m2, e1])
                e4 = P.ve(lambda e, k=k: e.tensor_copy(
                    colA[:, k * 8 + 1:k * 8 + 2], P6[k][:, 1:2]), after=[rp])
                vx = P.ve(lambda e, k=k: e.tensor_single_scalar(
                    xb16[:, k * 128:(k + 1) * 128], xb16[:, k * 128:(k + 1) * 128],
                    colA[:, k * 8 + 1:k * 8 + 2], ALU.mult), after=[e4, sq1])
                tp0 = TP(P, PT2[:], xb16[:, k * 128:(k + 1) * 128], eyeb[:],
                         [vx, ld_ey, prev_tp])
                ct0 = P.ve(lambda e, k=k: e.tensor_copy(
                    tb16[:, k * 128:(k + 1) * 128], PT2[:]), after=[tp0])
                prev_tp = ct0
                last_x[j2], last_t[j2], last_p[j2] = vx, ct0, e4
            for it in range(5):
                a_, b_, c_ = NS_COEFFS[it]
                mG, eG, mG2, vB2, mX, vX, mT = {}, {}, {}, {}, {}, {}, {}
                for j2 in mats:
                    k = j2 % 6
                    mG[j2] = MM(P, P6[k][:], xb16[:, k * 128:(k + 1) * 128],
                                xb16[:, k * 128:(k + 1) * 128], True, True,
                                [last_x[j2], last_p[j2]])
                for j2 in mats:
                    k = j2 % 6
                    eG[j2] = P.sc(lambda e, k=k: e.copy(
                        ab16[:, k * 128:(k + 1) * 128], P6[k][:]), after=[mG[j2]])
                for j2 in mats:
                    k = j2 % 6
                    mG2[j2] = MM(P, P6[k][:], ab16[:, k * 128:(k + 1) * 128],
                                 ab16[:, k * 128:(k + 1) * 128], True, True,
                                 [eG[j2]])
                for j2 in mats:
                    k = j2 % 6
                    v1_ = P.sc(lambda e, k=k, c_=c_: e.activation(
                        bm16[:, k * 128:(k + 1) * 128], P6[k][:], AF.Copy,
                        scale=c_), after=[mG2[j2]])
                    vB2[j2] = P.ve(lambda e, k=k, b_=b_: e.scalar_tensor_tensor(
                        bm16[:, k * 128:(k + 1) * 128],
                        ab16[:, k * 128:(k + 1) * 128], b_,
                        bm16[:, k * 128:(k + 1) * 128], ALU.mult, ALU.add),
                        after=[v1_])
                for j2 in mats:
                    k = j2 % 6
                    mX[j2] = MM(P, P6[k][:], tb16[:, k * 128:(k + 1) * 128],
                                bm16[:, k * 128:(k + 1) * 128], True, True,
                                [vB2[j2], last_t[j2]])
                for j2 in mats:
                    k = j2 % 6
                    vX[j2] = P.ve(lambda e, k=k, a_=a_: e.scalar_tensor_tensor(
                        xb16[:, k * 128:(k + 1) * 128],
                        xb16[:, k * 128:(k + 1) * 128], a_, P6[k][:],
                        ALU.mult, ALU.add), after=[mX[j2]])
                    last_x[j2] = vX[j2]
                    last_p[j2] = vX[j2]
                if it < 4:
                    for j2 in mats:
                        k = j2 % 6
                        mT[j2] = MM(P, P6[k][:], bm16[:, k * 128:(k + 1) * 128],
                                    tb16[:, k * 128:(k + 1) * 128], True, True,
                                    [vX[j2]])
                    for j2 in mats:
                        k = j2 % 6
                        vT = P.ve(lambda e, k=k, a_=a_: e.scalar_tensor_tensor(
                            tb16[:, k * 128:(k + 1) * 128],
                            tb16[:, k * 128:(k + 1) * 128], a_, P6[k][:],
                            ALU.mult, ALU.add), after=[mT[j2]])
                        last_t[j2] = vT
                        last_p[j2] = vT
            for j2 in mats:
                k = j2 % 6
                vw = P.ve(lambda e, k=k: e.tensor_add(
                    phA[:, OF_W2 + k * 128:OF_W2 + (k + 1) * 128],
                    xb16[:, k * 128:(k + 1) * 128],
                    phA[:, OF_DW + k * 128:OF_DW + (k + 1) * 128]),
                    after=[last_x[j2]])
                sq2 = P.sc(lambda e, k=k: e.activation(
                    phA[:, OF_SC + k * 128:OF_SC + (k + 1) * 128],
                    phA[:, OF_W2 + k * 128:OF_W2 + (k + 1) * 128],
                    AF.Square, accum_out=colA[:, k * 8 + 2:k * 8 + 3]),
                    after=[vw])
                s2_ = P.sc(lambda e, k=k: e.activation(
                    colA[:, k * 8 + 3:k * 8 + 4], colA[:, k * 8 + 2:k * 8 + 3],
                    AF.Sqrt), after=[sq2])
                e5 = P.ve(lambda e, k=k: e.tensor_scalar_add(
                    colA[:, k * 8 + 3:k * 8 + 4], colA[:, k * 8 + 3:k * 8 + 4],
                    1e-5), after=[s2_])
                e6 = P.ve(lambda e, k=k: e.reciprocal(
                    colA[:, k * 8 + 4:k * 8 + 5], colA[:, k * 8 + 3:k * 8 + 4]),
                    after=[e5])
                sq3 = P.sc(lambda e, k=k: e.activation(
                    phA[:, OF_SC + k * 128:OF_SC + (k + 1) * 128],
                    phA[:, OF_WO + k * 128:OF_WO + (k + 1) * 128],
                    AF.Square, accum_out=colA[:, k * 8 + 5:k * 8 + 6]),
                    after=[s2_, ldws[j2]])
                s3_ = P.sc(lambda e, k=k: e.activation(
                    colA[:, k * 8 + 6:k * 8 + 7], colA[:, k * 8 + 5:k * 8 + 6],
                    AF.Sqrt), after=[sq3])
                e7 = P.ve(lambda e, k=k: e.tensor_mul(
                    colA[:, k * 8 + 7:k * 8 + 8], colA[:, k * 8 + 4:k * 8 + 5],
                    colA[:, k * 8 + 6:k * 8 + 7]), after=[s3_, e6])
                vwf = P.ve(lambda e, k=k: e.tensor_single_scalar(
                    phA[:, OF_W2 + k * 128:OF_W2 + (k + 1) * 128],
                    phA[:, OF_W2 + k * 128:OF_W2 + (k + 1) * 128],
                    colA[:, k * 8 + 7:k * 8 + 8], ALU.mult), after=[e7, sq2])
                cwf = P.ve(lambda e, k=k: e.tensor_copy(
                    xb16[:, k * 128:(k + 1) * 128],
                    phA[:, OF_W2 + k * 128:OF_W2 + (k + 1) * 128]),
                    after=[vwf])
                tpw = TP(P, PT2[:], xb16[:, k * 128:(k + 1) * 128], eyeb[:],
                         [cwf, prev_tp])
                cww = P.ve(lambda e, j2=j2: e.tensor_copy(
                    wfT[j2 // 16][:, (j2 % 16) * 128:(j2 % 16 + 1) * 128],
                    PT2[:]), after=[tpw])
                prev_tp = cww
                final_ops.append((cww, tpw))
        ns_last = final_ops[-1][0]
        phA_tail = [ns_last, evd_prev, slot_mms[46], slot_mms[47]]
        for g in reversed(gP6):
            g.__exit__(None, None, None)
        gPT2.__exit__(None, None, None)
        gPW1.__exit__(None, None, None)
        gA.__exit__(None, None, None)

        gB2 = nc.sbuf_tensor("s_phB2", [128, 16 * NT], BF16, side="right")
        qnb = gB2.__enter__()
        gB3 = nc.sbuf_tensor("s_phB3", [128, 4 * NT], F32, side="right")
        scrB = gB3.__enter__()
        gB4 = nc.sbuf_tensor("s_phB4", [128, 4 * NT], BF16, side="right")
        gg = gB4.__enter__()
        gPS = [nc.psum_tensor(f"s_aps{i}", [128, 512], F32) for i in range(8)]
        PS8 = [g.__enter__() for g in gPS]
        qnv = qn_d.rearrange("(h p) n -> p h n", h=16)
        ld_qn = [P.sy_dma(qnb[:, c * 4 * NT:(c + 1) * 4 * NT]
                          .rearrange("p (h n) -> p h n", h=4),
                          qnv[:, c * 4:(c + 1) * 4, :], after=phA_tail)
                 for c in range(4)]
        ap_tail = []
        prev_eon, prev_erp = {}, {}
        for g in range(8):
            steps = [(s // 2, s % 2) for s in range(4 * g, 4 * g + 4)]
            mg, mh, egl, eghl, mo, sqo, zmm, ezl, rrp, erp = (
                {}, {}, {}, {}, {}, {}, {}, {}, {}, {})
            for i, (h, hh) in enumerate(steps):
                qseg = qnb[:, h * NT + hh * 512:h * NT + (hh + 1) * 512]
                mg[i] = MM(P, PS8[i][:], wfT[0][:, h * 128:(h + 1) * 128], qseg,
                           True, True, [ld_qn[h // 4], prev_eon.get(i)])
                mh[i] = MM(P, PS8[4 + i][:], wfT[2][:, h * 128:(h + 1) * 128],
                           qseg, True, True, [ld_qn[h // 4], prev_erp.get(i)])
            for i in range(4):
                egl[i] = P.sc(lambda e, i=i: e.activation(
                    gg[:, i * 512:(i + 1) * 512], PS8[i][:], AF.Silu),
                    after=[mg[i]])
            for i in range(4):
                eghl[i] = P.ve(lambda e, i=i: e.tensor_mul(
                    gg[:, 2048 + i * 512:2048 + (i + 1) * 512],
                    gg[:, i * 512:(i + 1) * 512], PS8[4 + i][:]),
                    after=[egl[i], mh[i]])
            for i, (h, hh) in enumerate(steps):
                mo[i] = MM(P, PS8[i][:], wfT[1][:, h * 128:(h + 1) * 128],
                           gg[:, 2048 + i * 512:2048 + (i + 1) * 512],
                           True, True, [eghl[i], egl[i]])
            for i in range(4):
                sqo[i] = P.sc(lambda e, i=i: e.activation(
                    scrB[:, i * 512:(i + 1) * 512], PS8[i][:], AF.Square),
                    after=[mo[i]])
            for i in range(4):
                zmm[i] = MM(P, PS8[4 + i][0:1, :], onesf[:],
                            scrB[:, i * 512:(i + 1) * 512], True, True,
                            [sqo[i], m1, eghl[i]])
            for i in range(4):
                ezl[i] = P.ve(lambda e, i=i: e.tensor_copy(
                    scrB[0:1, i * 512:(i + 1) * 512], PS8[4 + i][0:1, :]),
                    after=[zmm[i]])
            er = P.ve(lambda e: e.tensor_scalar(
                scrB[0:1, 0:2048], scrB[0:1, 0:2048], 1.0 / 128, 1e-5,
                ALU.mult, ALU.add), after=list(ezl.values()))
            sr = P.sc(lambda e: e.activation(
                scrB[0:1, 0:2048], scrB[0:1, 0:2048], AF.Sqrt), after=[er])
            vr = P.ve(lambda e: e.reciprocal(
                scrB[0:1, 0:2048], scrB[0:1, 0:2048]), after=[sr])
            for i in range(4):
                rrp[i] = MM(P, PS8[4 + i][:], onesr[0:1, :],
                            scrB[0:1, i * 512:(i + 1) * 512], True, True,
                            [vr, m2, ezl[i]])
            for i in range(4):
                erp[i] = P.sc(lambda e, i=i: e.copy(
                    scrB[:, 2048 + i * 512:2048 + (i + 1) * 512],
                    PS8[4 + i][:]), after=[rrp[i], vr])
            for i, (h, hh) in enumerate(steps):
                eon = P.ve(lambda e, i=i, h=h, hh=hh: e.scalar_tensor_tensor(
                    ocat[:, h * NT + hh * 512:h * NT + (hh + 1) * 512],
                    PS8[i][:], onwc[:, 0:1],
                    scrB[:, 2048 + i * 512:2048 + (i + 1) * 512],
                    ALU.mult, ALU.mult), after=[erp[i], ld_on, mo[i], sqo[i]])
                prev_eon[i] = eon
                prev_erp[i] = erp[i]
                ap_tail.append(eon)
        gB4.__exit__(None, None, None)
        gB3.__exit__(None, None, None)
        gB2.__exit__(None, None, None)

        PG2, PH2 = PS8[0], PS8[1]
        for g in reversed(gPS[2:]):
            g.__exit__(None, None, None)
        gC = nc.sbuf_tensor("s_phC", [128, 2 * NCH * 128], BF16, side="right")
        wos = gC.__enter__()
        resid = []
        oc_lds = {}
        last_mm = {}
        for co in range(NCH):
            sl = co % 2
            oc_lds[co] = P.gp_dma(
                wos[:, sl * NCH * 128:(sl + 1) * NCH * 128].rearrange(
                    "p (g m) -> p g m", g=NCH),
                woT_d[:, co, :, :].rearrange("g p m -> p g m"),
                after=[last_mm.get(co - 2)] + (ap_tail[-4:] if co < 2 else []))
            mm = None
            for ci in range(NCH):
                mm = MM(P, PG2[:],
                        wos[:, sl * NCH * 128 + ci * 128:sl * NCH * 128 + (ci + 1) * 128],
                        ocat[:, ci * NT:ci * NT + 512],
                        ci == 0, ci == NCH - 1,
                        [oc_lds[co]] + ap_tail[-4:] +
                        ([resid[-1]] if (co and ci == 0) else []))
            mmb = None
            for ci in range(NCH):
                mmb = MM(P, PH2[:],
                         wos[:, sl * NCH * 128 + ci * 128:sl * NCH * 128 + (ci + 1) * 128],
                         ocat[:, ci * NT + 512:ci * NT + NT],
                         ci == 0, ci == NCH - 1,
                         [oc_lds[co]] + ap_tail[-4:] +
                         ([resid[-1]] if (co and ci == 0) else []))
            last_mm[co] = mmb
            rv1 = P.ve(lambda e, co=co: e.scalar_tensor_tensor(
                xs[:, co * NT:co * NT + 512], PG2[:], 1.0,
                xs[:, co * NT:co * NT + 512], ALU.mult, ALU.add),
                after=[mm, ld_x])
            rv2 = P.ve(lambda e, co=co: e.scalar_tensor_tensor(
                xs[:, co * NT + 512:co * NT + NT], PH2[:], 1.0,
                xs[:, co * NT + 512:co * NT + NT], ALU.mult, ALU.add),
                after=[mmb, ld_x, rv1])
            resid.append(rv2)
        st1 = P.sy_dma(xo_d.rearrange("(g p) n -> p g n", g=NCH),
                       xs[:].rearrange("p (g n) -> p g n", g=NCH),
                       after=resid)
        gC.__exit__(None, None, None)
        for g in reversed(gPS[:2]):
            g.__exit__(None, None, None)
        gXN = nc.sbuf_tensor("s_xnb", [128, NCH * NT], BF16, side="right")
        xnb = gXN.__enter__()
        PA = E(nc.psum_tensor("s_PA", [128, 1024], F32))
        PB = E(nc.psum_tensor("s_PB", [128, 1024], F32))
        xn_ops, _ = emit_ln(P, nc, E, "L", xs, xnb, PA, PB,
                            resid[-1], extra_first=resid + [st1],
                            xs_readers=[st1])
        st2 = P.sy_dma(xn_d.rearrange("(g p) n -> p g n", g=NCH),
                       xnb[:].rearrange("p (g n) -> p g n", g=NCH),
                       after=xn_ops)
        P.gp(lambda e: e.nop(), after=[st1, st2])
        gXN.__exit__(None, None, None)
        P.finalize()
    return nc



_NC_CACHE = {}


def _get_nc(name, builder, *args):
    if name not in _NC_CACHE:
        _NC_CACHE[name] = builder(*args)
    return _NC_CACHE[name]


def _run(nc, in_maps):
    res = run_bass_kernel_spmd(nc, in_maps, core_ids=list(range(NCORES)))
    return res.results


def kernel(**inp):
    ac = np.ascontiguousarray
    x0T = ac(np.asarray(inp["L0_attn_x"])[0].T)
    EYEB = np.eye(128).astype(BF)
    EYEF = np.eye(128, dtype=np.float32)
    SELS = np.zeros((16, D), np.float32)
    for e in range(16):
        SELS[e, e * 128:(e + 1) * 128] = 1.0
    TRI = np.zeros((128, 4 * 512), dtype=BF)
    for m in range(4):
        for p in range(128):
            TRI[p, m * 512 + 128 * m + p:(m + 1) * 512] = 1
    CMASK = []
    kk = np.arange(SEQ)[:, None]
    for i in range(NCORES):
        qq = i * NT + np.arange(NT)[None, :]
        CMASK.append((kk <= qq).astype(BF))

    simw = np.asarray(inp["L0_ffn_sim_matrix"], np.float32)
    gates = np.asarray(inp["L0_ffn_gates"], np.float32).reshape(1, 16)
    WQ = ac(np.asarray(inp["L0_ffn_q_proj"]).transpose(1, 0, 2).reshape(D, D))
    WK = ac(np.asarray(inp["L0_ffn_k_proj"]).transpose(1, 0, 2).reshape(D, D))
    WV = ac(np.asarray(inp["L0_ffn_v_proj"]).transpose(1, 0, 2).reshape(D, D))
    OC = ac(np.asarray(inp["L0_ffn_o_proj"]).reshape(D, D)
            .reshape(16, 128, 16, 128).transpose(0, 2, 1, 3))
    w_qkv0 = np.asarray(inp["L0_attn_w_qkv"], np.float32)
    w_out0 = np.asarray(inp["L0_attn_w_out"], np.float32)

    wqkvT1 = ac(np.asarray(inp["L1_attn_w_qkv"]).T)
    wlrT1 = ac(np.asarray(inp["L1_attn_w_lr"]).T)
    w0_1 = np.asarray(inp["L1_attn_w0"], np.float32)
    w1_1 = np.asarray(inp["L1_attn_w1"], np.float32)
    w2_1 = np.asarray(inp["L1_attn_w2"], np.float32)
    w0T = ac(w0_1.transpose(0, 2, 1).reshape(D, 128))
    w2T = ac(w2_1.transpose(0, 2, 1).reshape(D, 128))
    w1f = ac(w1_1.reshape(D, 128))
    w0n = ac(w0_1.reshape(D, 128))
    w1n = ac(w1_1.reshape(D, 128))
    w2n = ac(w2_1.reshape(D, 128))
    onw = ac(np.asarray(inp["L1_attn_o_norm_w"], np.float32).reshape(128, 1))
    woT1 = ac(np.asarray(inp["L1_attn_w_o"]).T
              .reshape(16, 128, 16, 128).transpose(0, 2, 1, 3))
    w1T_f = ac(np.asarray(inp["L1_ffn_w1"]).T)
    w2T_f = ac(np.asarray(inp["L1_ffn_w2"]).T)
    w3T_f = ac(np.asarray(inp["L1_ffn_w3"]).T)

    nc_pre0 = _get_nc("pre0", build_ffn0pre, False)
    nc_pre1 = _get_nc("pre1", build_ffn0pre, True)
    nc_post = _get_nc("post", build_ffn0post)
    nc_attn = _get_nc("attn0", build_attn0)
    nc_l1ffn = _get_nc("l1ffn", build_l1ffn)
    nc_l1pre = _get_nc("l1pre", build_l1attn_pre)
    nc_l1post = _get_nc("l1post", build_l1attn_post)
    H = 5632

    def shards(xT):
        return [ac(xT[:, i * NT:(i + 1) * NT]) for i in range(NCORES)]

    xsh = shards(x0T)
    r1 = _run(nc_pre0, [dict(x=xsh[i], simw=simw, gates=gates,
                             wq=WQ, wk=WK, wv=WV, eyeb_d=EYEB, eyef_d=EYEF)
                        for i in range(NCORES)])
    kfull = np.concatenate([r1[i]["kf"] for i in range(NCORES)], axis=1)
    vfull = np.concatenate([r1[i]["vt"] for i in range(NCORES)], axis=0)
    r2 = _run(nc_post, [dict(x=xsh[i], qf=r1[i]["qf"], kfull=kfull, vfull=vfull,
                             wTi=r1[i]["wT"], oc=OC, sels=SELS, cmask=CMASK[i])
                        for i in range(NCORES)])
    x1 = [r2[i]["xo"] for i in range(NCORES)]
    xn1full = np.concatenate([r2[i]["xn"] for i in range(NCORES)], axis=1)
    attn_maps = []
    for i in range(NCORES):
        h0, h1 = 2 * i, 2 * i + 1
        cols = []
        for h in (h0, h1):
            cols += [w_qkv0[:, h * 128:(h + 1) * 128],
                     w_qkv0[:, D + h * 128:D + (h + 1) * 128],
                     w_qkv0[:, 2 * D + h * 128:2 * D + (h + 1) * 128]]
        wq6 = ac(np.concatenate(cols, axis=1))
        wo2 = ac(w_out0[h0 * 128:(h1 + 1) * 128, :])
        attn_maps.append(dict(
            xnf=xn1full, wq6=wq6, wo2=wo2, tri=TRI,
            w1t=w1T_f, w2t=w2T_f, w3t=w3T_f,
            sl1=ac(w1T_f[:, i * (H // 8):(i + 1) * (H // 8)]),
            sl2=ac(w2T_f[:, i * (H // 8):(i + 1) * (H // 8)]),
            sl3=ac(w3T_f[:, i * (D // 8):(i + 1) * (D // 8)])))
    r3 = _run(nc_attn, attn_maps)
    q1 = np.concatenate([r3[i]["q1o"] for i in range(NCORES)], axis=1)
    q2 = np.concatenate([r3[i]["q2o"] for i in range(NCORES)], axis=1)
    q3 = np.concatenate([r3[i]["q3o"] for i in range(NCORES)], axis=1)
    q1t = ac(q1.reshape(16, 128, 44, 128).transpose(2, 0, 1, 3))
    q2t = ac(q2.reshape(16, 128, 44, 128).transpose(2, 0, 1, 3))
    q3t = ac(q3.reshape(44, 128, 16, 128).transpose(0, 2, 1, 3))
    maps4 = []
    for i in range(NCORES):
        parts = np.concatenate(
            [np.asarray(r3[r]["part"][:, i * NT:(i + 1) * NT]) for r in range(NCORES)],
            axis=0)
        maps4.append(dict(x=x1[i], parts=ac(parts), simw=simw, gates=gates,
                          wq=WQ, wk=WK, wv=WV, eyeb_d=EYEB, eyef_d=EYEF))
    r4 = _run(nc_pre1, maps4)
    x2 = [r4[i]["xout"] for i in range(NCORES)]
    kfull = np.concatenate([r4[i]["kf"] for i in range(NCORES)], axis=1)
    vfull = np.concatenate([r4[i]["vt"] for i in range(NCORES)], axis=0)
    r5 = _run(nc_post, [dict(x=x2[i], qf=r4[i]["qf"], kfull=kfull, vfull=vfull,
                             wTi=r4[i]["wT"], oc=OC, sels=SELS, cmask=CMASK[i])
                        for i in range(NCORES)])
    x3 = [r5[i]["xo"] for i in range(NCORES)]
    xn3 = [r5[i]["xn"] for i in range(NCORES)]
    r7 = _run(nc_l1ffn, [dict(x=x3[i], xni=xn3[i], q1t=q1t, q2t=q2t, q3t=q3t)
                         for i in range(NCORES)])
    x4 = [r7[i]["xo"] for i in range(NCORES)]
    xn4 = [r7[i]["xn"] for i in range(NCORES)]
    r8 = _run(nc_l1pre, [dict(xni=xn4[i], wqkvT=wqkvT1, wlrT=wlrT1,
                              w0T=w0T, w2T=w2T, w1f=w1f, eyeb_d=EYEB)
                         for i in range(NCORES)])
    dwall = np.concatenate([np.asarray(r8[i]["dwp"]) for i in range(NCORES)], axis=0)
    r9 = _run(nc_l1post, [dict(x=x4[i], qni=r8[i]["qn"], dwall=dwall,
                               w0n=w0n, w1n=w1n, w2n=w2n, onw=onw, woT=woT1,
                               eyeb_d=EYEB, eyef_d=EYEF)
                          for i in range(NCORES)])
    x5 = [r9[i]["xo"] for i in range(NCORES)]
    xn5 = [r9[i]["xn"] for i in range(NCORES)]
    r10 = _run(nc_l1ffn, [dict(x=x5[i], xni=xn5[i], q1t=q1t, q2t=q2t, q3t=q3t)
                          for i in range(NCORES)])
    x6 = np.concatenate([np.asarray(r10[i]["xo"]) for i in range(NCORES)], axis=1)
    return ac(x6.T).reshape(1, SEQ, D).astype(np.float32)


# version 06
# speedup vs baseline: 1.2037x; 1.2037x over previous
import math
import contextlib
import numpy as np
import ml_dtypes
import concourse.bass as bass
import concourse.mybir as mybir
from concourse.bass_utils import run_bass_kernel_spmd

F32 = mybir.dt.float32
BF16 = mybir.dt.bfloat16
AF = mybir.ActivationFunctionType
ALU = mybir.AluOpType
AX = mybir.AxisListType
BF = ml_dtypes.bfloat16

NCORES = 8
SEQ = 8192
D = 2048
NT = 1024
NCH = D // 128
RS = 1.0 / math.sqrt(128.0)

BASE_LR = 0.01
BASE_LR_INV = BASE_LR + math.log(-math.expm1(-BASE_LR))
NS_COEFFS = [(4.0848, -6.8946, 2.927), (3.9505, -6.3029, 2.6377),
             (3.7418, -5.5913, 2.3037), (2.8769, -3.1427, 1.2046),
             (2.8366, -3.0525, 1.2012)]

ENG_TE, ENG_SC, ENG_VE, ENG_GP, ENG_SYQ, ENG_GPQ = range(6)
_DMA_ENGS = (ENG_SYQ, ENG_GPQ)


class _Op:
    __slots__ = ("eng", "fn", "after", "idx", "inc", "seq", "planned")

    def __init__(self, eng, fn, after, idx):
        self.eng = eng
        self.fn = fn
        self.after = after
        self.idx = idx
        self.inc = 0
        self.planned = 0


class Prog:
    def __init__(self, nc):
        self.nc = nc
        self.ops = []

    def _add(self, eng, fn, after):
        deps = []
        for a in after:
            if a is None:
                continue
            if isinstance(a, (list, tuple)):
                deps.extend(x for x in a if x is not None)
            else:
                deps.append(a)
        op = _Op(eng, fn, deps, len(self.ops))
        self.ops.append(op)
        return op.idx

    def te(self, fn, after=()):
        return self._add(ENG_TE, fn, after)

    def sc(self, fn, after=()):
        return self._add(ENG_SC, fn, after)

    def ve(self, fn, after=()):
        return self._add(ENG_VE, fn, after)

    def gp(self, fn, after=()):
        return self._add(ENG_GP, fn, after)

    def sy_dma(self, out, in_, after=()):
        return self._add(ENG_SYQ, lambda e: e.dma_start(out=out, in_=in_), after)

    def gp_dma(self, out, in_, after=()):
        return self._add(ENG_GPQ, lambda e: e.dma_start(out=out, in_=in_), after)



class Prog2(Prog):

    DMA_POOL = 12
    STRICT = True

    def finalize(self, final_engine=None):
        nc = self.nc
        ops = self.ops
        K = self.DMA_POOL
        dma_no = {}
        qcount = {ENG_SYQ: 0, ENG_GPQ: 0}
        for op in ops:
            if op.eng in _DMA_ENGS:
                dma_no[op.idx] = qcount[op.eng]
                qcount[op.eng] += 1
                op.inc = 16
        if self.STRICT:
            for op in ops:
                if op.eng not in _DMA_ENGS:
                    op.inc = 1
        else:
            for op in ops:
                for d in op.after:
                    p = ops[d]
                    if p.eng not in _DMA_ENGS and p.eng != op.eng and p.inc == 0:
                        p.inc = 1
        counts = [0] * 6
        for op in ops:
            if op.inc and op.eng not in _DMA_ENGS:
                counts[op.eng] += op.inc
                op.planned = counts[op.eng]
            elif op.eng in _DMA_ENGS:
                op.planned = 16 * (dma_no[op.idx] // K + 1)
        streams = [[] for _ in range(6)]
        for op in ops:
            streams[op.eng].append(op)
        gp_merged = sorted(streams[ENG_GP] + streams[ENG_GPQ], key=lambda o: o.idx)
        extra = {}
        for q in _DMA_ENGS:
            qs = streams[q]
            for i, op in enumerate(qs):
                if i >= K:
                    extra.setdefault(op.idx, []).append(qs[i - K].idx)
        exec_streams = [streams[ENG_TE], streams[ENG_SC], streams[ENG_VE],
                        gp_merged, streams[ENG_SYQ]]
        pos = [0] * len(exec_streams)
        done = [False] * len(ops)
        ndone = 0
        while ndone < len(ops):
            progress = False
            for e in range(len(exec_streams)):
                while pos[e] < len(exec_streams[e]):
                    op = exec_streams[e][pos[e]]
                    deps = op.after + extra.get(op.idx, [])
                    if all(done[d] for d in deps):
                        done[op.idx] = True
                        ndone += 1
                        pos[e] += 1
                        progress = True
                    else:
                        break
            if not progress:
                stuck = []
                for e in range(len(exec_streams)):
                    if pos[e] < len(exec_streams[e]):
                        o = exec_streams[e][pos[e]]
                        stuck.append((o.idx, o.eng, o.after))
                raise RuntimeError(f"Prog deadlock; stuck (idx,eng,after): {stuck[:6]}")

        with contextlib.ExitStack() as ctx:
            csem = {e: ctx.enter_context(nc.semaphore(f"c{e}"))
                    for e in (ENG_TE, ENG_SC, ENG_VE, ENG_GP)}
            dsem = {q: [ctx.enter_context(nc.semaphore(f"d{q}_{j}")) for j in range(K)]
                    for q in _DMA_ENGS}
            blk = ctx.enter_context(nc.Block())

            def sem_of(p):
                if p.eng in _DMA_ENGS:
                    return dsem[p.eng][dma_no[p.idx] % K]
                return csem[p.eng]

            def emit_stream(eng_obj, stream, own_engs):
                import os
                dbg = os.environ.get("PROG2_TRACE")
                waited = {}
                for op in stream:
                    if dbg:
                        import sys as _sys
                        print(f"[{op.eng}] op{op.idx} inc={op.inc} planned={op.planned} after={op.after}", file=_sys.stderr)
                    if op.eng in _DMA_ENGS:
                        n = dma_no[op.idx]
                        if n >= K:
                            sname = (op.eng, n % K)
                            tgt = 16 * (n // K)
                            if waited.get(sname, 0) < tgt:
                                eng_obj.wait_ge(dsem[op.eng][n % K], tgt)
                                waited[sname] = tgt
                    for d in op.after:
                        p = ops[d]
                        if (not self.STRICT) and p.eng in own_engs and p.eng not in _DMA_ENGS:
                            continue
                        key = (p.eng, dma_no[p.idx] % K) if p.eng in _DMA_ENGS else (p.eng,)
                        tgt = p.planned
                        if waited.get(key, 0) < tgt:
                            if dbg:
                                import sys as _sys
                                print(f"    wait {key} >= {tgt}", file=_sys.stderr)
                            eng_obj.wait_ge(sem_of(p), tgt)
                            waited[key] = tgt
                    inst = op.fn(eng_obj)
                    if op.inc:
                        assert inst is not None, f"op {op.idx} returned None"
                        inst.then_inc(sem_of(op), op.inc)

            @blk.tensor
            def _(e):
                emit_stream(e, streams[ENG_TE], (ENG_TE,))

            @blk.scalar
            def _(e):
                emit_stream(e, streams[ENG_SC], (ENG_SC,))

            @blk.vector
            def _(e):
                emit_stream(e, streams[ENG_VE], (ENG_VE,))

            @blk.gpsimd
            def _(e):
                emit_stream(e, gp_merged, (ENG_GP,))

            @blk.sync
            def _(e):
                emit_stream(e, streams[ENG_SYQ], ())


def MM(P, out, lhsT, rhs, start, stop, after):
    return P.te(lambda e, o=out, l=lhsT, r=rhs, s=start, t=stop:
                e.matmul(o, l, r, start=s, stop=t), after)


def TP(P, out, in_, ident, after):
    return P.te(lambda e, o=out, i=in_, d=ident: e.transpose(o, i, d), after)



def build_ffn0pre(with_attnsum, debug=False):
    nc = bass.Bass("TRN2", target_bir_lowering=False, debug=False)
    x_d = nc.dram_tensor("x", [D, NT], F32, kind="ExternalInput").ap()
    sim_d = nc.dram_tensor("simw", [D, 16], F32, kind="ExternalInput").ap()
    gates_d = nc.dram_tensor("gates", [1, 16], F32, kind="ExternalInput").ap()
    wq_d = nc.dram_tensor("wq", [D, D], F32, kind="ExternalInput").ap()
    wk_d = nc.dram_tensor("wk", [D, D], F32, kind="ExternalInput").ap()
    wv_d = nc.dram_tensor("wv", [D, D], F32, kind="ExternalInput").ap()
    eyeb_d = nc.dram_tensor("eyeb_d", [128, 128], BF16, kind="ExternalInput").ap()
    eyef_d = nc.dram_tensor("eyef_d", [128, 128], F32, kind="ExternalInput").ap()
    if with_attnsum:
        parts_d = nc.dram_tensor("parts", [NCORES * D, NT], BF16, kind="ExternalInput").ap()
        xout_d = nc.dram_tensor("xout", [D, NT], F32, kind="ExternalOutput").ap()
    qf_d = nc.dram_tensor("qf", [128, NT], BF16, kind="ExternalOutput").ap()
    kf_d = nc.dram_tensor("kf", [128, NT], BF16, kind="ExternalOutput").ap()
    vt_d = nc.dram_tensor("vt", [NT, 128], BF16, kind="ExternalOutput").ap()
    wtm_d = nc.dram_tensor("wtm", [NT, 16], F32, kind="ExternalOutput").ap()
    wT_d = nc.dram_tensor("wT", [16, NT], F32, kind="ExternalOutput").ap()
    if debug:
        ltdbg_d = nc.dram_tensor("ltdbg", [16, NT], F32, kind="ExternalOutput").ap()
        lgdbg_d = nc.dram_tensor("lgdbg", [128, 8 * 16], F32, kind="ExternalOutput").ap()
        gscdbg_d = nc.dram_tensor("gscdbg", [128, 8 * 16], F32, kind="ExternalOutput").ap()
        invcdbg_d = nc.dram_tensor("invcdbg", [128, 8], F32, kind="ExternalOutput").ap()
        xnbdbg_d = nc.dram_tensor("xnbdbg", [128, NCH * NT], BF16, kind="ExternalOutput").ap()
        coldbg_d = nc.dram_tensor("coldbg", [128, 32], F32, kind="ExternalOutput").ap()
        gtdbg_d = nc.dram_tensor("gtdbg", [1, 32], F32, kind="ExternalOutput").ap()
        simbdbg_d = nc.dram_tensor("simbdbg", [128, NCH * 16], BF16, kind="ExternalOutput").ap()

    with contextlib.ExitStack() as ctx:
        E = ctx.enter_context
        xs_guard = nc.sbuf_tensor("s_xs", [128, NCH * NT], F32, side="right")
        xs = xs_guard.__enter__()
        xnb = E(nc.sbuf_tensor("s_xnb", [128, NCH * NT], BF16))
        xstat = E(nc.sbuf_tensor("s_xstat", [128, 2 * NT], BF16))
        sqs = E(nc.sbuf_tensor("s_sqs", [128, 2 * NT], BF16))
        simb = E(nc.sbuf_tensor("s_simb", [128, NCH * 16], BF16))
        simsq = E(nc.sbuf_tensor("s_simsq", [128, NCH * 16], F32))
        row = E(nc.sbuf_tensor("s_row", [1, 6 * NT], F32))
        gt = E(nc.sbuf_tensor("s_gt", [1, 32], F32))
        col = E(nc.sbuf_tensor("s_col", [128, 32], F32))
        invc = E(nc.sbuf_tensor("s_invc", [128, 8], F32))
        eyeb = E(nc.sbuf_tensor("s_eyeb", [128, 128], BF16))
        eyef = E(nc.sbuf_tensor("s_eyef", [128, 128], F32))
        onesb = E(nc.sbuf_tensor("s_onesb", [128, 1], BF16))
        onesf = E(nc.sbuf_tensor("s_onesf", [128, 1], F32))
        onesr = E(nc.sbuf_tensor("s_onesr", [1, 128], F32))
        mrep = E(nc.sbuf_tensor("s_mrep", [128, NT], F32))
        rrep = E(nc.sbuf_tensor("s_rrep", [128, NT], F32))
        ltsb = E(nc.sbuf_tensor("s_ltsb", [16, NT], F32))
        lg = E(nc.sbuf_tensor("s_lg", [128, 8 * 16], F32))
        gsc = E(nc.sbuf_tensor("s_gsc", [128, 8 * 16], F32))
        wtm = E(nc.sbuf_tensor("s_wtm", [128, 8 * 16], F32))
        wTt = E(nc.sbuf_tensor("s_wTt", [16, NT], F32))
        sgr = E(nc.sbuf_tensor("s_sgr", [128, 16], F32))
        qa = E(nc.sbuf_tensor("s_qa", [128, 8 * 128], F32))
        ka = E(nc.sbuf_tensor("s_ka", [128, 8 * 128], F32))
        va = E(nc.sbuf_tensor("s_va", [128, 8 * 128], F32))
        qab = E(nc.sbuf_tensor("s_qab", [128, 8 * 128], BF16))
        kab = E(nc.sbuf_tensor("s_kab", [128, 8 * 128], BF16))
        vtb = E(nc.sbuf_tensor("s_vtb", [128, 8 * 128], BF16))
        qfb = E(nc.sbuf_tensor("s_qfb", [128, NT], BF16))
        kfb = E(nc.sbuf_tensor("s_kfb", [128, NT], BF16))
        if with_attnsum:
            pts = E(nc.sbuf_tensor("s_pts", [128, 4 * NT], BF16))

        PA = E(nc.psum_tensor("s_PA", [128, 1024], F32))
        PB = E(nc.psum_tensor("s_PB", [128, 1024], F32))
        PQ = E(nc.psum_tensor("s_PQ", [128, 512], F32))
        PK = E(nc.psum_tensor("s_PK", [128, 512], F32))
        PV = E(nc.psum_tensor("s_PV", [128, 512], F32))
        PTB = E(nc.psum_tensor("s_PTB", [128, 128], BF16))

        P = Prog2(nc)
        ld_x = P.sy_dma(xs[:], x_d.rearrange("(g p) n -> p g n", g=NCH))
        ld_sim = P.gp_dma(simb[:].rearrange("p (g e) -> p g e", g=NCH),
                          sim_d.rearrange("(g p) e -> p g e", g=NCH))
        ld_eb = P.sy_dma(eyeb[:], eyeb_d)
        ld_ef = P.sy_dma(eyef[:], eyef_d)
        ld_gt = P.sy_dma(gt[0:1, 0:16], gates_d)
        ms_ones = P.ve(lambda e: e.memset(onesb[:], 1.0))
        ms_onesf = P.ve(lambda e: e.memset(onesf[:], 1.0))
        ms_onesr = P.ve(lambda e: e.memset(onesr[:], 1.0))

        x_ready = ld_x
        if with_attnsum:
            ldp, mmc = {}, {}
            sum_evs = []
            for g in range(NCH):
                PSX = (PA, PB)[g % 2]
                mml = {}
                for r in range(NCORES):
                    n = g * NCORES + r
                    slot = n % 4
                    q = P.sy_dma if n % 2 == 0 else P.gp_dma
                    ldp[(g, r)] = q(pts[:, slot * NT:(slot + 1) * NT],
                                    parts_d[r * D + g * 128:r * D + (g + 1) * 128, :],
                                    after=[mmc.get(n - 4)])
                    for half in range(2):
                        war = ([sum_evs[-4], sum_evs[-3]]
                               if len(sum_evs) >= 4 else []) if r == 0 else []
                        mml[half] = MM(P, PSX[:, half * 512:(half + 1) * 512],
                                       eyeb[:],
                                       pts[:, slot * NT + half * 512:
                                            slot * NT + (half + 1) * 512],
                                       r == 0, r == NCORES - 1,
                                       [ldp[(g, r)], ld_eb] + war)
                    mmc[g * NCORES + r] = mml[1]
                for half in range(2):
                    ev = P.ve(lambda e, g=g, half=half, PSX=PSX: e.scalar_tensor_tensor(
                        xs[:, g * NT + half * 512:g * NT + (half + 1) * 512],
                        PSX[:, half * 512:(half + 1) * 512], 1.0,
                        xs[:, g * NT + half * 512:g * NT + (half + 1) * 512],
                        ALU.mult, ALU.add), after=[mml[half], ld_x])
                    sum_evs.append(ev)
            x_ready = sum_evs[-1]
            st_x = P.sy_dma(xout_d.rearrange("(g p) n -> p g n", g=NCH),
                            xs[:].rearrange("p (g n) -> p g n", g=NCH),
                            after=[x_ready] + sum_evs)
        else:
            st_x = None

        s1a = s1b = s2a = s2b = None
        casts, sqss, s1ops, s2ops = [], [], [], []
        for g in range(NCH):
            slot = g % 2
            war_cast = [s1ops[2 * (g - 2) + 1], sqss[g - 2]] if g >= 2 else []
            c = P.ve(lambda e, g=g, s=slot: e.tensor_copy(
                xstat[:, s * NT:(s + 1) * NT], xs[:, g * NT:(g + 1) * NT]),
                after=[x_ready] + war_cast + ([casts[g - 2]] if g >= 2 else []) +
                      ([st_x] if (with_attnsum and g == 0) else []))
            casts.append(c)
            war_sq = [s2ops[2 * (g - 2) + 1]] if g >= 2 else []
            q = P.sc(lambda e, s=slot: e.activation(
                sqs[:, s * NT:(s + 1) * NT], xstat[:, s * NT:(s + 1) * NT], AF.Square),
                after=[c] + war_sq + ([sqss[g - 2]] if g >= 2 else []))
            sqss.append(q)
            for h in range(2):
                s1ops.append(MM(P, PA[0:1, h * 512:(h + 1) * 512], onesb[:],
                                xstat[:, slot * NT + h * 512:slot * NT + (h + 1) * 512],
                                g == 0, g == NCH - 1, [c, ms_ones]))
                s2ops.append(MM(P, PB[0:1, h * 512:(h + 1) * 512], onesb[:],
                                sqs[:, slot * NT + h * 512:slot * NT + (h + 1) * 512],
                                g == 0, g == NCH - 1, [q, ms_ones]))
        mean = row[0:1, 0 * NT:1 * NT]
        var = row[0:1, 1 * NT:2 * NT]
        rstd = row[0:1, 2 * NT:3 * NT]
        invrn = row[0:1, 3 * NT:4 * NT]
        t4 = row[0:1, 4 * NT:5 * NT]
        t5 = row[0:1, 5 * NT:6 * NT]
        v1 = P.ve(lambda e: e.tensor_scalar_mul(mean, PA[0:1, 0:NT], 1.0 / D),
                  after=[s1ops[-1], s1ops[-2]])
        v2 = P.ve(lambda e: e.tensor_scalar_mul(var, PB[0:1, 0:NT], 1.0 / D),
                  after=[s2ops[-1], s2ops[-2]])
        v3 = P.ve(lambda e: e.tensor_mul(t4, mean, mean), after=[v1])
        v4 = P.ve(lambda e: e.tensor_sub(var, var, t4), after=[v2, v3])
        v5 = P.ve(lambda e: e.tensor_scalar_add(t4, var, 1e-5), after=[v4])
        s_sq = P.sc(lambda e: e.activation(t5, t4, AF.Sqrt), after=[v5])
        v6 = P.ve(lambda e: e.reciprocal(rstd, t5), after=[s_sq])
        v7 = P.ve(lambda e: e.tensor_scalar_mul(t4, var, float(D)), after=[v4, s_sq])
        v8 = P.ve(lambda e: e.tensor_mul(t4, t4, rstd), after=[v6, v7])
        v9 = P.ve(lambda e: e.tensor_mul(t4, t4, rstd), after=[v8])
        s_rn = P.sc(lambda e: e.activation(t5, t4, AF.Sqrt), after=[v9])
        v10 = P.ve(lambda e: e.tensor_scalar_max(t4, t5, 1e-12), after=[s_rn])
        v11 = P.ve(lambda e: e.reciprocal(invrn, t4), after=[v10])

        reps = []
        for h in range(2):
            reps.append(MM(P, PA[:, h * 512:(h + 1) * 512], onesr[0:1, :],
                           mean[0:1, h * 512:(h + 1) * 512], True, True,
                           [v1, v3, ms_onesr]))
        ev_m = P.ve(lambda e: e.tensor_copy(mrep[:], PA[:, 0:NT]), after=reps)
        reps2 = []
        for h in range(2):
            reps2.append(MM(P, PA[:, h * 512:(h + 1) * 512], onesr[0:1, :],
                            rstd[0:1, h * 512:(h + 1) * 512], True, True,
                            [v6, ev_m]))
        ev_r = P.ve(lambda e: e.tensor_copy(rrep[:], PA[:, 0:NT]), after=reps2)

        xn_ops = []
        for g in range(NCH):
            a = P.ve(lambda e, g=g: e.tensor_sub(
                xs[:, g * NT:(g + 1) * NT], xs[:, g * NT:(g + 1) * NT], mrep[:]),
                after=[ev_m, casts[g]] + ([sqss[g]] if True else []))
            b = P.ve(lambda e, g=g: e.tensor_mul(
                xnb[:, g * NT:(g + 1) * NT], xs[:, g * NT:(g + 1) * NT], rrep[:]),
                after=[a, ev_r])
            xn_ops.append(b)

        sim_sqs = []
        for g in range(NCH):
            sim_sqs.append(P.sc(lambda e, g=g: e.activation(
                simsq[:, g * 16:(g + 1) * 16], simb[:, g * 16:(g + 1) * 16], AF.Square),
                after=[ld_sim]))
        cn_ops = []
        for g in range(NCH):
            cn_ops.append(MM(P, PB[0:1, 0:16], onesf[:], simsq[:, g * 16:(g + 1) * 16],
                             g == 0, g == NCH - 1,
                             [sim_sqs[g], ms_onesf, v2]))
        cnrow = gt[0:1, 16:32]
        ev_cn = P.ve(lambda e: e.tensor_copy(cnrow, PB[0:1, 0:16]), after=[cn_ops[-1]])
        s_cns = P.sc(lambda e: e.activation(cnrow, cnrow, AF.Sqrt), after=[ev_cn])
        v_cnm = P.ve(lambda e: e.tensor_scalar_max(cnrow, cnrow, 1e-12), after=[s_cns])
        s_sg = P.sc(lambda e: e.activation(gt[0:1, 0:16], gt[0:1, 0:16], AF.Sigmoid),
                    after=[ld_gt])
        rep_sg = MM(P, PA[:, 0:16], onesr[0:1, :], gt[0:1, 0:16], True, True,
                    [s_sg, ev_r])
        ev_sg = P.ve(lambda e: e.tensor_copy(sgr[:], PA[:, 0:16]), after=[rep_sg])
        tp_cn = TP(P, PA[0:16, 16:17], cnrow, eyef[0:1, 0:1], [v_cnm, ev_sg, ld_ef])
        ev_cnc = P.ve(lambda e: e.tensor_copy(col[0:16, 0:1], PA[0:16, 16:17]),
                      after=[tp_cn])
        v_cnr = P.ve(lambda e: e.reciprocal(col[0:16, 1:2], col[0:16, 0:1]),
                     after=[ev_cnc])
        tp_inv, ev_inv = [], []
        for j in range(8):
            prev_rd = ev_inv[j - 1] if j else ev_cnc
            t = TP(P, PA[:, 17 + j:18 + j], invrn[0:1, j * 128:(j + 1) * 128],
                   eyef[0:1, 0:1], [v11, ev_sg, prev_rd])
            tp_inv.append(t)
            ev_inv.append(P.ve(lambda e, j=j: e.tensor_copy(
                invc[:, j:j + 1], PA[:, 17 + j:18 + j]), after=[t]))

        lt_ops = []
        for g in range(NCH):
            for h in range(2):
                lt_ops.append(MM(P, PB[0:16, h * 512:(h + 1) * 512],
                                 simb[:, g * 16:(g + 1) * 16],
                                 xnb[:, g * NT + h * 512:g * NT + (h + 1) * 512],
                                 g == 0, g == NCH - 1,
                                 [ld_sim, xn_ops[g], ev_cn]))
        ev_lt = P.ve(lambda e: e.tensor_single_scalar(
            ltsb[:], PB[0:16, :], col[0:16, 1:2], ALU.mult),
            after=[lt_ops[-1], lt_ops[-2], v_cnr])

        w_ops = []
        prev_ev = None
        for j in range(8):
            tp = TP(P, PB[0:128, 0:16], ltsb[0:16, j * 128:(j + 1) * 128],
                    eyef[0:16, 0:16], [ev_lt, ld_ef, prev_ev])
            L = lg[:, j * 16:(j + 1) * 16]
            G = gsc[:, j * 16:(j + 1) * 16]
            W = wtm[:, j * 16:(j + 1) * 16]
            ev = P.ve(lambda e, L=L, j=j: e.scalar_tensor_tensor(
                L, PB[0:128, 0:16], invc[:, j:j + 1], sgr[:], ALU.mult, ALU.subtract),
                after=[tp, ev_inv[j], ev_sg])
            prev_ev = ev
            g1 = P.ve(lambda e, G=G, L=L: e.tensor_scalar_max(G, L, 0.0), after=[ev])
            g2 = P.ve(lambda e, W=W, L=L: e.tensor_single_scalar(W, L, 0.0, ALU.is_gt),
                      after=[ev])
            g3 = P.ve(lambda e, W=W: e.reduce_sum(col[:, 0:1], W, axis=AX.X), after=[g2])
            g4 = P.ve(lambda e, L=L: e.reduce_max(col[:, 1:2], L, axis=AX.X), after=[ev])
            g5 = P.ve(lambda e, L=L: e.tensor_single_scalar(
                col[:, 8:24], L, col[:, 1:2], ALU.is_ge), after=[g4])
            g6 = P.ve(lambda e, L=L: e.scalar_tensor_tensor(
                col[:, 8:24], col[:, 8:24], -1e9, L, ALU.mult, ALU.add), after=[g5])
            g7 = P.ve(lambda e: e.reduce_max(col[:, 2:3], col[:, 8:24], axis=AX.X),
                      after=[g6])
            g8 = P.ve(lambda e, L=L: e.tensor_single_scalar(
                col[:, 8:24], L, col[:, 2:3], ALU.is_ge), after=[g7])
            g9 = P.ve(lambda e, W=W: e.tensor_sub(col[:, 8:24], col[:, 8:24], W),
                      after=[g8])
            g10 = P.ve(lambda e: e.tensor_single_scalar(
                col[:, 3:4], col[:, 0:1], 0.5, ALU.is_lt), after=[g3])
            g11 = P.ve(lambda e, W=W: e.scalar_tensor_tensor(
                W, col[:, 8:24], col[:, 3:4], W, ALU.mult, ALU.add), after=[g9, g10])
            s_e = P.sc(lambda e, L=L, G=G: e.activation(L, G, AF.Exp), after=[g1, g8])
            g12 = P.ve(lambda e, L=L, W=W: e.tensor_mul(L, L, W), after=[s_e, g11])
            g13 = P.ve(lambda e, L=L: e.reduce_sum(col[:, 4:5], L, axis=AX.X),
                       after=[g12])
            g14 = P.ve(lambda e: e.reciprocal(col[:, 5:6], col[:, 4:5]), after=[g13])
            g15 = P.ve(lambda e, W=W, L=L: e.tensor_single_scalar(
                W, L, col[:, 5:6], ALU.mult), after=[g14])
            w_ops.append(g15)

        if debug:
            dbg_lt = P.sy_dma(ltdbg_d, ltsb[:], after=[ev_lt])
            dbg_lg = P.sy_dma(lgdbg_d, lg[:], after=w_ops)
            dbg_gsc = P.sy_dma(gscdbg_d, gsc[:], after=w_ops)
            dbg_invc = P.sy_dma(invcdbg_d, invc[:], after=ev_inv)
            dbg_xnb = P.sy_dma(xnbdbg_d, xnb[:], after=[xn_ops[-1]] + xn_ops)
            dbg_simb = P.sy_dma(simbdbg_d, simb[:], after=[ld_sim])
            dbg_col = P.sy_dma(coldbg_d, col[:], after=[v_cnr])
            dbg_gt = P.sy_dma(gtdbg_d, gt[:], after=[v_cnm, s_sg])
        wT_ops = []
        for j in range(8):
            tp = TP(P, PB[0:16, 16:144], wtm[:, j * 16:(j + 1) * 16], eyef[:],
                    [w_ops[-1]] + ([wT_ops[-1]] if wT_ops else []))
            wT_ops.append(P.ve(lambda e, j=j: e.tensor_copy(
                wTt[0:16, j * 128:(j + 1) * 128], PB[0:16, 16:144]), after=[tp]))

        xs_guard.__exit__(None, None, None)
        wsp = E(nc.sbuf_tensor("s_wsp", [128, 3 * NCH * 512], BF16, side="right"))
        acc_done = []
        last_evac = {}
        last_ps_read = {}
        for span in range(4):
            wls = []
            for wi, wd in enumerate((wq_d, wk_d, wv_d)):
                prev = acc_done[-1] if acc_done else [xn_ops[-1], st_x]
                wls.append(P.gp_dma(
                    wsp[:, wi * NCH * 512:(wi + 1) * NCH * 512].rearrange(
                        "p (g n) -> p g n", g=NCH),
                    wd[:, span * 512:(span + 1) * 512].rearrange(
                        "(g p) n -> p g n", g=NCH),
                    after=[prev]))
            span_evacs = []
            for t in range(8):
                mms = {0: [], 1: [], 2: []}
                for g in range(NCH):
                    lhs = xnb[:, g * NT + t * 128:g * NT + (t + 1) * 128]
                    for wi, PS in enumerate((PQ, PK, PV)):
                        war = last_ps_read.get(wi, None)
                        mms[wi].append(MM(
                            P, PS[:], lhs,
                            wsp[:, wi * NCH * 512 + g * 512:wi * NCH * 512 + (g + 1) * 512],
                            g == 0, g == NCH - 1,
                            [xn_ops[g], wls[wi]] + ([war] if g == 0 else [])))
                for wi, acc in enumerate((qa, ka, va)):
                    PS = (PQ, PK, PV)[wi]
                    evs = []
                    for eb in range(4):
                        eidx = span * 4 + eb
                        wcol = wtm[:, t * 16 + eidx:t * 16 + eidx + 1]
                        src = PS[:, eb * 128:(eb + 1) * 128]
                        dst = acc[:, t * 128:(t + 1) * 128]
                        if span == 0 and eb == 0:
                            evs.append(P.ve(lambda e, d=dst, s=src, w=wcol:
                                            e.tensor_single_scalar(d, s, w, ALU.mult),
                                            after=[mms[wi][-1], w_ops[t]]))
                        else:
                            chain = evs[-1] if evs else last_evac[(wi, t)]
                            evs.append(P.ve(lambda e, d=dst, s=src, w=wcol:
                                            e.scalar_tensor_tensor(d, s, w, d, ALU.mult, ALU.add),
                                            after=[mms[wi][-1], w_ops[t], chain]))
                    last_evac[(wi, t)] = evs[-1]
                    last_ps_read[wi] = evs[-1]
                    span_evacs.extend(evs)
            acc_done.append(span_evacs[-1])
        all_evacs = [last_evac[(wi, t)] for wi in range(3) for t in range(8)]

        c_q = P.ve(lambda e: e.tensor_copy(qab[:], qa[:]), after=all_evacs)
        c_k = P.ve(lambda e: e.tensor_copy(kab[:], ka[:]), after=all_evacs)
        c_v = P.ve(lambda e: e.tensor_copy(vtb[:], va[:]), after=all_evacs)
        tp_ops = []
        prev_cp = None
        for t in range(16):
            src = qab if t < 8 else kab
            tt = t % 8
            tp = TP(P, PTB[:], src[:, tt * 128:(tt + 1) * 128], eyeb[:],
                    [c_q if t < 8 else c_k, ld_eb] + ([prev_cp] if prev_cp else []))
            dstt = (t < 8)
            cp = P.ve(lambda e, t=t, tt=tt, d=dstt: e.tensor_copy(
                (qfb if d else kfb)[:, tt * 128:(tt + 1) * 128], PTB[:]), after=[tp])
            prev_cp = cp
            tp_ops.append(cp)

        stores = []
        stores.append(P.sy_dma(qf_d, qfb[:], after=[tp_ops[7]] + tp_ops[:8]))
        stores.append(P.sy_dma(kf_d, kfb[:], after=tp_ops[8:]))
        stores.append(P.sy_dma(vt_d.rearrange("(t p) d -> p t d", p=128),
                               vtb[:].rearrange("p (t d) -> p t d", t=8), after=[c_v]))
        stores.append(P.sy_dma(wtm_d.rearrange("(t p) e -> p t e", p=128),
                               wtm[:].rearrange("p (t e) -> p t e", t=8), after=w_ops))
        stores.append(P.sy_dma(wT_d, wTt[:], after=wT_ops))
        if with_attnsum:
            stores.append(st_x)
        if debug:
            stores += [dbg_lt, dbg_lg, dbg_gsc, dbg_invc, dbg_xnb, dbg_simb, dbg_col, dbg_gt]
        P.gp(lambda e: e.nop(), after=stores)
        P.finalize()

    return nc



def emit_ln(P, nc, E, tag, xs, xnb, PA, PB, x_ready, extra_first=(), xs_readers=()):
    xstat = E(nc.sbuf_tensor(f"s_xst{tag}", [128, 2 * NT], BF16))
    sqs = E(nc.sbuf_tensor(f"s_sq{tag}", [128, 2 * NT], BF16))
    row = E(nc.sbuf_tensor(f"s_row{tag}", [1, 6 * NT], F32))
    onesb = E(nc.sbuf_tensor(f"s_ob{tag}", [128, 1], BF16))
    onesr = E(nc.sbuf_tensor(f"s_or{tag}", [1, 128], F32))
    mrep = E(nc.sbuf_tensor(f"s_mr{tag}", [128, NT], F32))
    rrep = E(nc.sbuf_tensor(f"s_rr{tag}", [128, NT], F32))
    m1 = P.ve(lambda e: e.memset(onesb[:], 1.0))
    m2 = P.ve(lambda e: e.memset(onesr[:], 1.0))
    casts, sqss, s1ops, s2ops = [], [], [], []
    for g in range(NCH):
        slot = g % 2
        war_cast = [s1ops[2 * (g - 2) + 1], sqss[g - 2], casts[g - 2]] if g >= 2 else []
        c = P.ve(lambda e, g=g, s=slot: e.tensor_copy(
            xstat[:, s * NT:(s + 1) * NT], xs[:, g * NT:(g + 1) * NT]),
            after=[x_ready] + war_cast + (list(extra_first) if g == 0 else []))
        casts.append(c)
        war_sq = [s2ops[2 * (g - 2) + 1], sqss[g - 2]] if g >= 2 else []
        q = P.sc(lambda e, s=slot: e.activation(
            sqs[:, s * NT:(s + 1) * NT], xstat[:, s * NT:(s + 1) * NT], AF.Square),
            after=[c] + war_sq)
        sqss.append(q)
        for h in range(2):
            s1ops.append(MM(P, PA[0:1, h * 512:(h + 1) * 512], onesb[:],
                            xstat[:, slot * NT + h * 512:slot * NT + (h + 1) * 512],
                            g == 0, g == NCH - 1, [c, m1] + (list(extra_first) if g == 0 else [])))
            s2ops.append(MM(P, PB[0:1, h * 512:(h + 1) * 512], onesb[:],
                            sqs[:, slot * NT + h * 512:slot * NT + (h + 1) * 512],
                            g == 0, g == NCH - 1, [q, m1] + (list(extra_first) if g == 0 else [])))
    mean = row[0:1, 0 * NT:1 * NT]
    var = row[0:1, 1 * NT:2 * NT]
    rstd = row[0:1, 2 * NT:3 * NT]
    t4 = row[0:1, 4 * NT:5 * NT]
    t5 = row[0:1, 5 * NT:6 * NT]
    v1 = P.ve(lambda e: e.tensor_scalar_mul(mean, PA[0:1, 0:NT], 1.0 / D),
              after=[s1ops[-1], s1ops[-2], m2])
    v2 = P.ve(lambda e: e.tensor_scalar_mul(var, PB[0:1, 0:NT], 1.0 / D),
              after=[s2ops[-1], s2ops[-2]])
    v3 = P.ve(lambda e: e.tensor_mul(t4, mean, mean), after=[v1])
    v4 = P.ve(lambda e: e.tensor_sub(var, var, t4), after=[v2, v3])
    v5 = P.ve(lambda e: e.tensor_scalar_add(t4, var, 1e-5), after=[v4])
    s_sq = P.sc(lambda e: e.activation(t5, t4, AF.Sqrt), after=[v5, sqss[-1]])
    v6 = P.ve(lambda e: e.reciprocal(rstd, t5), after=[s_sq])
    reps = []
    for h in range(2):
        reps.append(MM(P, PA[:, h * 512:(h + 1) * 512], onesr[0:1, :],
                       mean[0:1, h * 512:(h + 1) * 512], True, True, [v1, v3, m2]))
    ev_m = P.ve(lambda e: e.tensor_copy(mrep[:], PA[:, 0:NT]), after=reps + [v6])
    reps2 = []
    for h in range(2):
        reps2.append(MM(P, PA[:, h * 512:(h + 1) * 512], onesr[0:1, :],
                        rstd[0:1, h * 512:(h + 1) * 512], True, True, [v6, ev_m]))
    ev_r = P.ve(lambda e: e.tensor_copy(rrep[:], PA[:, 0:NT]), after=reps2)
    xn_ops = []
    prev_b = None
    for g in range(NCH):
        a = P.ve(lambda e, g=g: e.tensor_sub(
            xs[:, g * NT:(g + 1) * NT], xs[:, g * NT:(g + 1) * NT], mrep[:]),
            after=[ev_m, casts[g], sqss[g] if g >= NCH - 2 else None,
                   s1ops[2 * g + 1], s2ops[2 * g + 1]] + list(xs_readers))
        b = P.ve(lambda e, g=g: e.tensor_mul(
            xnb[:, g * NT:(g + 1) * NT], xs[:, g * NT:(g + 1) * NT], rrep[:]),
            after=[a, ev_r])
        xn_ops.append(b)
    stats = dict(mean=mean, var=var, rstd=rstd, row=row, v4=v4, v6=v6,
                 ev_m=ev_m, ev_r=ev_r, reps2=reps2, onesb=onesb, onesr=onesr,
                 m1=m1, m2=m2)
    return xn_ops, stats



def build_ffn0post():
    nc = bass.Bass("TRN2", target_bir_lowering=False, debug=False)
    x_d = nc.dram_tensor("x", [D, NT], F32, kind="ExternalInput").ap()
    qf_d = nc.dram_tensor("qf", [128, NT], BF16, kind="ExternalInput").ap()
    kfull_d = nc.dram_tensor("kfull", [128, SEQ], BF16, kind="ExternalInput").ap()
    vfull_d = nc.dram_tensor("vfull", [SEQ, 128], BF16, kind="ExternalInput").ap()
    wT_d = nc.dram_tensor("wTi", [16, NT], F32, kind="ExternalInput").ap()
    oc_d = nc.dram_tensor("oc", [NCH, NCH, 128, 128], F32, kind="ExternalInput").ap()
    sels_d = nc.dram_tensor("sels", [16, D], F32, kind="ExternalInput").ap()
    cmask_d = nc.dram_tensor("cmask", [SEQ, NT], BF16, kind="ExternalInput").ap()
    xo_d = nc.dram_tensor("xo", [D, NT], F32, kind="ExternalOutput").ap()
    xn_d = nc.dram_tensor("xn", [D, NT], BF16, kind="ExternalOutput").ap()
    import os as _os
    _dbg2 = bool(_os.environ.get("FFN0POST_DEBUG"))
    if _dbg2:
        wadbg_d = nc.dram_tensor("wadbg", [128, NCH * NT], BF16, kind="ExternalOutput").ap()
        zdbg_d = nc.dram_tensor("zdbg", [1, NT], F32, kind="ExternalOutput").ap()
        rcdbg_d = nc.dram_tensor("rcdbg", [16, NT], F32, kind="ExternalOutput").ap()

    with contextlib.ExitStack() as ctx:
        E = ctx.enter_context
        xs = E(nc.sbuf_tensor("s_xs", [128, NCH * NT], F32))
        xnb = E(nc.sbuf_tensor("s_xnb", [128, NCH * NT], BF16))
        wTs = E(nc.sbuf_tensor("s_wTs", [16, NT], F32))
        selsb = E(nc.sbuf_tensor("s_sels", [16, D], F32))
        rowa = E(nc.sbuf_tensor("s_rowa", [1, 3 * NT], F32))
        bcol = E(nc.sbuf_tensor("s_bcol", [128, 1], F32))
        onesb2 = E(nc.sbuf_tensor("s_onesb2", [128, 1], BF16))
        onesr2 = E(nc.sbuf_tensor("s_onesr2", [1, 128], F32))
        rcombo = E(nc.sbuf_tensor("s_rcombo", [16, NT], F32))
        repsb = E(nc.sbuf_tensor("s_repsb", [128, NT], F32))
        g_q = nc.sbuf_tensor("s_qfb", [128, NT], BF16, side="right"); qfb = g_q.__enter__()
        g_k = nc.sbuf_tensor("s_kfb", [128, SEQ], BF16, side="right"); kfb = g_k.__enter__()
        g_v = nc.sbuf_tensor("s_vfb", [128, 64 * 128], BF16, side="right"); vfb = g_v.__enter__()
        g_sq = nc.sbuf_tensor("s_sq", [128, NT], BF16, side="right"); sq = g_sq.__enter__()
        g_E = nc.sbuf_tensor("s_Eslot", [128, 2 * NT], BF16, side="right"); Eslot = g_E.__enter__()
        g_m = nc.sbuf_tensor("s_mslot", [128, 2 * NT], BF16, side="right"); mslot = g_m.__enter__()
        PS0 = E(nc.psum_tensor("s_PS0", [128, 1024], F32))
        PS1 = E(nc.psum_tensor("s_PS1", [128, 1024], F32))
        PAO = E(nc.psum_tensor("s_PAO", [128, 1024], F32))
        PZ = E(nc.psum_tensor("s_PZ", [1, 1024], F32))

        P = Prog2(nc)
        ld_x = P.sy_dma(xs[:], x_d.rearrange("(g p) n -> p g n", g=NCH))
        ld_q = P.sy_dma(qfb[:], qf_d)
        ld_k = P.sy_dma(kfb[:], kfull_d)
        ld_v = P.sy_dma(vfb[:].rearrange("p (t d) -> p t d", t=64),
                        vfull_d.rearrange("(t p) d -> p t d", p=128))
        ld_w = P.sy_dma(wTs[:], wT_d)
        ld_sel = P.sy_dma(selsb[:], sels_d)
        m1 = P.ve(lambda e: e.memset(onesb2[:], 1.0))
        m2 = P.ve(lambda e: e.memset(onesr2[:], 1.0))

        sq_q = P.sc(lambda e: e.activation(sq[:], qfb[:], AF.Square), after=[ld_q])
        mmq = [MM(P, PZ[0:1, h * 512:(h + 1) * 512], onesb2[:],
                  sq[:, h * 512:(h + 1) * 512], True, True, [sq_q, m1])
               for h in range(2)]
        ev_q2 = P.ve(lambda e: e.tensor_copy(rowa[0:1, 0:NT], PZ[0:1, :]), after=mmq)
        vq = P.ve(lambda e: e.reduce_max(rowa[0:1, 2 * NT:2 * NT + 1],
                                         rowa[0:1, 0:NT], axis=AX.X), after=[ev_q2])
        k2ops = []
        prev_sqk = sq_q
        prev_ev = ev_q2
        for c in range(8):
            sk = P.sc(lambda e, c=c: e.activation(
                sq[:], kfb[:, c * NT:(c + 1) * NT], AF.Square),
                after=[ld_k, prev_ev, prev_sqk])
            mk = [MM(P, PZ[0:1, h * 512:(h + 1) * 512], onesb2[:],
                     sq[:, h * 512:(h + 1) * 512], True, True, [sk, prev_ev])
                  for h in range(2)]
            ev = P.ve(lambda e: e.tensor_copy(rowa[0:1, NT:2 * NT], PZ[0:1, :]), after=mk)
            ev2 = P.ve(lambda e, c=c: e.reduce_max(
                rowa[0:1, 2 * NT + 8 + c:2 * NT + 9 + c], rowa[0:1, NT:2 * NT],
                axis=AX.X), after=[ev])
            prev_sqk = sk
            prev_ev = ev2
            k2ops.append(ev2)
        vk = P.ve(lambda e: e.reduce_max(rowa[0:1, 2 * NT + 1:2 * NT + 2],
                                         rowa[0:1, 2 * NT + 8:2 * NT + 16], axis=AX.X), after=k2ops)
        vm = P.ve(lambda e: e.tensor_mul(rowa[0:1, 2 * NT:2 * NT + 1],
                                         rowa[0:1, 2 * NT:2 * NT + 1],
                                         rowa[0:1, 2 * NT + 1:2 * NT + 2]), after=[vq, vk])
        s_m = P.sc(lambda e: e.activation(rowa[0:1, 2 * NT:2 * NT + 1],
                                          rowa[0:1, 2 * NT:2 * NT + 1], AF.Sqrt),
                   after=[vm, prev_sqk])
        v_b = P.ve(lambda e: e.tensor_scalar_mul(rowa[0:1, 2 * NT:2 * NT + 1],
                                                 rowa[0:1, 2 * NT:2 * NT + 1],
                                                 -1.001 * RS), after=[s_m])
        rep_b = MM(P, PS0[:, 0:1], onesr2[0:1, :], rowa[0:1, 2 * NT:2 * NT + 1],
                   True, True, [v_b, m2])
        ev_b = P.ve(lambda e: e.tensor_copy(bcol[:], PS0[:, 0:1]), after=[rep_b])

        PS = (PS0, PS1)
        mload, sTop, expop, mulop, avop, zop = {}, {}, {}, {}, {}, {}
        for kt in range(64):
            sl = kt % 2
            mload[kt] = P.sy_dma(
                mslot[:, sl * NT:(sl + 1) * NT],
                cmask_d[kt * 128:(kt + 1) * 128, :],
                after=[mulop.get(kt - 2)])
            for h in range(2):
                sTop[(kt, h)] = MM(
                    P, PS[sl][:, h * 512:(h + 1) * 512],
                    kfb[:, kt * 128:(kt + 1) * 128],
                    qfb[:, h * 512:(h + 1) * 512],
                    True, True,
                    [ld_k, ld_q, ev_b if kt < 2 else None,
                     expop.get((kt - 2, h)), zop.get((kt - 2, 1))])
                expop[(kt, h)] = P.sc(lambda e, sl=sl, h=h: e.activation(
                    Eslot[:, sl * NT + h * 512:sl * NT + (h + 1) * 512],
                    PS[sl][:, h * 512:(h + 1) * 512],
                    AF.Exp, scale=RS, bias=bcol[:]),
                    after=[sTop[(kt, h)], ev_b, avop.get((kt - 2, h)),
                           zop.get((kt - 2, h)), s_m])
            mulop[kt] = P.ve(lambda e, sl=sl: e.tensor_mul(
                Eslot[:, sl * NT:(sl + 1) * NT],
                Eslot[:, sl * NT:(sl + 1) * NT],
                mslot[:, sl * NT:(sl + 1) * NT]),
                after=[expop[(kt, 0)], expop[(kt, 1)], mload[kt], ev_b])
            for h in range(2):
                avop[(kt, h)] = MM(
                    P, PAO[:, h * 512:(h + 1) * 512],
                    vfb[:, kt * 128:(kt + 1) * 128],
                    Eslot[:, sl * NT + h * 512:sl * NT + (h + 1) * 512],
                    kt == 0, kt == 63,
                    [ld_v, mulop[kt]])
                zop[(kt, h)] = MM(
                    P, PZ[0:1, h * 512:(h + 1) * 512], onesb2[:],
                    Eslot[:, sl * NT + h * 512:sl * NT + (h + 1) * 512],
                    kt == 0, kt == 63,
                    [mulop[kt], ev_q2 if kt == 0 else None,
                     prev_ev if kt == 0 else None])
        ph1_done = [avop[(63, 0)], avop[(63, 1)], zop[(63, 0)], zop[(63, 1)],
                    mulop[63], expop[(63, 0)], expop[(63, 1)],
                    sTop[(63, 0)], sTop[(63, 1)], mload[63]]
        for g in (g_m, g_E, g_sq, g_v, g_k, g_q):
            g.__exit__(None, None, None)
        g_wa = nc.sbuf_tensor("s_wa", [128, NCH * NT], BF16, side="right"); wa = g_wa.__enter__()
        g_oc = nc.sbuf_tensor("s_ocs", [128, 2 * NCH * 128], BF16, side="right"); ocs = g_oc.__enter__()
        ev_z = P.ve(lambda e: e.tensor_copy(rowa[0:1, 0:NT], PZ[0:1, :]),
                    after=[zop[(63, 0)], zop[(63, 1)], vk])
        v_rz = P.ve(lambda e: e.reciprocal(rowa[0:1, NT:2 * NT], rowa[0:1, 0:NT]),
                    after=[ev_z])
        rep_rz = [MM(P, PS0[0:16, h * 512:(h + 1) * 512], onesr2[0:1, 0:16],
                     rowa[0:1, NT + h * 512:NT + (h + 1) * 512], True, True,
                     [v_rz, expop[(63, 0)], expop[(63, 1)], m2])
                  for h in range(2)]
        ev_rc = P.ve(lambda e: e.tensor_mul(rcombo[:], wTs[:], PS0[0:16, :]),
                     after=rep_rz + [ld_w])
        wa_ops = []
        prev_cp = ev_rc
        for e_i in range(16):
            rep = [MM(P, PS1[:, h * 512:(h + 1) * 512],
                      selsb[:, e_i * 128:(e_i + 1) * 128],
                      rcombo[0:16, h * 512:(h + 1) * 512], True, True,
                      [ev_rc, ld_sel, prev_cp])
                   for h in range(2)]
            cp = P.sc(lambda e: e.copy(repsb[:], PS1[:, :]), after=rep)
            w_ = P.ve(lambda e, e_i=e_i: e.tensor_mul(
                wa[:, e_i * NT:(e_i + 1) * NT], PAO[:, :], repsb[:]),
                after=[cp] + ph1_done)
            prev_cp = w_
            wa_ops.append(w_)

        if _dbg2:
            d1 = P.sy_dma(wadbg_d, wa[:], after=wa_ops)
            d2 = P.sy_dma(zdbg_d, rowa[0:1, 0:NT], after=[ev_z, v_rz])
            d3 = P.sy_dma(rcdbg_d, rcombo[:], after=[ev_rc])
        oc_loads = {}
        mmo = {}
        evo = []
        for co in range(NCH):
            sl = co % 2
            oc_loads[co] = P.gp_dma(
                ocs[:, sl * NCH * 128:(sl + 1) * NCH * 128].rearrange(
                    "p (g c) -> p g c", g=NCH),
                oc_d[:, co, :, :].rearrange("g p c -> p g c"),
                after=[mmo.get((co - 2, NCH - 1, 1))] + (ph1_done if co < 2 else []))
            for g in range(NCH):
                for h in range(2):
                    mmo[(co, g, h)] = MM(
                        P, PAO[:, h * 512:(h + 1) * 512],
                        ocs[:, sl * NCH * 128 + g * 128:sl * NCH * 128 + (g + 1) * 128],
                        wa[:, g * NT + h * 512:g * NT + (h + 1) * 512],
                        g == 0, g == NCH - 1,
                        [oc_loads[co], wa_ops[g],
                         wa_ops[-1] if (co == 0 and g == 0) else None,
                         evo[-1] if (g == 0 and evo) else None])
            ev = P.ve(lambda e, co=co: e.scalar_tensor_tensor(
                xs[:, co * NT:(co + 1) * NT], PAO[:, :], 0.5,
                xs[:, co * NT:(co + 1) * NT], ALU.mult, ALU.add),
                after=[mmo[(co, NCH - 1, 0)], mmo[(co, NCH - 1, 1)], ld_x])
            evo.append(ev)

        st1 = P.sy_dma(xo_d.rearrange("(g p) n -> p g n", g=NCH),
                       xs[:].rearrange("p (g n) -> p g n", g=NCH),
                       after=evo)
        g_oc.__exit__(None, None, None)
        g_wa.__exit__(None, None, None)
        xn_ops, _stats = emit_ln(P, nc, E, "L", xs, xnb, PS0, PS1,
                                 evo[-1], extra_first=[ev_rc, wa_ops[-1]] + evo,
                                 xs_readers=[st1])
        st2 = P.sy_dma(xn_d.rearrange("(g p) n -> p g n", g=NCH),
                       xnb[:].rearrange("p (g n) -> p g n", g=NCH),
                       after=xn_ops)
        P.gp(lambda e: e.nop(), after=[st1, st2] + ([d1, d2, d3] if _dbg2 else []))
        P.finalize()
    return nc



def build_attn0():
    nc = bass.Bass("TRN2", target_bir_lowering=False, debug=False)
    xnf_d = nc.dram_tensor("xnf", [D, SEQ], BF16, kind="ExternalInput").ap()
    wq6_d = nc.dram_tensor("wq6", [D, 768], F32, kind="ExternalInput").ap()
    wo2_d = nc.dram_tensor("wo2", [256, D], F32, kind="ExternalInput").ap()
    tri_d = nc.dram_tensor("tri", [128, 4 * 512], BF16, kind="ExternalInput").ap()
    part_d = nc.dram_tensor("part", [D, SEQ], BF16, kind="ExternalOutput").ap()
    _H = 5632
    qdram = dict(
        w1t=nc.dram_tensor("w1t", [D, _H], F32, kind="ExternalInput").ap(),
        w2t=nc.dram_tensor("w2t", [D, _H], F32, kind="ExternalInput").ap(),
        w3t=nc.dram_tensor("w3t", [_H, D], F32, kind="ExternalInput").ap(),
        sl1=nc.dram_tensor("sl1", [D, _H // 8], F32, kind="ExternalInput").ap(),
        sl2=nc.dram_tensor("sl2", [D, _H // 8], F32, kind="ExternalInput").ap(),
        sl3=nc.dram_tensor("sl3", [_H, D // 8], F32, kind="ExternalInput").ap(),
        q1o=nc.dram_tensor("q1o", [D, _H // 8], BF16, kind="ExternalOutput").ap(),
        q2o=nc.dram_tensor("q2o", [D, _H // 8], BF16, kind="ExternalOutput").ap(),
        q3o=nc.dram_tensor("q3o", [_H, D // 8], BF16, kind="ExternalOutput").ap(),
    )

    NSP = SEQ // 512
    xnf_v = xnf_d.rearrange("(g p) n -> p g n", g=NCH)
    with contextlib.ExitStack() as ctx:
        E = ctx.enter_context
        qk = [E(nc.sbuf_tensor(f"s_qk{j}", [128, 2 * SEQ], BF16)) for j in range(2)]
        vsb = [E(nc.sbuf_tensor(f"s_v{j}", [128, 64 * 128], BF16)) for j in range(2)]
        aosb = [E(nc.sbuf_tensor(f"s_ao{j}", [128, SEQ], BF16)) for j in range(2)]
        wq6b = E(nc.sbuf_tensor("s_wq6", [128, NCH * 768], BF16))
        wo2b = E(nc.sbuf_tensor("s_wo2", [128, 2 * D], BF16))
        trib = E(nc.sbuf_tensor("s_tri", [128, 4 * 512], BF16))
        xsl = E(nc.sbuf_tensor("s_xsl", [128, 4 * 512], BF16))
        rowa = E(nc.sbuf_tensor("s_rowa", [1, NT + 64], F32))
        bcol = [E(nc.sbuf_tensor(f"s_bc{j}", [128, 1], F32)) for j in range(2)]
        onesb = E(nc.sbuf_tensor("s_onesb", [128, 1], BF16))
        onesr = E(nc.sbuf_tensor("s_onesr", [1, 128], F32))
        rzrep = E(nc.sbuf_tensor("s_rzrep", [128, 512], F32))
        Esl = E(nc.sbuf_tensor("s_Esl", [128, 6 * 512], BF16))
        sq = Esl[:, 0:1024]
        ppsb = E(nc.sbuf_tensor("s_ppsb", [128, 2 * 512], BF16))
        PQ0 = E(nc.psum_tensor("s_PQ0", [128, 512], F32))
        PK0 = E(nc.psum_tensor("s_PK0", [128, 512], F32))
        PQ1 = E(nc.psum_tensor("s_PQ1", [128, 512], F32))
        PK1 = E(nc.psum_tensor("s_PK1", [128, 512], F32))
        PV0 = E(nc.psum_tensor("s_PV0", [128, 512], F32))
        PV1 = E(nc.psum_tensor("s_PV1", [128, 512], F32))
        PZ = E(nc.psum_tensor("s_PZ", [1, 512], F32))
        PZ2 = E(nc.psum_tensor("s_PZ2", [1, 512], F32))
        PRP = PV0

        P = Prog2(nc)
        ld_w = P.gp_dma(wq6b[:].rearrange("p (g r) -> p g r", g=NCH),
                        wq6_d.rearrange("(g p) r -> p g r", g=NCH))
        ld_o = P.gp_dma(wo2b[:].rearrange("p (j c) -> p j c", j=2),
                        wo2_d.rearrange("(j p) c -> p j c", j=2))
        ld_t = P.sy_dma(trib[:], tri_d)
        quant_sts = emit_quant(P, nc, E, qdram)
        m1 = P.ve(lambda e: e.memset(onesb[:], 1.0))
        m2 = P.ve(lambda e: e.memset(onesr[:], 1.0))

        PQK = (PQ0, PK0, PQ1, PK1)
        ev_all = []
        mm_last = {}
        ld_hist = []
        ev_hist = []
        ev_by_ps = {}
        for sp in range(NSP):
            mms = {r: [] for r in range(6)}
            ldc2 = {}
            for c in range(NCH):
                cp = c // 2
                if c % 2 == 0:
                    sl2 = cp % 2
                    war_mm = [mm_last.get((sp, c - 3)), mm_last.get((sp, c - 4))] \
                        if c >= 4 else ([mm_last.get((sp - 1, NCH - 1)),
                                         mm_last.get((sp - 1, NCH - 2))] if sp else [])
                    q = P.sy_dma if cp % 2 == 0 else P.gp_dma
                    ld2 = q(xsl[:, sl2 * 1024:(sl2 + 1) * 1024].rearrange(
                                "p (c n) -> p c n", c=2),
                            xnf_v[:, c:c + 2, sp * 512:(sp + 1) * 512],
                            after=war_mm)
                    ldc2[cp] = ld2
                ldc = ldc2[cp]
                ld_hist.append(ldc)
                tile = xsl[:, (cp % 2) * 1024 + (c % 2) * 512:
                           (cp % 2) * 1024 + (c % 2) * 512 + 512]
                for r, coloff in enumerate((0, 128, 384, 512)):
                    war_ps = [ev_by_ps.get(r)] if c == 0 else []
                    mms[r].append(MM(
                        P, PQK[r][:],
                        wq6b[:, c * 768 + coloff:c * 768 + coloff + 128],
                        tile, c == 0, c == NCH - 1,
                        [ldc, ld_w] + war_ps))
                for j in range(2):
                    for sub in range(4):
                        war_ps = [ev_by_ps.get(4 + j)] if c == 0 else []
                        mms[4 + j].append(MM(
                            P, (PV0, PV1)[j][:, sub * 128:(sub + 1) * 128],
                            tile[:, sub * 128:(sub + 1) * 128],
                            wq6b[:, c * 768 + 256 + j * 384:c * 768 + 384 + j * 384],
                            c == 0 and sub == 0, c == NCH - 1 and sub == 3,
                            [ldc, ld_w] + war_ps))
                mm_last[(sp, c)] = mms[5][-1]
            evs = []
            for j in range(2):
                e1 = P.sc(lambda e, j=j, sp=sp: e.activation(
                    qk[j][:, sp * 512:(sp + 1) * 512], (PQ0, PQ1)[j][:], AF.Copy),
                    after=[mms[2 * j][-1]])
                ev_by_ps[2 * j] = e1
                e2 = P.sc(lambda e, j=j, sp=sp: e.activation(
                    qk[j][:, SEQ + sp * 512:SEQ + (sp + 1) * 512], (PK0, PK1)[j][:], AF.Copy),
                    after=[mms[2 * j + 1][-1]])
                ev_by_ps[2 * j + 1] = e2
                e3 = P.ve(lambda e, j=j, sp=sp: e.tensor_copy(
                    vsb[j][:, sp * 512:(sp + 1) * 512], (PV0, PV1)[j][:]),
                    after=[mms[4 + j][-1]])
                ev_by_ps[4 + j] = e3
                evs += [e1, e2, e3]
            ev_hist.append(evs[-1])
            ev_all.extend(evs)

        bnd = []
        for j in range(2):
            deps_prev = [bnd[-1]] if bnd else []
            sq_ops = []
            mx_ops = []
            prev = [m1, m2] + ev_all + deps_prev
            for half in range(2):
                for c in range(8):
                    sqo = P.sc(lambda e, j=j, half=half, c=c: e.activation(
                        sq, qk[j][:, half * SEQ + c * NT:half * SEQ + (c + 1) * NT],
                        AF.Square), after=prev + ([sq_ops[-1]] if sq_ops else []) +
                        ([mx_ops[-1]] if mx_ops else []))
                    evh = []
                    for h in range(2):
                        mmh = MM(P, PZ[0:1, :], onesb[:],
                                 sq[:, h * 512:(h + 1) * 512], True, True,
                                 [sqo] + ([mx_ops[-1]] if mx_ops else []) + evh)
                        evh.append(P.ve(lambda e, h=h: e.tensor_copy(
                            rowa[0:1, h * 512:(h + 1) * 512], PZ[0:1, :]), after=[mmh]))
                    mx = P.ve(lambda e, j=j, half=half, c=c: e.reduce_max(
                        rowa[0:1, NT + half * 16 + c:NT + half * 16 + c + 1],
                        rowa[0:1, 0:1024], axis=AX.X), after=evh)
                    sq_ops.append(sqo)
                    mx_ops.append(mx)
            mq = P.ve(lambda e, j=j: e.reduce_max(rowa[0:1, NT + 40:NT + 41],
                                                  rowa[0:1, NT:NT + 8], axis=AX.X),
                      after=mx_ops)
            mk_ = P.ve(lambda e, j=j: e.reduce_max(rowa[0:1, NT + 41:NT + 42],
                                                   rowa[0:1, NT + 16:NT + 24], axis=AX.X),
                       after=mx_ops)
            vm = P.ve(lambda e: e.tensor_mul(rowa[0:1, NT + 40:NT + 41],
                                             rowa[0:1, NT + 40:NT + 41],
                                             rowa[0:1, NT + 41:NT + 42]), after=[mq, mk_])
            sm = P.sc(lambda e: e.activation(rowa[0:1, NT + 40:NT + 41],
                                             rowa[0:1, NT + 40:NT + 41], AF.Sqrt),
                      after=[vm] + sq_ops)
            vb = P.ve(lambda e: e.tensor_scalar_mul(rowa[0:1, NT + 40:NT + 41],
                                                    rowa[0:1, NT + 40:NT + 41],
                                                    -1.001 * RS), after=[sm])
            rp = MM(P, PRP[:, 0:1], onesr[0:1, :], rowa[0:1, NT + 40:NT + 41],
                    True, True, [vb, m2])
            eb = P.ve(lambda e, j=j: e.tensor_copy(bcol[j][:], PRP[:, 0:1]), after=[rp])
            bnd.append(eb)

        PS = (PQ0, PK0, PQ1, PK1)
        PRPh = (PRP, PV1)
        PZh = (PZ, PZ2)
        ao_done = []
        prev_erp = None
        prev_eao = None
        expop = [{} for _ in range(2)]
        avop = [{} for _ in range(2)]
        zops = [{} for _ in range(2)]
        ev_qc = [[], []]
        steps = []
        for qc in range(NSP):
            steps += [(qc, kt) for kt in range(4 * qc + 4)]
            steps.append(("z", qc))
        sched = []
        for idx in range(len(steps)):
            sched.append((0, steps[idx]))
            sched.append((1, steps[idx]))
        pend = []

        def flush_one():
            jh2, qc2, kt2, dm = pend.pop(0)
            nkt2 = 4 * qc2 + 4
            sle2 = 3 * jh2 + kt2 % 3
            avop[jh2][(qc2, kt2)] = MM(
                P, PRPh[jh2][:], vsb[jh2][:, kt2 * 128:(kt2 + 1) * 128],
                Esl[:, sle2 * 512:(sle2 + 1) * 512],
                kt2 == 0, kt2 == nkt2 - 1,
                [dm, (bnd[1] if qc2 == 0 else ev_qc[jh2][-1]) if kt2 == 0 else None])
            zops[jh2][(qc2, kt2)] = MM(
                P, PZh[jh2][0:1, :], onesb[:],
                Esl[:, sle2 * 512:(sle2 + 1) * 512],
                kt2 == 0, kt2 == nkt2 - 1,
                [dm, (bnd[1] if qc2 == 0 else ev_qc[jh2][-1]) if kt2 == 0 else None])

        for jh, step in sched:
            if step[0] == "z":
                while any(p[0] == jh for p in pend):
                    flush_one()
                qc = step[1]
                nkt = 4 * qc + 4
                ez = P.ve(lambda e, jh=jh: e.tensor_copy(
                    rowa[0:1, 0:512], PZh[jh][0:1, :]),
                    after=[zops[jh][(qc, nkt - 1)], prev_eao])
                rz = P.ve(lambda e: e.reciprocal(rowa[0:1, 512:1024], rowa[0:1, 0:512]),
                          after=[ez])
                rpz = MM(P, PS[2 * jh][:], onesr[0:1, :], rowa[0:1, 512:1024],
                         True, True, [rz, expop[jh][(qc, nkt - 1)]])
                erp = P.sc(lambda e, jh=jh: e.copy(rzrep[:], PS[2 * jh][:]),
                           after=[rpz, prev_eao])
                prev_erp = erp
                eao = P.ve(lambda e, jh=jh, qc=qc: e.tensor_mul(
                    aosb[jh][:, qc * 512:(qc + 1) * 512], PRPh[jh][:], rzrep[:]),
                    after=[erp, avop[jh][(qc, nkt - 1)]])
                prev_eao = eao
                ev_qc[jh].append(eao)
                if qc == NSP - 1:
                    ao_done.append(eao)
                continue
            qc, kt = step
            nkt = 4 * qc + 4
            sl = 2 * jh + kt % 2
            sle = 3 * jh + kt % 3
            qrhs = qk[jh][:, qc * 512:(qc + 1) * 512]
            klhs = qk[jh][:, SEQ + kt * 128:SEQ + kt * 128 + 128]
            deps = [bnd[jh], expop[jh].get((qc, kt - 2))]
            if kt < 2:
                deps += [ev_qc[jh][-1] if ev_qc[jh] else None]
                if qc == 0:
                    deps += ev_all[-6:] + [bnd[1]]
                else:
                    deps += [expop[jh].get((qc - 1, 4 * qc - 2 + (kt % 2)))]
                    if kt == 0:
                        deps += [prev_erp]
            sT = MM(P, PS[sl][:], klhs, qrhs, True, True, deps)
            ex = P.sc(lambda e, sle=sle, sl=sl, b=bcol[jh]: e.activation(
                Esl[:, sle * 512:(sle + 1) * 512], PS[sl][:], AF.Exp,
                scale=RS, bias=b[:]),
                after=[sT, bnd[1], avop[jh].get((qc, kt - 3)),
                       zops[jh].get((qc, kt - 3))])
            expop[jh][(qc, kt)] = ex
            m_i = kt - 4 * qc
            dep_mask = ex
            if m_i >= 0:
                dep_mask = P.ve(lambda e, sle=sle, m_i=m_i: e.tensor_mul(
                    Esl[:, sle * 512:(sle + 1) * 512],
                    Esl[:, sle * 512:(sle + 1) * 512],
                    trib[:, m_i * 512:(m_i + 1) * 512]),
                    after=[ex, ld_t])
            pend.append((jh, qc, kt, dep_mask))
            while len(pend) > 4:
                flush_one()

        sts = []
        mmo = {}
        prev_ev_bank = {}
        prev_ev = None
        for co in range(NCH):
            for sp in range(NSP):
                sl = (co * NSP + sp) % 2
                bank = (PV0, PV1)[sl]
                for j in range(2):
                    mmo[(co, sp, j)] = MM(
                        P, bank[:],
                        wo2b[:, j * D + co * 128:j * D + (co + 1) * 128],
                        aosb[j][:, sp * 512:(sp + 1) * 512],
                        j == 0, j == 1,
                        [ld_o, ao_done[j],
                         prev_ev_bank.get(sl) if j == 0 else None,
                         ao_done[1] if (co == 0 and sp < 2) else None,
                         prev_eao if (co == 0 and sp < 2) else None])
                ev = P.ve(lambda e, sl=sl, bank=bank: e.tensor_copy(
                    ppsb[:, sl * 512:(sl + 1) * 512], bank[:]),
                    after=[mmo[(co, sp, 1)], prev_ev,
                           sts[-2] if len(sts) >= 2 else None])
                st = P.sy_dma(part_d[co * 128:(co + 1) * 128, sp * 512:(sp + 1) * 512],
                              ppsb[:, sl * 512:(sl + 1) * 512], after=[ev])
                prev_ev_bank[sl] = ev
                prev_ev = ev
                sts.append(st)
        P.gp(lambda e: e.nop(), after=sts + quant_sts)
        P.finalize()
    return nc



def emit_quant(P, nc, E, dram):
    H = 5632
    SL12 = H // NCORES
    SL3 = D // NCORES
    w1_d, w2_d, w3_d = dram["w1t"], dram["w2t"], dram["w3t"]
    sl1_d, sl2_d, sl3_d = dram["sl1"], dram["sl2"], dram["sl3"]
    q1_d, q2_d, q3_d = dram["q1o"], dram["q2o"], dram["q3o"]
    scr_d = nc.dram_tensor("q_scr", [1, 128], F32, kind="Internal").ap()
    scr2_d = nc.dram_tensor("q_scr2", [1, 8], F32, kind="Internal").ap()

    blk = E(nc.sbuf_tensor("s_qblk", [128, 2 * 2048], F32))
    qa = E(nc.sbuf_tensor("s_qqa", [128, 704], F32))
    qb = E(nc.sbuf_tensor("s_qqb", [128, 704], F32))
    qo = E(nc.sbuf_tensor("s_qqo", [128, 704], BF16))
    acc = E(nc.sbuf_tensor("s_qacc", [128, 152], F32))
    thr = E(nc.sbuf_tensor("s_qthr", [128, 8], F32))
    g1 = E(nc.sbuf_tensor("s_qg1", [1, 144], F32))

    specs = [("w1", w1_d, D, H, [(0, 2048), (2048, 4096), (4096, 5632)], SL12, sl1_d, q1_d, 0),
             ("w2", w2_d, D, H, [(0, 2048), (2048, 4096), (4096, 5632)], SL12, sl2_d, q2_d, 1),
             ("w3", w3_d, H, D, [(0, 2048)], SL3, sl3_d, q3_d, 2)]
    aoff = [0, 50, 100]
    sts = []
    prev_tail = None
    for mi, (nm, wd, R, C, SPL, SL, sld, qd, tcol) in enumerate(specs):
        nch = R // 128
        wv = wd.rearrange("(g p) c -> p g c", g=nch)
        abs_ops = []
        nblk = 0
        for g in range(nch):
            for (c0, c1) in SPL:
                CS = c1 - c0
                sl = nblk % 2
                war = [abs_ops[nblk - 2]] if nblk >= 2 else ([prev_tail] if prev_tail else [])
                ld = P.gp_dma(blk[:, sl * 2048:sl * 2048 + CS],
                              wv[:, g, c0:c1], after=war)
                ab = P.sc(lambda e, sl=sl, CS=CS, col=aoff[mi] + nblk: e.activation(
                    blk[:, sl * 2048:sl * 2048 + CS], blk[:, sl * 2048:sl * 2048 + CS],
                    AF.Abs, accum_out=acc[:, col:col + 1]),
                    after=[ld] + ([abs_ops[-1]] if abs_ops else []))
                abs_ops.append(ab)
                nblk += 1
        red = P.ve(lambda e, mi=mi, nblk=nblk: e.reduce_sum(
            acc[:, 148:149], acc[:, aoff[mi]:aoff[mi] + nblk],
            axis=AX.X), after=abs_ops + ([prev_tail] if prev_tail else []))
        d1 = P.gp_dma(scr_d.rearrange("o p -> p o"), acc[:, 148:149], after=[red])
        d2 = P.gp_dma(g1[0:1, 16:144], scr_d, after=[d1])
        vtot = P.ve(lambda e, mi=mi: e.reduce_sum(
            g1[0:1, mi * 2:mi * 2 + 1], g1[0:1, 16:144], axis=AX.X), after=[d2, red])
        gi = g1[0:1, mi * 2:mi * 2 + 1]
        e2 = P.ve(lambda e, gi=gi, R=R, C=C: e.tensor_scalar(
            gi, gi, 0.5 / (R * C), 0.5e-8, ALU.mult, ALU.add), after=[vtot])
        e3 = P.ve(lambda e, gi=gi, mi=mi: e.tensor_scalar_mul(
            g1[0:1, mi * 2 + 1:mi * 2 + 2], gi, -1.0), after=[e2])
        d3 = P.gp_dma(scr2_d[0:1, mi * 2:mi * 2 + 2], g1[0:1, mi * 2:mi * 2 + 2],
                      after=[e3])
        d4 = P.gp_dma(thr[:, 2 * tcol:2 * tcol + 1],
                      scr2_d[0:1, mi * 2:mi * 2 + 1].broadcast_to([128, 1]),
                      after=[d3])
        d5 = P.gp_dma(thr[:, 2 * tcol + 1:2 * tcol + 2],
                      scr2_d[0:1, mi * 2 + 1:mi * 2 + 2].broadcast_to([128, 1]),
                      after=[d3])
        slv = sld.rearrange("(g p) c -> p g c", g=nch)
        qv = qd.rearrange("(g p) c -> p g c", g=nch)
        q_sts = []
        prev_q = None
        a2s = []
        for g in range(nch):
            sl = g % 2
            ldq = P.gp_dma(blk[:, sl * 2048:sl * 2048 + SL], slv[:, g, :],
                           after=[abs_ops[-1]] + ([a2s[g - 2]] if g >= 2 else []))
            a1 = P.sc(lambda e, sl=sl, SL=SL, c=tcol: e.activation(
                qa[:, 0:SL], blk[:, sl * 2048:sl * 2048 + SL], AF.Sign,
                bias=thr[:, 2 * c + 1:2 * c + 2]),
                after=[ldq, d4, d5] + ([prev_q] if prev_q else []))
            a2 = P.sc(lambda e, sl=sl, SL=SL, c=tcol: e.activation(
                qb[:, 0:SL], blk[:, sl * 2048:sl * 2048 + SL], AF.Sign,
                bias=thr[:, 2 * c:2 * c + 1]),
                after=[ldq, d4, d5, a1])
            a2s.append(a2)
            vv = P.ve(lambda e, SL=SL: e.tensor_add(qo[:, 0:SL], qa[:, 0:SL], qb[:, 0:SL]),
                      after=[a1, a2] + ([q_sts[-1]] if q_sts else []))
            stq = P.gp_dma(qv[:, g, :], qo[:, 0:SL], after=[vv])
            prev_q = vv
            q_sts.append(stq)
        sts.extend(q_sts)
        prev_tail = q_sts[-1]
    return sts



def build_l1ffn():
    nc = bass.Bass("TRN2", target_bir_lowering=False, debug=False)
    H = 5632
    NH = H // 128
    HB = NH // 2
    x_d = nc.dram_tensor("x", [D, NT], F32, kind="ExternalInput").ap()
    xn_d = nc.dram_tensor("xni", [D, NT], BF16, kind="ExternalInput").ap()
    q1_d = nc.dram_tensor("q1t", [NH, NCH, 128, 128], BF16, kind="ExternalInput").ap()
    q2_d = nc.dram_tensor("q2t", [NH, NCH, 128, 128], BF16, kind="ExternalInput").ap()
    q3_d = nc.dram_tensor("q3t", [NH, NCH, 128, 128], BF16, kind="ExternalInput").ap()
    xo_d = nc.dram_tensor("xo", [D, NT], F32, kind="ExternalOutput").ap()
    xn2_d = nc.dram_tensor("xn", [D, NT], BF16, kind="ExternalOutput").ap()

    with contextlib.ExitStack() as ctx:
        E = ctx.enter_context
        xs = E(nc.sbuf_tensor("s_xs", [128, NCH * NT], F32))
        xnb = E(nc.sbuf_tensor("s_xnb", [128, NCH * NT], BF16))
        w12 = E(nc.sbuf_tensor("s_w12", [128, 4 * NCH * 128], BF16))
        w3s = E(nc.sbuf_tensor("s_w3s", [128, 2 * HB * 128], BF16))
        sgs = E(nc.sbuf_tensor("s_sgs", [128, 4 * 512], F32))
        g_h = nc.sbuf_tensor("s_hsb", [128, HB * NT], BF16, side="right")
        hsb = g_h.__enter__()
        gPX = [nc.psum_tensor(f"s_PXI{i}", [128, 512], F32) for i in range(4)]
        PX = [g.__enter__() for g in gPX]
        gPO = [nc.psum_tensor(f"s_POI{i}", [128, 512], F32) for i in range(2)]
        PO = [g.__enter__() for g in gPO]

        P = Prog2(nc)
        ld_x = P.sy_dma(xs[:], x_d.rearrange("(g p) n -> p g n", g=NCH))
        ld_xn = P.sy_dma(xnb[:], xn_d.rearrange("(g p) n -> p g n", g=NCH))

        resid = []
        h_last = {}
        mm_tail = {}
        sg_ops = {}
        rv_ops = {}
        for phase in range(2):
            for lmt in range(HB):
                mt = phase * HB + lmt
                sl = mt % 2
                war = ([mm_tail.get((mt - 2, hf, i2))
                        for hf in range(2) for i2 in range(2)]
                       if mt >= 2 else [])
                l1 = P.gp_dma(w12[:, sl * 2 * NCH * 128:sl * 2 * NCH * 128 + NCH * 128]
                              .rearrange("p (g m) -> p g m", g=NCH),
                              q1_d[mt].rearrange("g p m -> p g m"), after=war)
                l2 = P.gp_dma(w12[:, sl * 2 * NCH * 128 + NCH * 128:(sl + 1) * 2 * NCH * 128]
                              .rearrange("p (g m) -> p g m", g=NCH),
                              q2_d[mt].rearrange("g p m -> p g m"), after=war)
                base = sl * 2 * NCH * 128
                for half in range(2):
                    mm1 = mm2 = None
                    for c in range(NCH):
                        rhs = xnb[:, c * NT + half * 512:c * NT + (half + 1) * 512]
                        mm1 = MM(P, PX[half][:],
                                 w12[:, base + c * 128:base + (c + 1) * 128],
                                 rhs, c == 0, c == NCH - 1,
                                 [l1, ld_xn] +
                                 ([sg_ops.get((mt - 1, half))] if c == 0 else []))
                        mm2 = MM(P, PX[2 + half][:],
                                 w12[:, base + NCH * 128 + c * 128:
                                     base + NCH * 128 + (c + 1) * 128],
                                 rhs, c == 0, c == NCH - 1,
                                 [l2, ld_xn] +
                                 ([h_last.get((mt - 1, half))] if c == 0 else []))
                    mm_tail[(mt, half, 0)] = mm1
                    mm_tail[(mt, half, 1)] = mm2
                    sgr = sgs[:, ((mt % 2) * 2 + half) * 512:
                              ((mt % 2) * 2 + half + 1) * 512]
                    sg = P.sc(lambda e, sgr=sgr, half=half: e.activation(
                        sgr, PX[half][:], AF.Sigmoid, scale=0.5),
                        after=[mm1, h_last.get((mt - 2, half))])
                    sg_ops[(mt, half)] = sg
                    hh = P.ve(lambda e, lmt=lmt, half=half, sgr=sgr:
                              e.scalar_tensor_tensor(
                                  hsb[:, lmt * NT + half * 512:
                                      lmt * NT + (half + 1) * 512],
                                  PX[2 + half][:], 0.5, sgr,
                                  ALU.mult, ALU.mult),
                              after=[sg, mm2] +
                              ([mm_tail.get((phase - 1, NCH - 1, "o", 1))]
                               if lmt == 0 and half == 0 else []))
                    h_last[(mt, half)] = hh
            for co in range(NCH):
                sl = co % 2
                war = [mm_tail.get((phase, co - 2, "o", hf)) for hf in range(2)]                     if co >= 2 else                     ([mm_tail.get((phase - 1, NCH - 2, "o", hf)) for hf in range(2)] +
                     [mm_tail.get((phase - 1, NCH - 1, "o", hf)) for hf in range(2)]
                     if phase else [])
                l3 = P.gp_dma(w3s[:, sl * HB * 128:(sl + 1) * HB * 128]
                              .rearrange("p (g m) -> p g m", g=HB),
                              q3_d[phase * HB:(phase + 1) * HB, co, :, :]
                              .rearrange("g p m -> p g m"),
                              after=war)
                mmo = [None, None]
                for hc in range(HB):
                    for half in range(2):
                        mt = phase * HB + hc
                        mmo[half] = MM(
                            P, PO[half][:],
                            w3s[:, sl * HB * 128 + hc * 128:sl * HB * 128 + (hc + 1) * 128],
                            hsb[:, hc * NT + half * 512:hc * NT + (half + 1) * 512],
                            hc == 0, hc == HB - 1,
                            [l3, h_last[(mt, half)]] +
                            ([rv_ops.get((phase, co - 1, half)),
                              rv_ops.get((phase - 1, NCH - 1, half))]
                             if hc == 0 else []))
                for half in range(2):
                    mm_tail[(phase, co, "o", half)] = mmo[half]
                    rv = P.ve(lambda e, co=co, half=half: e.scalar_tensor_tensor(
                        xs[:, co * NT + half * 512:co * NT + (half + 1) * 512],
                        PO[half][:], 0.25,
                        xs[:, co * NT + half * 512:co * NT + (half + 1) * 512],
                        ALU.mult, ALU.add), after=[mmo[half], ld_x])
                    rv_ops[(phase, co, half)] = rv
                    resid.append(rv)

        st1 = P.sy_dma(xo_d.rearrange("(g p) n -> p g n", g=NCH),
                       xs[:].rearrange("p (g n) -> p g n", g=NCH),
                       after=resid)
        g_h.__exit__(None, None, None)
        for g in reversed(gPO):
            g.__exit__(None, None, None)
        for g in reversed(gPX):
            g.__exit__(None, None, None)
        PA = E(nc.psum_tensor("s_PA", [128, 1024], F32))
        PB = E(nc.psum_tensor("s_PB", [128, 1024], F32))
        xn_ops, _ = emit_ln(P, nc, E, "L", xs, xnb, PA, PB,
                            resid[-1], extra_first=resid + [st1],
                            xs_readers=[st1])
        st2 = P.sy_dma(xn2_d.rearrange("(g p) n -> p g n", g=NCH),
                       xnb[:].rearrange("p (g n) -> p g n", g=NCH),
                       after=xn_ops)
        P.gp(lambda e: e.nop(), after=[st1, st2])
        P.finalize()
    return nc



def build_l1attn_pre():
    nc = bass.Bass("TRN2", target_bir_lowering=False, debug=False)
    xn_d = nc.dram_tensor("xni", [D, NT], BF16, kind="ExternalInput").ap()
    wqkvT_d = nc.dram_tensor("wqkvT", [D, 3 * D], F32, kind="ExternalInput").ap()
    wlrT_d = nc.dram_tensor("wlrT", [D, 48], F32, kind="ExternalInput").ap()
    w0T_d = nc.dram_tensor("w0T", [D, 128], F32, kind="ExternalInput").ap()
    w2T_d = nc.dram_tensor("w2T", [D, 128], F32, kind="ExternalInput").ap()
    w1_d = nc.dram_tensor("w1f", [D, 128], F32, kind="ExternalInput").ap()
    eyeb_d = nc.dram_tensor("eyeb_d", [128, 128], BF16, kind="ExternalInput").ap()
    dwp_d = nc.dram_tensor("dwp", [3 * D, 128], F32, kind="ExternalOutput").ap()
    qn_d = nc.dram_tensor("qn", [D, NT], BF16, kind="ExternalOutput").ap()

    with contextlib.ExitStack() as ctx:
        E = ctx.enter_context
        xnb = E(nc.sbuf_tensor("s_xnb", [128, NCH * NT], BF16))
        ktm = E(nc.sbuf_tensor("s_ktm", [128, 8 * D], BF16))
        vtm = E(nc.sbuf_tensor("s_vtm", [128, 8 * D], BF16))
        wsp = E(nc.sbuf_tensor("s_wsp", [128, 2 * NCH * 512], BF16))
        wlrb = E(nc.sbuf_tensor("s_wlrb", [128, NCH * 48], BF16))
        qstrip = E(nc.sbuf_tensor("s_qstrip", [128, 4 * NT], BF16))
        qtmp = E(nc.sbuf_tensor("s_qtmp", [128, 2 * 512], BF16))
        sqscr = E(nc.sbuf_tensor("s_sqscr", [128, 128], F32))
        ncol = E(nc.sbuf_tensor("s_ncol", [128, 16], F32))
        lrtm = E(nc.sbuf_tensor("s_lrtm", [128, 8 * 48], F32))
        lsc = E(nc.sbuf_tensor("s_lsc", [128, 8 * 48], F32))
        lscb = E(nc.sbuf_tensor("s_lscb", [128, 8 * 48], F32))
        w0b = E(nc.sbuf_tensor("s_w0b", [128, 16 * 128], BF16))
        w2b = E(nc.sbuf_tensor("s_w2b", [128, 16 * 128], BF16))
        w1b = E(nc.sbuf_tensor("s_w1b", [128, 16 * 128], BF16))
        kfmh = E(nc.sbuf_tensor("s_kfmh", [128, 2 * NT], BF16))
        vfmh = E(nc.sbuf_tensor("s_vfmh", [128, 2 * NT], BF16))
        gbaf = E(nc.sbuf_tensor("s_gbaf", [128, NT], F32))
        sgf = E(nc.sbuf_tensor("s_sgf", [128, NT], F32))
        hbmf = E(nc.sbuf_tensor("s_hbmf", [128, NT], F32))
        dhf = E(nc.sbuf_tensor("s_dhf", [128, NT], F32))
        slgf = E(nc.sbuf_tensor("s_slgf", [128, NT], F32))
        hid16 = E(nc.sbuf_tensor("s_hid16", [128, NT], BF16))
        dhm16 = E(nc.sbuf_tensor("s_dhm16", [128, NT], BF16))
        dgb16 = E(nc.sbuf_tensor("s_dgb16", [128, NT], BF16))
        hlr = E(nc.sbuf_tensor("s_hlr", [128, NT], BF16))
        klr0 = E(nc.sbuf_tensor("s_klr0", [128, NT], BF16))
        klr2 = E(nc.sbuf_tensor("s_klr2", [128, NT], BF16))
        dwsml = E(nc.sbuf_tensor("s_dwsml", [128, 128], F32))
        eyeb = E(nc.sbuf_tensor("s_eyeb", [128, 128], BF16))
        PG = E(nc.psum_tensor("s_PG", [128, 1024], F32))
        PH = E(nc.psum_tensor("s_PH", [128, 1024], F32))
        PD = E(nc.psum_tensor("s_PD", [128, 1024], F32))
        PT2 = E(nc.psum_tensor("s_PT2", [128, 128], BF16))
        PW = E(nc.psum_tensor("s_PW", [128, 128], F32))

        P = Prog2(nc)
        ld_xn = P.sy_dma(xnb[:], xn_d.rearrange("(g p) n -> p g n", g=NCH))
        ld_ey = P.sy_dma(eyeb[:], eyeb_d)
        ld_w0 = P.gp_dma(w0b[:].rearrange("p (g m) -> p g m", g=NCH),
                         w0T_d.rearrange("(g p) m -> p g m", g=NCH))
        ld_w2 = P.gp_dma(w2b[:].rearrange("p (g m) -> p g m", g=NCH),
                         w2T_d.rearrange("(g p) m -> p g m", g=NCH))
        ld_w1 = P.gp_dma(w1b[:].rearrange("p (g m) -> p g m", g=NCH),
                         w1_d.rearrange("(g p) m -> p g m", g=NCH))
        ld_lr = P.gp_dma(wlrb[:].rearrange("p (g m) -> p g m", g=NCH),
                         wlrT_d.rearrange("(g p) m -> p g m", g=NCH))

        lr_v2 = []
        v0_prev = {}
        for nt in range(8):
            pb = PH[:, (nt % 2) * 512:(nt % 2) * 512 + 48]
            mmr = None
            for c in range(NCH):
                mmr = MM(P, pb,
                         xnb[:, c * NT + nt * 128:c * NT + (nt + 1) * 128],
                         wlrb[:, c * 48:(c + 1) * 48],
                         c == 0, c == NCH - 1,
                         [ld_xn, ld_lr] +
                         ([v0_prev.get(nt % 2)] if c == 0 else []))
            raw = lsc[:, nt * 48:(nt + 1) * 48]
            bb = lscb[:, nt * 48:(nt + 1) * 48]
            v0 = P.ve(lambda e, raw=raw, pb=pb: e.tensor_scalar(
                raw, pb, 1.0, BASE_LR_INV, ALU.mult, ALU.add), after=[mmr])
            v0_prev[nt % 2] = v0
            a1 = P.sc(lambda e, raw=raw, bb=bb: e.activation(bb, raw, AF.Abs),
                      after=[v0])
            a2 = P.sc(lambda e, bb=bb: e.activation(bb, bb, AF.Exp, scale=-1.0),
                      after=[a1])
            a3 = P.sc(lambda e, bb=bb: e.activation(bb, bb, AF.Ln, bias=1.0),
                      after=[a2])
            dst = lrtm[:, nt * 48:(nt + 1) * 48]
            v1 = P.ve(lambda e, d=dst, raw=raw: e.tensor_scalar_max(d, raw, 0.0),
                      after=[v0, a1])
            v2 = P.ve(lambda e, d=dst, bb=bb: e.tensor_add(d, d, bb),
                      after=[v1, a3])
            lr_v2.append(v2)

        state = dict(tp=None, stq=None, sq=None, gmm={}, pd={}, ncol={},
                     qtmp={}, slot={})
        ktail, vtail = {}, {}
        st_qs = []

        def emit_span(sp, gi):
            j = sp // 4
            hbase = (sp % 4) * 4
            sl = gi % 2
            wld = P.gp_dma(
                wsp[:, sl * 8192:(sl + 1) * 8192].rearrange(
                    "p (g m) -> p g m", g=NCH),
                wqkvT_d[:, sp * 512:(sp + 1) * 512].rearrange(
                    "(g p) m -> p g m", g=NCH),
                after=[state["slot"].get(gi - 2)])
            for nt in range(8):
                gx = (gi * 8 + nt) % 2
                pb = PD[:, gx * 512:gx * 512 + 512]
                mmq = None
                for c in range(NCH):
                    mmq = MM(P, pb,
                             xnb[:, c * NT + nt * 128:c * NT + (nt + 1) * 128],
                             wsp[:, sl * 8192 + c * 512:sl * 8192 + (c + 1) * 512],
                             c == 0, c == NCH - 1,
                             [ld_xn, wld] +
                             ([state["pd"].get(gx)] if c == 0 else []))
                state["slot"][gi] = mmq
                if j == 0:
                    dst = qtmp[:, (nt % 2) * 512:(nt % 2) * 512 + 512]
                    war = state["qtmp"].get(nt % 2, [])
                elif j == 1:
                    dst = ktm[:, nt * D + hbase * 128:nt * D + hbase * 128 + 512]
                    war = []
                else:
                    dst = vtm[:, nt * D + hbase * 128:nt * D + hbase * 128 + 512]
                    war = []
                evq = P.sc(lambda e, d=dst, pb=pb: e.activation(d, pb, AF.Silu),
                           after=[mmq] + war)
                state["pd"][gx] = evq
                if j == 2:
                    for hs in range(4):
                        vtail[(hbase + hs, nt)] = evq
                    continue
                nb = (nt % 2) * 8
                sqs = []
                for hs in range(4):
                    tile = dst[:, hs * 128:(hs + 1) * 128]
                    sqo = P.sc(lambda e, t=tile, c=nb + hs: e.activation(
                        sqscr[:], t, AF.Square, accum_out=ncol[:, c:c + 1]),
                        after=[evq, state["sq"], state["ncol"].get(nt % 2)])
                    state["sq"] = sqo
                    sqs.append(sqo)
                sro = P.sc(lambda e, nb=nb: e.activation(
                    ncol[:, nb + 4:nb + 8], ncol[:, nb:nb + 4], AF.Sqrt),
                    after=sqs + state["ncol"].get(nt % 2, []))
                va = P.ve(lambda e, nb=nb: e.tensor_scalar_add(
                    ncol[:, nb + 4:nb + 8], ncol[:, nb + 4:nb + 8], 1e-5),
                    after=[sro])
                vb = P.ve(lambda e, nb=nb: e.reciprocal(
                    ncol[:, nb + 4:nb + 8], ncol[:, nb + 4:nb + 8]), after=[va])
                vcs = []
                for hs in range(4):
                    tile = dst[:, hs * 128:(hs + 1) * 128]
                    vc = P.ve(lambda e, t=tile, c=nb + 4 + hs: e.tensor_single_scalar(
                        t, t, ncol[:, c:c + 1], ALU.mult), after=[vb])
                    vcs.append(vc)
                    if j == 1:
                        ktail[(hbase + hs, nt)] = vc
                    else:
                        tp = TP(P, PT2[:], tile, eyeb[:],
                                [vc, ld_ey, state["tp"]])
                        cp = P.ve(lambda e, hs=hs, nt=nt: e.tensor_copy(
                            qstrip[:, hs * NT + nt * 128:hs * NT + (nt + 1) * 128],
                            PT2[:]), after=[tp, state["stq"]])
                        state["tp"] = cp
                        vcs[-1] = cp
                state["ncol"][nt % 2] = vcs
                if j == 0:
                    state["qtmp"][nt % 2] = vcs
            if j == 0:
                stq = P.sy_dma(
                    qn_d[hbase * 128:(hbase + 4) * 128, :].rearrange(
                        "(h p) n -> p h n", h=4),
                    qstrip[:].rearrange("p (h n) -> p h n", h=4),
                    after=[state["tp"]])
                state["stq"] = stq
                st_qs.append(stq)

        for gi, sp in enumerate((4, 5, 6, 7, 8, 9, 10, 11)):
            emit_span(sp, gi)

        pw = dict(e_d=None, e_g=None, e_h=None, e_s=None, e_slg=None,
                  v_hid=None, v_dhm=None, v_dg=None, vb2=None, vb4=None,
                  vdgb=None, dw0=None, dw1=None, dw2=None, scl=None,
                  dwev=None, stw=None)
        dw_sts = []
        for h in range(16):
            par = h % 2
            kbuf = kfmh[:, par * NT:(par + 1) * NT]
            vbuf = vfmh[:, par * NT:(par + 1) * NT]
            cpk, cpv = [], []
            for nt in range(8):
                tpk = TP(P, PT2[:], ktm[:, nt * D + h * 128:nt * D + (h + 1) * 128],
                         eyeb[:], [ktail[(h, nt)], ld_ey, state["tp"]])
                ck = P.sc(lambda e, kbuf=kbuf, nt=nt: e.copy(
                    kbuf[:, nt * 128:(nt + 1) * 128], PT2[:]), after=[tpk])
                state["tp"] = ck
                cpk.append(ck)
                tpv = TP(P, PT2[:], vtm[:, nt * D + h * 128:nt * D + (h + 1) * 128],
                         eyeb[:], [vtail[(h, nt)], state["tp"]])
                cv = P.ve(lambda e, vbuf=vbuf, nt=nt: e.tensor_copy(
                    vbuf[:, nt * 128:(nt + 1) * 128], PT2[:]), after=[tpv])
                state["tp"] = cv
                cpv.append(cv)
            mm_g = []
            for nt in range(8):
                mm_g.append(MM(P, PG[:, nt * 128:(nt + 1) * 128],
                               kbuf[:, nt * 128:(nt + 1) * 128],
                               w0b[:, h * 128:(h + 1) * 128],
                               nt % 4 == 0, nt % 4 == 3,
                               [cpk[nt], ld_w0] +
                               ([pw["e_d"]] if nt in (0, 4) else [])))
            e_g = P.sc(lambda e: e.copy(gbaf[:], PG[:]),
                       after=[mm_g[3], mm_g[7], pw["vb2"], pw["e_slg"]])
            mm_h = []
            for nt in range(8):
                mm_h.append(MM(P, PH[:, nt * 128:(nt + 1) * 128],
                               kbuf[:, nt * 128:(nt + 1) * 128],
                               w2b[:, h * 128:(h + 1) * 128],
                               nt % 4 == 0, nt % 4 == 3,
                               [cpk[nt], ld_w2] +
                               (([pw["e_h"]] if h else
                                 [v0_prev[0], v0_prev[1]]) if nt in (0, 4) else [])))
            e_h = P.sc(lambda e: e.copy(hbmf[:], PH[:]),
                       after=[mm_h[3], mm_h[7], pw["v_dg"], pw["v_hid"]])
            mm_d = []
            for nt in range(8):
                mm_d.append(MM(P, PG[:, nt * 128:(nt + 1) * 128],
                               vbuf[:, nt * 128:(nt + 1) * 128],
                               w1b[:, h * 128:(h + 1) * 128],
                               nt % 4 == 0, nt % 4 == 3,
                               [cpv[nt], ld_w1] +
                               ([e_g] if nt in (0, 4) else [])))
            e_d = P.sc(lambda e: e.copy(dhf[:], PG[:]),
                       after=[mm_d[3], mm_d[7], pw["vdgb"]])
            e_s = P.sc(lambda e: e.activation(sgf[:], gbaf[:], AF.Sigmoid),
                       after=[e_g, pw["vb4"]])
            e_slg = P.sc(lambda e: e.activation(slgf[:], gbaf[:], AF.Silu),
                         after=[e_g, pw["vdgb"]])
            v_hid = P.ve(lambda e: e.tensor_mul(hid16[:], slgf[:], hbmf[:]),
                         after=[e_slg, e_h, pw["scl"]])
            v_dhm = P.ve(lambda e: e.tensor_mul(dhm16[:], dhf[:], slgf[:]),
                         after=[e_slg, e_d, pw["dw2"]])
            v_dg = P.ve(lambda e: e.tensor_mul(dhf[:], dhf[:], hbmf[:]),
                        after=[v_dhm, e_d, e_h])
            v_b1 = P.sc(lambda e: e.activation(slgf[:], sgf[:], AF.Copy,
                                               scale=-1.0, bias=1.0),
                        after=[e_s, v_hid, v_dhm])
            v_b2 = P.ve(lambda e: e.tensor_mul(slgf[:], gbaf[:], slgf[:]),
                        after=[v_b1])
            v_b3 = P.sc(lambda e: e.activation(slgf[:], slgf[:], AF.Copy,
                                               bias=1.0), after=[v_b2])
            v_b4 = P.ve(lambda e: e.tensor_mul(slgf[:], slgf[:], sgf[:]),
                        after=[v_b3])
            v_dgb = P.ve(lambda e: e.tensor_mul(dgb16[:], dhf[:], slgf[:]),
                         after=[v_b4, v_dg, pw["dw0"]])
            scs = []
            for nt in range(8):
                l0 = lrtm[:, nt * 48 + h * 3 + 0:nt * 48 + h * 3 + 1]
                l1c = lrtm[:, nt * 48 + h * 3 + 1:nt * 48 + h * 3 + 2]
                l2c = lrtm[:, nt * 48 + h * 3 + 2:nt * 48 + h * 3 + 3]
                sc0 = P.ve(lambda e, nt=nt, h=h, l=l0: e.tensor_single_scalar(
                    klr0[:, nt * 128:(nt + 1) * 128],
                    ktm[:, nt * D + h * 128:nt * D + (h + 1) * 128],
                    l, ALU.mult),
                    after=[ktail[(h, nt)], lr_v2[nt], pw["dw0"]])
                sc1 = P.ve(lambda e, nt=nt, l=l1c: e.tensor_single_scalar(
                    hlr[:, nt * 128:(nt + 1) * 128],
                    hid16[:, nt * 128:(nt + 1) * 128],
                    l, ALU.mult), after=[v_hid, lr_v2[nt], pw["dw1"]])
                sc2 = P.ve(lambda e, nt=nt, h=h, l=l2c: e.tensor_single_scalar(
                    klr2[:, nt * 128:(nt + 1) * 128],
                    ktm[:, nt * D + h * 128:nt * D + (h + 1) * 128],
                    l, ALU.mult),
                    after=[ktail[(h, nt)], lr_v2[nt], pw["dw2"]])
                scs.append((sc0, sc1, sc2))
            dw_specs = (
                (dgb16, klr0, v_dgb, 0),
                (None, hlr, None, 1),
                (dhm16, klr2, v_dhm, 2))
            for lh, rh, dep, j2 in dw_specs:
                mmw = None
                for nt in range(8):
                    lhs = (vtm[:, nt * D + h * 128:nt * D + (h + 1) * 128]
                           if j2 == 1 else lh[:, nt * 128:(nt + 1) * 128])
                    dps = [scs[nt][j2]]
                    if j2 == 1:
                        dps.append(vtail[(h, nt)])
                    else:
                        dps.append(dep)
                    mmw = MM(P, PW[:], lhs, rh[:, nt * 128:(nt + 1) * 128],
                             nt == 0, nt == 7,
                             dps + ([pw["dwev"]] if nt == 0 else []))
                evw = P.ve(lambda e: e.tensor_copy(dwsml[:], PW[:]),
                           after=[mmw, pw["stw"]])
                stw = P.sy_dma(dwp_d[(j2 * 16 + h) * 128:(j2 * 16 + h + 1) * 128, :],
                               dwsml[:], after=[evw])
                dw_sts.append(stw)
                pw["dwev"] = evw
                pw["stw"] = stw
                pw["dw%d" % j2] = mmw
            pw.update(e_d=e_d, e_g=e_g, e_h=e_h, e_s=e_s, e_slg=e_slg,
                      v_hid=v_hid, v_dhm=v_dhm, v_dg=v_dg, vb2=v_b2,
                      vb4=v_b4, vdgb=v_dgb, scl=scs[-1][1])
            if h in (1, 3, 5, 7):
                qi = (h - 1) // 2
                emit_span(qi, 8 + qi)
        P.gp(lambda e: e.nop(), after=dw_sts + st_qs)
        P.finalize()
    return nc



def build_l1attn_post():
    nc = bass.Bass("TRN2", target_bir_lowering=False, debug=False)
    x_d = nc.dram_tensor("x", [D, NT], F32, kind="ExternalInput").ap()
    qn_d = nc.dram_tensor("qni", [D, NT], BF16, kind="ExternalInput").ap()
    dwall_d = nc.dram_tensor("dwall", [NCORES * 3 * D, 128], F32, kind="ExternalInput").ap()
    w0n_d = nc.dram_tensor("w0n", [D, 128], F32, kind="ExternalInput").ap()
    w1n_d = nc.dram_tensor("w1n", [D, 128], F32, kind="ExternalInput").ap()
    w2n_d = nc.dram_tensor("w2n", [D, 128], F32, kind="ExternalInput").ap()
    onw_d = nc.dram_tensor("onw", [128, 1], F32, kind="ExternalInput").ap()
    woT_d = nc.dram_tensor("woT", [NCH, NCH, 128, 128], F32, kind="ExternalInput").ap()
    eyeb_d = nc.dram_tensor("eyeb_d", [128, 128], BF16, kind="ExternalInput").ap()
    eyef_d = nc.dram_tensor("eyef_d", [128, 128], F32, kind="ExternalInput").ap()
    xo_d = nc.dram_tensor("xo", [D, NT], F32, kind="ExternalOutput").ap()
    xn_d = nc.dram_tensor("xn", [D, NT], BF16, kind="ExternalOutput").ap()

    dwv = dwall_d.rearrange("(r j p) m -> p r j m", r=NCORES, j=48)
    wn_ds = (w0n_d, w1n_d, w2n_d)
    with contextlib.ExitStack() as ctx:
        E = ctx.enter_context
        xs = E(nc.sbuf_tensor("s_xs", [128, NCH * NT], F32))
        ocat = E(nc.sbuf_tensor("s_ocat", [128, 16 * NT], BF16))
        wfT = [E(nc.sbuf_tensor(f"s_wfT{i}", [128, 16 * 128], BF16)) for i in range(3)]
        eyeb = E(nc.sbuf_tensor("s_eyeb", [128, 128], BF16))
        eyef = E(nc.sbuf_tensor("s_eyef", [128, 128], F32))
        onesf = E(nc.sbuf_tensor("s_onesf", [128, 1], F32))
        onesr = E(nc.sbuf_tensor("s_onesr", [1, 128], F32))
        onwc = E(nc.sbuf_tensor("s_onw", [128, 1], F32))
        xb16 = E(nc.sbuf_tensor("s_xb16", [128, 6 * 128], BF16))
        tb16 = E(nc.sbuf_tensor("s_tb16", [128, 6 * 128], BF16))
        ab16 = E(nc.sbuf_tensor("s_ab16", [128, 6 * 128], BF16))
        bm16 = E(nc.sbuf_tensor("s_bm16", [128, 6 * 128], BF16))
        colA = E(nc.sbuf_tensor("s_colA", [128, 48], F32))
        g1A = E(nc.sbuf_tensor("s_g1A", [1, 24], F32))

        P = Prog2(nc)
        ld_x = P.sy_dma(xs[:], x_d.rearrange("(g p) n -> p g n", g=NCH))
        ld_ey = P.sy_dma(eyeb[:], eyeb_d)
        ld_ef = P.sy_dma(eyef[:], eyef_d)
        ld_on = P.sy_dma(onwc[:], onw_d)
        m1 = P.ve(lambda e: e.memset(onesf[:], 1.0))
        m2 = P.ve(lambda e: e.memset(onesr[:], 1.0))

        gA = nc.sbuf_tensor("s_phA", [128, 40 * 128], F32, side="right")
        phA = gA.__enter__()
        OF_DW = 0
        OF_WO = 6 * 128
        OF_SL = 12 * 128
        OF_SC = 28 * 128
        OF_W2 = 34 * 128
        gPW1 = nc.psum_tensor("s_PW1", [128, 128], F32)
        PW1 = gPW1.__enter__()
        gPT2 = nc.psum_tensor("s_PT2", [128, 128], BF16)
        PT2 = gPT2.__enter__()
        gP6 = [nc.psum_tensor(f"s_P6_{k}", [128, 128], F32) for k in range(6)]
        P6 = [g.__enter__() for g in gP6]
        prev_tp = None
        evd_prev = None
        slot_mms = {}
        ldws = {}
        final_ops = []
        for b in range(8):
            mats = list(range(b * 6, b * 6 + 6))
            last_x, last_t, last_p = {}, {}, {}
            for j2 in mats:
                k = j2 % 6
                sl = j2 % 2
                dst = phA[:, OF_SL + sl * 8 * 128:OF_SL + (sl + 1) * 8 * 128]
                qf = P.sy_dma if sl == 0 else P.gp_dma
                ldj = qf(dst.rearrange("p (r m) -> p r m", r=8),
                         dwv[:, :, j2, :], after=[slot_mms.get(j2 - 2)])
                ldws[j2] = P.sy_dma(
                    phA[:, OF_WO + k * 128:OF_WO + (k + 1) * 128],
                    wn_ds[j2 // 16][(j2 % 16) * 128:(j2 % 16 + 1) * 128, :],
                    after=[final_ops[j2 - 6][0]] if j2 >= 6 else [])
                mms = None
                for r in range(NCORES):
                    mms = MM(P, PW1[:], eyef[:],
                             phA[:, OF_SL + sl * 8 * 128 + r * 128:
                                 OF_SL + sl * 8 * 128 + (r + 1) * 128],
                             r == 0, r == NCORES - 1,
                             [ldj, ld_ef, evd_prev] if r == 0 else [ldj])
                slot_mms[j2] = mms
                old = [final_ops[j2 - 6][0]] if j2 >= 6 else []
                oldt = [final_ops[j2 - 6][1]] if j2 >= 6 else []
                evd = P.ve(lambda e, k=k: e.tensor_copy(
                    phA[:, OF_DW + k * 128:OF_DW + (k + 1) * 128], PW1[:]),
                    after=[mms] + old)
                evd_prev = evd
                cst = P.ve(lambda e, k=k: e.tensor_copy(
                    xb16[:, k * 128:(k + 1) * 128],
                    phA[:, OF_DW + k * 128:OF_DW + (k + 1) * 128]),
                    after=[evd] + oldt)
                sq1 = P.sc(lambda e, k=k: e.activation(
                    phA[:, OF_SC + k * 128:OF_SC + (k + 1) * 128],
                    xb16[:, k * 128:(k + 1) * 128], AF.Square,
                    accum_out=colA[:, k * 8:k * 8 + 1]),
                    after=[cst] + old)
                nmm = MM(P, P6[k][0:1, 0:1], onesf[:], colA[:, k * 8:k * 8 + 1],
                         True, True, [sq1, m1] + old)
                e1 = P.ve(lambda e, k=k: e.tensor_copy(
                    g1A[0:1, k * 4:k * 4 + 1], P6[k][0:1, 0:1]), after=[nmm])
                s1_ = P.sc(lambda e, k=k: e.activation(
                    g1A[0:1, k * 4 + 1:k * 4 + 2], g1A[0:1, k * 4:k * 4 + 1],
                    AF.Sqrt), after=[e1])
                e2 = P.ve(lambda e, k=k: e.tensor_scalar_add(
                    g1A[0:1, k * 4 + 1:k * 4 + 2], g1A[0:1, k * 4 + 1:k * 4 + 2],
                    1e-7), after=[s1_])
                e3 = P.ve(lambda e, k=k: e.reciprocal(
                    g1A[0:1, k * 4 + 2:k * 4 + 3], g1A[0:1, k * 4 + 1:k * 4 + 2]),
                    after=[e2])
                rp = MM(P, P6[k][:, 1:2], onesr[0:1, :],
                        g1A[0:1, k * 4 + 2:k * 4 + 3], True, True, [e3, m2, e1])
                e4 = P.ve(lambda e, k=k: e.tensor_copy(
                    colA[:, k * 8 + 1:k * 8 + 2], P6[k][:, 1:2]), after=[rp])
                vx = P.ve(lambda e, k=k: e.tensor_single_scalar(
                    xb16[:, k * 128:(k + 1) * 128], xb16[:, k * 128:(k + 1) * 128],
                    colA[:, k * 8 + 1:k * 8 + 2], ALU.mult), after=[e4, sq1])
                tp0 = TP(P, PT2[:], xb16[:, k * 128:(k + 1) * 128], eyeb[:],
                         [vx, ld_ey, prev_tp])
                ct0 = P.ve(lambda e, k=k: e.tensor_copy(
                    tb16[:, k * 128:(k + 1) * 128], PT2[:]), after=[tp0])
                prev_tp = ct0
                last_x[j2], last_t[j2], last_p[j2] = vx, ct0, e4
            for it in range(5):
                a_, b_, c_ = NS_COEFFS[it]
                mG, eG, mG2, vB2, mX, vX, mT = {}, {}, {}, {}, {}, {}, {}
                for j2 in mats:
                    k = j2 % 6
                    mG[j2] = MM(P, P6[k][:], xb16[:, k * 128:(k + 1) * 128],
                                xb16[:, k * 128:(k + 1) * 128], True, True,
                                [last_x[j2], last_p[j2]])
                for j2 in mats:
                    k = j2 % 6
                    eG[j2] = P.sc(lambda e, k=k: e.copy(
                        ab16[:, k * 128:(k + 1) * 128], P6[k][:]), after=[mG[j2]])
                for j2 in mats:
                    k = j2 % 6
                    mG2[j2] = MM(P, P6[k][:], ab16[:, k * 128:(k + 1) * 128],
                                 ab16[:, k * 128:(k + 1) * 128], True, True,
                                 [eG[j2]])
                for j2 in mats:
                    k = j2 % 6
                    v1_ = P.sc(lambda e, k=k, c_=c_: e.activation(
                        bm16[:, k * 128:(k + 1) * 128], P6[k][:], AF.Copy,
                        scale=c_), after=[mG2[j2]])
                    vB2[j2] = P.ve(lambda e, k=k, b_=b_: e.scalar_tensor_tensor(
                        bm16[:, k * 128:(k + 1) * 128],
                        ab16[:, k * 128:(k + 1) * 128], b_,
                        bm16[:, k * 128:(k + 1) * 128], ALU.mult, ALU.add),
                        after=[v1_])
                for j2 in mats:
                    k = j2 % 6
                    mX[j2] = MM(P, P6[k][:], tb16[:, k * 128:(k + 1) * 128],
                                bm16[:, k * 128:(k + 1) * 128], True, True,
                                [vB2[j2], last_t[j2]])
                for j2 in mats:
                    k = j2 % 6
                    vX[j2] = P.ve(lambda e, k=k, a_=a_: e.scalar_tensor_tensor(
                        xb16[:, k * 128:(k + 1) * 128],
                        xb16[:, k * 128:(k + 1) * 128], a_, P6[k][:],
                        ALU.mult, ALU.add), after=[mX[j2]])
                    last_x[j2] = vX[j2]
                    last_p[j2] = vX[j2]
                if it < 4:
                    for j2 in mats:
                        k = j2 % 6
                        mT[j2] = MM(P, P6[k][:], bm16[:, k * 128:(k + 1) * 128],
                                    tb16[:, k * 128:(k + 1) * 128], True, True,
                                    [vX[j2]])
                    for j2 in mats:
                        k = j2 % 6
                        vT = P.ve(lambda e, k=k, a_=a_: e.scalar_tensor_tensor(
                            tb16[:, k * 128:(k + 1) * 128],
                            tb16[:, k * 128:(k + 1) * 128], a_, P6[k][:],
                            ALU.mult, ALU.add), after=[mT[j2]])
                        last_t[j2] = vT
                        last_p[j2] = vT
            for j2 in mats:
                k = j2 % 6
                vw = P.ve(lambda e, k=k: e.tensor_add(
                    phA[:, OF_W2 + k * 128:OF_W2 + (k + 1) * 128],
                    xb16[:, k * 128:(k + 1) * 128],
                    phA[:, OF_DW + k * 128:OF_DW + (k + 1) * 128]),
                    after=[last_x[j2]])
                sq2 = P.sc(lambda e, k=k: e.activation(
                    phA[:, OF_SC + k * 128:OF_SC + (k + 1) * 128],
                    phA[:, OF_W2 + k * 128:OF_W2 + (k + 1) * 128],
                    AF.Square, accum_out=colA[:, k * 8 + 2:k * 8 + 3]),
                    after=[vw])
                s2_ = P.sc(lambda e, k=k: e.activation(
                    colA[:, k * 8 + 3:k * 8 + 4], colA[:, k * 8 + 2:k * 8 + 3],
                    AF.Sqrt), after=[sq2])
                e5 = P.ve(lambda e, k=k: e.tensor_scalar_add(
                    colA[:, k * 8 + 3:k * 8 + 4], colA[:, k * 8 + 3:k * 8 + 4],
                    1e-5), after=[s2_])
                e6 = P.ve(lambda e, k=k: e.reciprocal(
                    colA[:, k * 8 + 4:k * 8 + 5], colA[:, k * 8 + 3:k * 8 + 4]),
                    after=[e5])
                sq3 = P.sc(lambda e, k=k: e.activation(
                    phA[:, OF_SC + k * 128:OF_SC + (k + 1) * 128],
                    phA[:, OF_WO + k * 128:OF_WO + (k + 1) * 128],
                    AF.Square, accum_out=colA[:, k * 8 + 5:k * 8 + 6]),
                    after=[s2_, ldws[j2]])
                s3_ = P.sc(lambda e, k=k: e.activation(
                    colA[:, k * 8 + 6:k * 8 + 7], colA[:, k * 8 + 5:k * 8 + 6],
                    AF.Sqrt), after=[sq3])
                e7 = P.ve(lambda e, k=k: e.tensor_mul(
                    colA[:, k * 8 + 7:k * 8 + 8], colA[:, k * 8 + 4:k * 8 + 5],
                    colA[:, k * 8 + 6:k * 8 + 7]), after=[s3_, e6])
                vwf = P.ve(lambda e, k=k: e.tensor_single_scalar(
                    phA[:, OF_W2 + k * 128:OF_W2 + (k + 1) * 128],
                    phA[:, OF_W2 + k * 128:OF_W2 + (k + 1) * 128],
                    colA[:, k * 8 + 7:k * 8 + 8], ALU.mult), after=[e7, sq2])
                cwf = P.ve(lambda e, k=k: e.tensor_copy(
                    xb16[:, k * 128:(k + 1) * 128],
                    phA[:, OF_W2 + k * 128:OF_W2 + (k + 1) * 128]),
                    after=[vwf])
                tpw = TP(P, PT2[:], xb16[:, k * 128:(k + 1) * 128], eyeb[:],
                         [cwf, prev_tp])
                cww = P.ve(lambda e, j2=j2: e.tensor_copy(
                    wfT[j2 // 16][:, (j2 % 16) * 128:(j2 % 16 + 1) * 128],
                    PT2[:]), after=[tpw])
                prev_tp = cww
                final_ops.append((cww, tpw))
        ns_last = final_ops[-1][0]
        phA_tail = [ns_last, evd_prev, slot_mms[46], slot_mms[47]]
        for g in reversed(gP6):
            g.__exit__(None, None, None)
        gPT2.__exit__(None, None, None)
        gPW1.__exit__(None, None, None)
        gA.__exit__(None, None, None)

        gB2 = nc.sbuf_tensor("s_phB2", [128, 16 * NT], BF16, side="right")
        qnb = gB2.__enter__()
        gB3 = nc.sbuf_tensor("s_phB3", [128, 4 * NT], F32, side="right")
        scrB = gB3.__enter__()
        gB4 = nc.sbuf_tensor("s_phB4", [128, 4 * NT], BF16, side="right")
        gg = gB4.__enter__()
        gPS = [nc.psum_tensor(f"s_aps{i}", [128, 512], F32) for i in range(8)]
        PS8 = [g.__enter__() for g in gPS]
        qnv = qn_d.rearrange("(h p) n -> p h n", h=16)
        ld_qn = [P.sy_dma(qnb[:, c * 4 * NT:(c + 1) * 4 * NT]
                          .rearrange("p (h n) -> p h n", h=4),
                          qnv[:, c * 4:(c + 1) * 4, :], after=phA_tail)
                 for c in range(4)]
        ap_tail = []
        prev_eon, prev_erp = {}, {}
        for g in range(8):
            steps = [(s // 2, s % 2) for s in range(4 * g, 4 * g + 4)]
            mg, mh, egl, eghl, mo, sqo, zmm, ezl, rrp, erp = (
                {}, {}, {}, {}, {}, {}, {}, {}, {}, {})
            for i, (h, hh) in enumerate(steps):
                qseg = qnb[:, h * NT + hh * 512:h * NT + (hh + 1) * 512]
                mg[i] = MM(P, PS8[i][:], wfT[0][:, h * 128:(h + 1) * 128], qseg,
                           True, True, [ld_qn[h // 4], prev_eon.get(i)])
                mh[i] = MM(P, PS8[4 + i][:], wfT[2][:, h * 128:(h + 1) * 128],
                           qseg, True, True, [ld_qn[h // 4], prev_erp.get(i)])
            for i in range(4):
                egl[i] = P.sc(lambda e, i=i: e.activation(
                    gg[:, i * 512:(i + 1) * 512], PS8[i][:], AF.Silu),
                    after=[mg[i]])
            for i in range(4):
                eghl[i] = P.ve(lambda e, i=i: e.tensor_mul(
                    gg[:, 2048 + i * 512:2048 + (i + 1) * 512],
                    gg[:, i * 512:(i + 1) * 512], PS8[4 + i][:]),
                    after=[egl[i], mh[i]])
            for i, (h, hh) in enumerate(steps):
                mo[i] = MM(P, PS8[i][:], wfT[1][:, h * 128:(h + 1) * 128],
                           gg[:, 2048 + i * 512:2048 + (i + 1) * 512],
                           True, True, [eghl[i], egl[i]])
            for i in range(4):
                sqo[i] = P.sc(lambda e, i=i: e.activation(
                    scrB[:, i * 512:(i + 1) * 512], PS8[i][:], AF.Square),
                    after=[mo[i]])
            for i in range(4):
                zmm[i] = MM(P, PS8[4 + i][0:1, :], onesf[:],
                            scrB[:, i * 512:(i + 1) * 512], True, True,
                            [sqo[i], m1, eghl[i]])
            for i in range(4):
                ezl[i] = P.ve(lambda e, i=i: e.tensor_copy(
                    scrB[0:1, i * 512:(i + 1) * 512], PS8[4 + i][0:1, :]),
                    after=[zmm[i]])
            er = P.ve(lambda e: e.tensor_scalar(
                scrB[0:1, 0:2048], scrB[0:1, 0:2048], 1.0 / 128, 1e-5,
                ALU.mult, ALU.add), after=list(ezl.values()))
            sr = P.sc(lambda e: e.activation(
                scrB[0:1, 0:2048], scrB[0:1, 0:2048], AF.Sqrt), after=[er])
            vr = P.ve(lambda e: e.reciprocal(
                scrB[0:1, 0:2048], scrB[0:1, 0:2048]), after=[sr])
            for i in range(4):
                rrp[i] = MM(P, PS8[4 + i][:], onesr[0:1, :],
                            scrB[0:1, i * 512:(i + 1) * 512], True, True,
                            [vr, m2, ezl[i]])
            for i in range(4):
                erp[i] = P.sc(lambda e, i=i: e.copy(
                    scrB[:, 2048 + i * 512:2048 + (i + 1) * 512],
                    PS8[4 + i][:]), after=[rrp[i], vr])
            for i, (h, hh) in enumerate(steps):
                eon = P.ve(lambda e, i=i, h=h, hh=hh: e.scalar_tensor_tensor(
                    ocat[:, h * NT + hh * 512:h * NT + (hh + 1) * 512],
                    PS8[i][:], onwc[:, 0:1],
                    scrB[:, 2048 + i * 512:2048 + (i + 1) * 512],
                    ALU.mult, ALU.mult), after=[erp[i], ld_on, mo[i], sqo[i]])
                prev_eon[i] = eon
                prev_erp[i] = erp[i]
                ap_tail.append(eon)
        gB4.__exit__(None, None, None)
        gB3.__exit__(None, None, None)
        gB2.__exit__(None, None, None)

        PG2, PH2 = PS8[0], PS8[1]
        for g in reversed(gPS[2:]):
            g.__exit__(None, None, None)
        gC = nc.sbuf_tensor("s_phC", [128, 2 * NCH * 128], BF16, side="right")
        wos = gC.__enter__()
        resid = []
        oc_lds = {}
        last_mm = {}
        for co in range(NCH):
            sl = co % 2
            oc_lds[co] = P.gp_dma(
                wos[:, sl * NCH * 128:(sl + 1) * NCH * 128].rearrange(
                    "p (g m) -> p g m", g=NCH),
                woT_d[:, co, :, :].rearrange("g p m -> p g m"),
                after=[last_mm.get(co - 2)] + (ap_tail[-4:] if co < 2 else []))
            mm = None
            for ci in range(NCH):
                mm = MM(P, PG2[:],
                        wos[:, sl * NCH * 128 + ci * 128:sl * NCH * 128 + (ci + 1) * 128],
                        ocat[:, ci * NT:ci * NT + 512],
                        ci == 0, ci == NCH - 1,
                        [oc_lds[co]] + ap_tail[-4:] +
                        ([resid[-1]] if (co and ci == 0) else []))
            mmb = None
            for ci in range(NCH):
                mmb = MM(P, PH2[:],
                         wos[:, sl * NCH * 128 + ci * 128:sl * NCH * 128 + (ci + 1) * 128],
                         ocat[:, ci * NT + 512:ci * NT + NT],
                         ci == 0, ci == NCH - 1,
                         [oc_lds[co]] + ap_tail[-4:] +
                         ([resid[-1]] if (co and ci == 0) else []))
            last_mm[co] = mmb
            rv1 = P.ve(lambda e, co=co: e.scalar_tensor_tensor(
                xs[:, co * NT:co * NT + 512], PG2[:], 1.0,
                xs[:, co * NT:co * NT + 512], ALU.mult, ALU.add),
                after=[mm, ld_x])
            rv2 = P.ve(lambda e, co=co: e.scalar_tensor_tensor(
                xs[:, co * NT + 512:co * NT + NT], PH2[:], 1.0,
                xs[:, co * NT + 512:co * NT + NT], ALU.mult, ALU.add),
                after=[mmb, ld_x, rv1])
            resid.append(rv2)
        st1 = P.sy_dma(xo_d.rearrange("(g p) n -> p g n", g=NCH),
                       xs[:].rearrange("p (g n) -> p g n", g=NCH),
                       after=resid)
        gC.__exit__(None, None, None)
        for g in reversed(gPS[:2]):
            g.__exit__(None, None, None)
        gXN = nc.sbuf_tensor("s_xnb", [128, NCH * NT], BF16, side="right")
        xnb = gXN.__enter__()
        PA = E(nc.psum_tensor("s_PA", [128, 1024], F32))
        PB = E(nc.psum_tensor("s_PB", [128, 1024], F32))
        xn_ops, _ = emit_ln(P, nc, E, "L", xs, xnb, PA, PB,
                            resid[-1], extra_first=resid + [st1],
                            xs_readers=[st1])
        st2 = P.sy_dma(xn_d.rearrange("(g p) n -> p g n", g=NCH),
                       xnb[:].rearrange("p (g n) -> p g n", g=NCH),
                       after=xn_ops)
        P.gp(lambda e: e.nop(), after=[st1, st2])
        gXN.__exit__(None, None, None)
        P.finalize()
    return nc



_NC_CACHE = {}


def _get_nc(name, builder, *args):
    if name not in _NC_CACHE:
        _NC_CACHE[name] = builder(*args)
    return _NC_CACHE[name]


def _run(nc, in_maps):
    res = run_bass_kernel_spmd(nc, in_maps, core_ids=list(range(NCORES)))
    return res.results


def kernel(**inp):
    ac = np.ascontiguousarray
    x0T = ac(np.asarray(inp["L0_attn_x"])[0].T)
    EYEB = np.eye(128).astype(BF)
    EYEF = np.eye(128, dtype=np.float32)
    SELS = np.zeros((16, D), np.float32)
    for e in range(16):
        SELS[e, e * 128:(e + 1) * 128] = 1.0
    TRI = np.zeros((128, 4 * 512), dtype=BF)
    for m in range(4):
        for p in range(128):
            TRI[p, m * 512 + 128 * m + p:(m + 1) * 512] = 1
    CMASK = []
    kk = np.arange(SEQ)[:, None]
    for i in range(NCORES):
        qq = i * NT + np.arange(NT)[None, :]
        CMASK.append((kk <= qq).astype(BF))

    simw = np.asarray(inp["L0_ffn_sim_matrix"], np.float32)
    gates = np.asarray(inp["L0_ffn_gates"], np.float32).reshape(1, 16)
    WQ = ac(np.asarray(inp["L0_ffn_q_proj"]).transpose(1, 0, 2).reshape(D, D))
    WK = ac(np.asarray(inp["L0_ffn_k_proj"]).transpose(1, 0, 2).reshape(D, D))
    WV = ac(np.asarray(inp["L0_ffn_v_proj"]).transpose(1, 0, 2).reshape(D, D))
    OC = ac(np.asarray(inp["L0_ffn_o_proj"]).reshape(D, D)
            .reshape(16, 128, 16, 128).transpose(0, 2, 1, 3))
    w_qkv0 = np.asarray(inp["L0_attn_w_qkv"], np.float32)
    w_out0 = np.asarray(inp["L0_attn_w_out"], np.float32)

    wqkvT1 = ac(np.asarray(inp["L1_attn_w_qkv"]).T)
    wlrT1 = ac(np.asarray(inp["L1_attn_w_lr"]).T)
    w0_1 = np.asarray(inp["L1_attn_w0"], np.float32)
    w1_1 = np.asarray(inp["L1_attn_w1"], np.float32)
    w2_1 = np.asarray(inp["L1_attn_w2"], np.float32)
    w0T = ac(w0_1.transpose(0, 2, 1).reshape(D, 128))
    w2T = ac(w2_1.transpose(0, 2, 1).reshape(D, 128))
    w1f = ac(w1_1.reshape(D, 128))
    w0n = ac(w0_1.reshape(D, 128))
    w1n = ac(w1_1.reshape(D, 128))
    w2n = ac(w2_1.reshape(D, 128))
    onw = ac(np.asarray(inp["L1_attn_o_norm_w"], np.float32).reshape(128, 1))
    woT1 = ac(np.asarray(inp["L1_attn_w_o"]).T
              .reshape(16, 128, 16, 128).transpose(0, 2, 1, 3))
    w1T_f = ac(np.asarray(inp["L1_ffn_w1"]).T)
    w2T_f = ac(np.asarray(inp["L1_ffn_w2"]).T)
    w3T_f = ac(np.asarray(inp["L1_ffn_w3"]).T)

    nc_pre0 = _get_nc("pre0", build_ffn0pre, False)
    nc_pre1 = _get_nc("pre1", build_ffn0pre, True)
    nc_post = _get_nc("post", build_ffn0post)
    nc_attn = _get_nc("attn0", build_attn0)
    nc_l1ffn = _get_nc("l1ffn", build_l1ffn)
    nc_l1pre = _get_nc("l1pre", build_l1attn_pre)
    nc_l1post = _get_nc("l1post", build_l1attn_post)
    H = 5632

    def shards(xT):
        return [ac(xT[:, i * NT:(i + 1) * NT]) for i in range(NCORES)]

    xsh = shards(x0T)
    r1 = _run(nc_pre0, [dict(x=xsh[i], simw=simw, gates=gates,
                             wq=WQ, wk=WK, wv=WV, eyeb_d=EYEB, eyef_d=EYEF)
                        for i in range(NCORES)])
    kfull = np.concatenate([r1[i]["kf"] for i in range(NCORES)], axis=1)
    vfull = np.concatenate([r1[i]["vt"] for i in range(NCORES)], axis=0)
    r2 = _run(nc_post, [dict(x=xsh[i], qf=r1[i]["qf"], kfull=kfull, vfull=vfull,
                             wTi=r1[i]["wT"], oc=OC, sels=SELS, cmask=CMASK[i])
                        for i in range(NCORES)])
    x1 = [r2[i]["xo"] for i in range(NCORES)]
    xn1full = np.concatenate([r2[i]["xn"] for i in range(NCORES)], axis=1)
    attn_maps = []
    for i in range(NCORES):
        h0, h1 = 2 * i, 2 * i + 1
        cols = []
        for h in (h0, h1):
            cols += [w_qkv0[:, h * 128:(h + 1) * 128],
                     w_qkv0[:, D + h * 128:D + (h + 1) * 128],
                     w_qkv0[:, 2 * D + h * 128:2 * D + (h + 1) * 128]]
        wq6 = ac(np.concatenate(cols, axis=1))
        wo2 = ac(w_out0[h0 * 128:(h1 + 1) * 128, :])
        attn_maps.append(dict(
            xnf=xn1full, wq6=wq6, wo2=wo2, tri=TRI,
            w1t=w1T_f, w2t=w2T_f, w3t=w3T_f,
            sl1=ac(w1T_f[:, i * (H // 8):(i + 1) * (H // 8)]),
            sl2=ac(w2T_f[:, i * (H // 8):(i + 1) * (H // 8)]),
            sl3=ac(w3T_f[:, i * (D // 8):(i + 1) * (D // 8)])))
    r3 = _run(nc_attn, attn_maps)
    q1 = np.concatenate([r3[i]["q1o"] for i in range(NCORES)], axis=1)
    q2 = np.concatenate([r3[i]["q2o"] for i in range(NCORES)], axis=1)
    q3 = np.concatenate([r3[i]["q3o"] for i in range(NCORES)], axis=1)
    q1t = ac(q1.reshape(16, 128, 44, 128).transpose(2, 0, 1, 3))
    q2t = ac(q2.reshape(16, 128, 44, 128).transpose(2, 0, 1, 3))
    q3t = ac(q3.reshape(44, 128, 16, 128).transpose(0, 2, 1, 3))
    maps4 = []
    for i in range(NCORES):
        parts = np.concatenate(
            [np.asarray(r3[r]["part"][:, i * NT:(i + 1) * NT]) for r in range(NCORES)],
            axis=0)
        maps4.append(dict(x=x1[i], parts=ac(parts), simw=simw, gates=gates,
                          wq=WQ, wk=WK, wv=WV, eyeb_d=EYEB, eyef_d=EYEF))
    r4 = _run(nc_pre1, maps4)
    x2 = [r4[i]["xout"] for i in range(NCORES)]
    kfull = np.concatenate([r4[i]["kf"] for i in range(NCORES)], axis=1)
    vfull = np.concatenate([r4[i]["vt"] for i in range(NCORES)], axis=0)
    r5 = _run(nc_post, [dict(x=x2[i], qf=r4[i]["qf"], kfull=kfull, vfull=vfull,
                             wTi=r4[i]["wT"], oc=OC, sels=SELS, cmask=CMASK[i])
                        for i in range(NCORES)])
    x3 = [r5[i]["xo"] for i in range(NCORES)]
    xn3 = [r5[i]["xn"] for i in range(NCORES)]
    r7 = _run(nc_l1ffn, [dict(x=x3[i], xni=xn3[i], q1t=q1t, q2t=q2t, q3t=q3t)
                         for i in range(NCORES)])
    x4 = [r7[i]["xo"] for i in range(NCORES)]
    xn4 = [r7[i]["xn"] for i in range(NCORES)]
    r8 = _run(nc_l1pre, [dict(xni=xn4[i], wqkvT=wqkvT1, wlrT=wlrT1,
                              w0T=w0T, w2T=w2T, w1f=w1f, eyeb_d=EYEB)
                         for i in range(NCORES)])
    dwall = np.concatenate([np.asarray(r8[i]["dwp"]) for i in range(NCORES)], axis=0)
    r9 = _run(nc_l1post, [dict(x=x4[i], qni=r8[i]["qn"], dwall=dwall,
                               w0n=w0n, w1n=w1n, w2n=w2n, onw=onw, woT=woT1,
                               eyeb_d=EYEB, eyef_d=EYEF)
                          for i in range(NCORES)])
    x5 = [r9[i]["xo"] for i in range(NCORES)]
    xn5 = [r9[i]["xn"] for i in range(NCORES)]
    r10 = _run(nc_l1ffn, [dict(x=x5[i], xni=xn5[i], q1t=q1t, q2t=q2t, q3t=q3t)
                          for i in range(NCORES)])
    x6 = np.concatenate([np.asarray(r10[i]["xo"]) for i in range(NCORES)], axis=1)
    return ac(x6.T).reshape(1, SEQ, D).astype(np.float32)
